# version 02
# baseline (speedup 1.0000x reference)
import contextlib
import math

import numpy as np
import concourse.bass as bass
import concourse.mybir as mybir
from concourse.bass_utils import run_bass_kernel_spmd

F32 = mybir.dt.float32
BF16 = mybir.dt.bfloat16
I32 = mybir.dt.int32
AF = mybir.ActivationFunctionType
ALU = mybir.AluOpType
AX = mybir.AxisListType

S = 4096
DM = 1024
NEG = -1.0e30
LN_EPS = 1e-5
RMS_EPS = float(np.finfo(np.float32).eps)

ENGS = ("sync", "tensor", "vector", "scalar", "gpsimd")


class Ctr:

    def __init__(self, sem):
        self.sem = sem
        self.v = 0

    def inc(self, inst, n=1):
        inst.then_inc(self.sem, n)
        self.v += n
        return self.v


_NAME_CTR = [0]


class Bld:

    def __init__(self, nc, st):
        self.nc = nc
        self.st = st
        self.bars = 0
        self.ds = self.mksem("ds")
        self.dg = self.mksem("dg")
        self.dc = self.mksem("dc")
        self.et = self.mksem("et")
        self.ev = self.mksem("ev")
        self.es = self.mksem("es")
        self.eg = self.mksem("eg")
        self.w0 = self.mksem("w0")
        self.w1 = self.mksem("w1")
        self.w2 = self.mksem("w2")
        self.barsem = self.mksem("bar")
        self.identb = None
        self.identf = None
        self.iotac = None
        self.onesf = None
        self.onesb = None
        self.kc = None

    def name(self, base):
        _NAME_CTR[0] += 1
        return f"{base}{_NAME_CTR[0]}"

    def mksem(self, base):
        return Ctr(self.st.enter_context(self.nc.semaphore(self.name(base))))

    def sbuf(self, base, shape, dtype):
        return self.st.enter_context(
            self.nc.sbuf_tensor(self.name(base), shape, dtype))

    def psum(self, base, shape, dtype):
        return self.st.enter_context(
            self.nc.psum_tensor(self.name(base), shape, dtype))

    def dram(self, base, shape, dtype):
        return self.nc.dram_tensor(self.name(base), shape, dtype, kind="Internal")

    def bar(self):
        tgt = self.bars + len(ENGS)
        self.bars = tgt
        with self.nc.Block() as b:
            for en in ENGS:
                def mk(e, _t=tgt):
                    e.sem_inc(self.barsem.sem, 1)
                    e.wait_ge(self.barsem.sem, _t)
                getattr(b, en)(mk)

    def dma_fence(self):
        with self.nc.Block() as b:
            @b.sync
            def _(sy):
                sy.wait_ge(self.ds.sem, self.ds.v)
                sy.wait_ge(self.dg.sem, self.dg.v)
                sy.wait_ge(self.dc.sem, self.dc.v)


KC_LNEPS = 0
KC_MHALF = 1
KC_RMSEPS = 2
KC_TWO = 3
KC_ATT = 4
KC_ONE = 5


def emit_consts(B):
    nc = B.nc
    B.identb = B.sbuf("identb", [128, 128], BF16)
    B.identf = B.sbuf("identf", [128, 128], F32)
    B.iotac = B.sbuf("iotac", [128, 1], F32)
    B.onesf = B.sbuf("onesf", [1, 128], F32)
    B.onescf = B.sbuf("onescf", [128, 1], F32)
    B.onesb = B.sbuf("onesb", [128, 1], BF16)
    B.kc = B.sbuf("kc", [128, 8], F32)
    with nc.Block() as b:
        @b.gpsimd
        def _(g):
            g.memset(B.identb[:, :], 0.0)
            g.affine_select(out=B.identb[:, :], in_=B.identb[:, :],
                            compare_op=ALU.not_equal, fill=1.0, base=0,
                            pattern=[[-1, 128]], channel_multiplier=1)
            g.memset(B.identf[:, :], 0.0)
            g.affine_select(out=B.identf[:, :], in_=B.identf[:, :],
                            compare_op=ALU.not_equal, fill=1.0, base=0,
                            pattern=[[-1, 128]], channel_multiplier=1)
            g.iota(B.iotac[:, :], pattern=[[0, 1]], base=0, channel_multiplier=1,
                   allow_small_or_imprecise_dtypes=True)
            g.memset(B.onesf[0:1, :], 1.0)
            g.memset(B.onescf[:, :], 1.0)
            g.memset(B.onesb[:, :], 1.0)
            g.memset(B.kc[:, KC_LNEPS:KC_LNEPS + 1], LN_EPS)
            g.memset(B.kc[:, KC_MHALF:KC_MHALF + 1], -0.5)
            g.memset(B.kc[:, KC_RMSEPS:KC_RMSEPS + 1], RMS_EPS)
            g.memset(B.kc[:, KC_TWO:KC_TWO + 1], 2.0)
            g.memset(B.kc[:, KC_ATT:KC_ATT + 1], 0.125)
            g.memset(B.kc[:, KC_ONE:KC_ONE + 1], 1.0)
    B.bar()


def _flat(ap):
    nd = len(ap.shape)
    if nd == 1:
        return ap
    if nd == 2:
        return ap.rearrange("a b -> (a b)")
    if nd == 3:
        return ap.rearrange("a b c -> (a b c)")
    raise ValueError(nd)


def emit_cast_dram(B, dst, src):
    nc = B.nc
    n = 1
    for d in src.shape:
        n *= d
    sf = _flat(src)
    df = _flat(dst)
    with nc.Block() as b:
        @b.gpsimd
        def _(g):
            off = 0
            while off < n:
                m = min(2 * 1024 * 1024, n - off)
                B.dg.inc(g.dma_start(
                    df[off:off + m].rearrange("(a b) -> a b", a=1),
                    sf[off:off + m].rearrange("(a b) -> a b", a=1)), 16)
                off += m


def strided_ap(base_ap, extra_off, dims):
    return bass.AP(base_ap.tensor, base_ap.offset + extra_off, dims)


def emit_ln(B, src, dst, stats, scr, dve="vector"):
    nc = B.nc
    if not isinstance(src, bass.AP):
        src = src[:, :]
    if not isinstance(dst, bass.AP):
        dst = dst[:, :]
    if not isinstance(scr, bass.AP):
        scr = scr[:, :]
    D = src.shape[-1] if hasattr(src, "shape") else DM
    rinv = 1.0 / D
    with nc.Block() as b:
        @b.vector
        def _(v):
            B.ev.inc(v.reduce_sum(stats[:, 0:1], src, axis=AX.X, negate=True))

        @b.scalar
        def _(s):
            B.es.inc(s.activation(scr, src, AF.Square,
                                  accum_out=stats[:, 1:2]))
    with nc.Block() as b:
        @b.vector
        def _(v):
            v.wait_ge(B.es.sem, B.es.v)
            v.wait_ge(B.ev.sem, B.ev.v)
            B.ev.inc(v.tensor_scalar(stats[:, 0:1], stats[:, 0:1], rinv, None,
                                     ALU.mult))
            v.wait_ge(B.ev.sem, B.ev.v)
            B.ev.inc(v.tensor_mul(stats[:, 2:3], stats[:, 0:1], stats[:, 0:1]))
            B.ev.inc(v.tensor_scalar(stats[:, 1:2], stats[:, 1:2], rinv, None,
                                     ALU.mult))
            v.wait_ge(B.ev.sem, B.ev.v)
            B.ev.inc(v.tensor_sub(stats[:, 1:2], stats[:, 1:2], stats[:, 2:3]))

        @b.scalar
        def _(s):
            s.wait_ge(B.ev.sem, B.ev.v)
            B.es.inc(s.activation(stats[:, 2:3], stats[:, 1:2], AF.Ln,
                                  bias=B.kc[:, KC_LNEPS:KC_LNEPS + 1]))
            s.wait_ge(B.es.sem, B.es.v)
            B.es.inc(s.activation(stats[:, 3:4], stats[:, 2:3], AF.Exp,
                                  scale=B.kc[:, KC_MHALF:KC_MHALF + 1]))
    with nc.Block() as b:
        @b.vector
        def _(v):
            v.wait_ge(B.es.sem, B.es.v)
            B.ev.inc(v.tensor_scalar(dst, src, stats[:, 0:1], stats[:, 3:4],
                                     ALU.add, ALU.mult))
    return B.ev.v


def emit_load_ln_t(B, x_in, r0, ntt, xT, ts_free, xf, xhat, stats, scr, pbt):
    nc = B.nc
    for tt in range(ntt):
        with nc.Block() as b:
            @b.sync
            def _(sy, _tt=tt):
                B.ds.inc(sy.dma_start(
                    xf[:, :], x_in[r0 + _tt * 128:r0 + _tt * 128 + 128, :]), 16)

            @b.vector
            def _(v):
                v.wait_ge(B.ds.sem, B.ds.v)

            @b.scalar
            def _(s):
                s.wait_ge(B.ds.sem, B.ds.v)
        emit_ln(B, xf[:, :], xhat[:, :], stats, scr)
        with nc.Block() as b:
            @b.tensor
            def _(t, _tt=tt):
                t.wait_ge(B.ev.sem, B.ev.v)
                for c in range(8):
                    inst = t.transpose(pbt[:, c * 128:(c + 1) * 128],
                                       xhat[:, c * 128:(c + 1) * 128],
                                       B.identb[:, :])
                B.et.inc(inst)

            @b.scalar
            def _(s, _tt=tt):
                s.wait_ge(B.et.sem, B.et.v)
                dst = strided_ap(xT[:, :], _tt * 128,
                                 [[xT.shape[1], 128], [ts_free, 8], [1, 128]])
                B.es.inc(s.copy(dst, pbt[:, :]))
        B.bar()


def emit_epilogue_tile(B, ysb, x_in, x_out, row0, xf, osb, stats, scr):
    nc = B.nc
    with nc.Block() as b:
        @b.sync
        def _(sy):
            B.ds.inc(sy.dma_start(xf[:, :], x_in[row0:row0 + 128, :]), 16)
    emit_ln(B, ysb[:, :], scr[:, :], stats, osb[:, :])
    with nc.Block() as b:
        @b.vector
        def _(v):
            v.wait_ge(B.ds.sem, B.ds.v)
            v.wait_ge(B.ev.sem, B.ev.v)
            B.ev.inc(v.tensor_add(osb[:, :], scr[:, :], xf[:, :]))

        @b.sync
        def _(sy):
            sy.wait_ge(B.ev.sem, B.ev.v)
            B.ds.inc(sy.dma_start(x_out[row0:row0 + 128, :], osb[:, :]), 16)
    B.bar()



def emit_l1_ffn(B, x_in, x_out, W):
    nc = B.nc
    DH = 4096
    TS = 512
    NSW = S // TS
    NTT = TS // 128

    w1b = B.dram("l1fw1b", [DM, DH], BF16)
    w2b = B.dram("l1fw2b", [DM, DH], BF16)
    h1b = B.dram("l1fh1b", [DH, DH], BF16)
    h2b = B.dram("l1fh2b", [DH, DH], BF16)
    w3b = B.dram("l1fw3b", [DH, DM], BF16)
    emit_cast_dram(B, w1b[:, :], W["l1f_w1T"])
    emit_cast_dram(B, w2b[:, :], W["l1f_w2T"])
    emit_cast_dram(B, h1b[:, :], W["l1f_h1T"])
    emit_cast_dram(B, h2b[:, :], W["l1f_h2T"])
    emit_cast_dram(B, w3b[:, :], W["l1f_w3T"])
    B.dma_fence()

    with contextlib.ExitStack() as st:
        BB = Bld.__new__(Bld)
        BB.__dict__.update(B.__dict__)
        BB.st = st
        sb = BB.sbuf
        s1c = sb("s1c", [128, 32], F32)
        s2c = sb("s2c", [128, 32], F32)
        sh1c = sb("sh1c", [128, 32], F32)
        sh2c = sb("sh2c", [128, 32], F32)
        s3b = sb("s3b", [128, DM], F32)
        s3row = sb("s3row", [1, DM], F32)
        with nc.Block() as b:
            @b.sync
            def _(sy):
                B.ds.inc(sy.dma_start(s1c[:, :], W["l1f_s1"]), 16)
                B.ds.inc(sy.dma_start(s2c[:, :], W["l1f_s2"]), 16)
                B.ds.inc(sy.dma_start(sh1c[:, :], W["l1f_sh1"]), 16)
                B.ds.inc(sy.dma_start(sh2c[:, :], W["l1f_sh2"]), 16)
                B.ds.inc(sy.dma_start(
                    s3row[0:1, :],
                    W["l1f_s3"].rearrange("(a b) -> a b", a=1)), 16)
        ps3 = BB.psum("ps3", [128, 512], F32)
        for half in range(2):
            with nc.Block() as b:
                @b.tensor
                def _(t, _h=half):
                    t.wait_ge(B.ds.sem, B.ds.v)
                    B.et.inc(t.matmul(ps3[:, :], B.onesf[0:1, :],
                                      s3row[0:1, _h * 512:(_h + 1) * 512],
                                      start=True, stop=True))

                @b.vector
                def _(v, _h=half):
                    v.wait_ge(B.et.sem, B.et.v)
                    B.ev.inc(v.tensor_copy(s3b[:, _h * 512:(_h + 1) * 512],
                                           ps3[:, :]))
        B.bar()

        xT = sb("xT", [128, 8 * TS], BF16)
        hT = sb("hT", [128, 32 * TS], BF16)
        g1T = sb("g1T", [128, 32 * TS], BF16)
        wsa = sb("wsa", [128, 8 * 512], BF16)
        wsb_ = sb("wsb", [128, 8 * 512], BF16)
        wbig = sb("wbig", [128, 32 * 512], BF16)
        us = sb("us", [128, TS], F32)
        ysb = sb("ysb", [128, NTT * DM], F32)
        xf = sb("xf", [128, DM], F32)
        xhat = sb("xhat", [128, DM], BF16)
        stats = sb("stats", [128, 4], F32)
        scr = sb("scr", [128, DM], F32)
        osb = sb("osb", [128, DM], F32)
        pbt = BB.psum("pbt", [128, DM], BF16)
        py1 = [BB.psum("py1", [128, TS], F32) for _ in range(2)]
        py2 = [BB.psum("py2", [128, TS], F32) for _ in range(2)]

        for sw in range(NSW):
            r0 = sw * TS
            emit_load_ln_t(B, x_in, r0, NTT, xT, TS, xf, xhat, stats, scr, pbt)

            for og in range(8):
                with nc.Block() as b:
                    @b.sync
                    def _(sy, _og=og):
                        for ic in range(8):
                            B.ds.inc(sy.dma_start(
                                wsa[:, ic * 512:(ic + 1) * 512],
                                w1b[ic * 128:(ic + 1) * 128,
                                    _og * 512:(_og + 1) * 512]), 16)
                        for ic in range(8):
                            B.ds.inc(sy.dma_start(
                                wsb_[:, ic * 512:(ic + 1) * 512],
                                w2b[ic * 128:(ic + 1) * 128,
                                    _og * 512:(_og + 1) * 512]), 16)
                for j in range(4):
                    oc = og * 4 + j
                    with nc.Block() as b:
                        @b.tensor
                        def _(t, _j=j, _oc=oc):
                            t.wait_ge(B.ds.sem, B.ds.v)
                            for ic in range(8):
                                i1 = t.matmul(py1[_j % 2][:, :],
                                              wsa[:, ic * 512 + _j * 128:
                                                  ic * 512 + _j * 128 + 128],
                                              xT[:, ic * TS:(ic + 1) * TS],
                                              start=(ic == 0), stop=(ic == 7))
                            B.et.inc(i1)
                            for ic in range(8):
                                i2 = t.matmul(py2[_j % 2][:, :],
                                              wsb_[:, ic * 512 + _j * 128:
                                                   ic * 512 + _j * 128 + 128],
                                              xT[:, ic * TS:(ic + 1) * TS],
                                              start=(ic == 0), stop=(ic == 7))
                            B.et.inc(i2)

                        @b.scalar
                        def _(s, _j=j, _oc=oc):
                            s.wait_ge(B.et.sem, B.et.v - 1)
                            B.es.inc(s.activation(us[:, :], py1[_j % 2][:, :],
                                                  AF.Silu,
                                                  scale=s1c[:, _oc:_oc + 1]))

                        @b.vector
                        def _(v, _j=j, _oc=oc):
                            v.wait_ge(B.et.sem, B.et.v)
                            v.wait_ge(B.es.sem, B.es.v)
                            B.ev.inc(v.scalar_tensor_tensor(
                                hT[:, _oc * TS:(_oc + 1) * TS],
                                py2[_j % 2][:, :], s2c[:, _oc:_oc + 1],
                                us[:, :], ALU.mult, ALU.mult))
                    B.bar()

            for (wdram, scol, srcT, dstT) in ((h1b, sh1c, hT, g1T),
                                              (h2b, sh2c, g1T, hT)):
                for og in range(8):
                    with nc.Block() as b:
                        @b.sync
                        def _(sy, _og=og, _w=wdram):
                            for hc in range(32):
                                B.ds.inc(sy.dma_start(
                                    wbig[:, hc * 512:(hc + 1) * 512],
                                    _w[hc * 128:(hc + 1) * 128,
                                       _og * 512:(_og + 1) * 512]), 16)
                    for j in range(4):
                        oc = og * 4 + j
                        with nc.Block() as b:
                            @b.tensor
                            def _(t, _j=j, _src=srcT):
                                t.wait_ge(B.ds.sem, B.ds.v)
                                for hc in range(32):
                                    i1 = t.matmul(
                                        py1[_j % 2][:, :],
                                        wbig[:, hc * 512 + _j * 128:
                                             hc * 512 + _j * 128 + 128],
                                        _src[:, hc * TS:(hc + 1) * TS],
                                        start=(hc == 0), stop=(hc == 31))
                                B.et.inc(i1)

                            @b.scalar
                            def _(s, _j=j, _oc=oc, _sc=scol, _dst=dstT):
                                s.wait_ge(B.et.sem, B.et.v)
                                B.es.inc(s.activation(
                                    _dst[:, _oc * TS:(_oc + 1) * TS],
                                    py1[_j % 2][:, :], AF.Gelu,
                                    scale=_sc[:, _oc:_oc + 1]))
                        B.bar()

            for do in range(2):
                with nc.Block() as b:
                    @b.sync
                    def _(sy, _do=do):
                        for hc in range(32):
                            B.ds.inc(sy.dma_start(
                                wbig[:, hc * 512:(hc + 1) * 512],
                                w3b[hc * 128:(hc + 1) * 128,
                                    _do * 512:(_do + 1) * 512]), 16)
                for tt in range(NTT):
                    with nc.Block() as b:
                        @b.tensor
                        def _(t, _tt=tt, _do=do):
                            t.wait_ge(B.ds.sem, B.ds.v)
                            for hc in range(32):
                                i1 = t.matmul(
                                    py1[_tt % 2][:, :],
                                    hT[:, hc * TS + _tt * 128:
                                       hc * TS + _tt * 128 + 128],
                                    wbig[:, hc * 512:(hc + 1) * 512],
                                    start=(hc == 0), stop=(hc == 31))
                            B.et.inc(i1)

                        @b.vector
                        def _(v, _tt=tt, _do=do):
                            v.wait_ge(B.et.sem, B.et.v)
                            B.ev.inc(v.tensor_mul(
                                ysb[:, _tt * DM + _do * 512:
                                    _tt * DM + _do * 512 + 512],
                                py1[_tt % 2][:, :],
                                s3b[:, _do * 512:(_do + 1) * 512]))
                    B.bar()

            for tt in range(NTT):
                emit_epilogue_tile(B, ysb[:, tt * DM:(tt + 1) * DM],
                                   x_in, x_out, r0 + tt * 128,
                                   xf, osb, stats, scr)



def emit_l0_ffn(B, x_in, x_out, W):
    nc = B.nc
    E, H = 8, 2048
    TS = 1024
    NSW = S // TS
    NTT = TS // 128

    w1b = B.dram("l0fw1b", [E, DM, H], BF16)
    w2b = B.dram("l0fw2b", [E, H, DM], BF16)
    emit_cast_dram(B, w1b[:, :, :], W["l0f_w1"])
    emit_cast_dram(B, w2b[:, :, :], W["l0f_w2"])

    with contextlib.ExitStack() as st:
        BB = Bld.__new__(Bld)
        BB.__dict__.update(B.__dict__)
        BB.st = st
        sb = BB.sbuf

        gwT = sb("gwT", [128, 64], BF16)
        gbb = sb("gbb", [128, 8], F32)
        rwb = sb("rwb", [128, 8], F32)
        b2b = sb("b2b", [128, DM], F32)
        b1c = sb("b1c", [128, 128], F32)
        row8 = sb("row8", [1, 16], F32)
        b2sb = sb("b2sb", [8, DM], F32)
        b2row = sb("b2row", [1, DM], F32)
        psx = BB.psum("psx", [128, 512], F32)

        with nc.Block() as b:
            @b.gpsimd
            def _(g):
                for ic in range(8):
                    B.dg.inc(g.dma_start(
                        gwT[:, ic * 8:(ic + 1) * 8],
                        W["l0f_gwT"][ic * 128:(ic + 1) * 128, :]), 16)

            @b.sync
            def _(sy):
                B.ds.inc(sy.dma_start(
                    row8[0:1, 0:8],
                    W["l0f_gate_b"].rearrange("(a b) -> a b", a=1)), 16)
                B.ds.inc(sy.dma_start(
                    row8[0:1, 8:16],
                    W["l0f_rms_w"].rearrange("(a b) -> a b", a=1)), 16)
                B.ds.inc(sy.dma_start(b2sb[:, :], W["l0f_b2"]), 16)
                B.ds.inc(sy.dma_start(
                    b1c[:, :].rearrange("p (e c) -> p e c", e=8),
                    W["l0f_b1c"].rearrange("e p c -> p e c")), 16)
        with nc.Block() as b:
            @b.tensor
            def _(t):
                t.wait_ge(B.ds.sem, B.ds.v)
                B.et.inc(t.matmul(psx[:, 0:8], B.onesf[0:1, :],
                                  row8[0:1, 0:8], start=True, stop=True))
                B.et.inc(t.matmul(psx[:, 8:16], B.onesf[0:1, :],
                                  row8[0:1, 8:16], start=True, stop=True))

            @b.vector
            def _(v):
                v.wait_ge(B.et.sem, B.et.v)
                B.ev.inc(v.tensor_copy(gbb[:, :], psx[:, 0:8]))
                B.ev.inc(v.tensor_copy(rwb[:, :], psx[:, 8:16]))
        with nc.Block() as b:
            @b.tensor
            def _(t):
                t.wait_ge(B.ev.sem, B.ev.v)
                for hf in range(2):
                    B.et.inc(t.matmul(psx[0:1, 0:512], B.onescf[0:8, 0:1],
                                      b2sb[:, hf * 512:(hf + 1) * 512],
                                      start=True, stop=True))
                    t.wait_ge(B.ev.sem, B.ev.v + hf + 1)

            @b.vector
            def _(v):
                for hf in range(2):
                    v.wait_ge(B.et.sem, B.et.v - 1 + hf)
                    B.ev.inc(v.tensor_copy(
                        b2row[0:1, hf * 512:(hf + 1) * 512], psx[0:1, 0:512]))
        B.bar()
        with nc.Block() as b:
            @b.tensor
            def _(t):
                for hf in range(2):
                    B.et.inc(t.matmul(psx[:, 0:512], B.onesf[0:1, :],
                                      b2row[0:1, hf * 512:(hf + 1) * 512],
                                      start=True, stop=True))
                    t.wait_ge(B.ev.sem, B.ev.v + hf + 1)

            @b.vector
            def _(v):
                for hf in range(2):
                    v.wait_ge(B.et.sem, B.et.v - 1 + hf)
                    B.ev.inc(v.tensor_copy(
                        b2b[:, hf * 512:(hf + 1) * 512], psx[:, 0:512]))
        B.bar()

        xT = sb("xT", [128, 8 * TS], BF16)
        hT = sb("hT", [128, 16 * TS], BF16)
        yacc = sb("yacc", [128, NTT * DM], F32)
        w1sb = sb("w1sb", [128, 8 * H], BF16)
        w2sb = sb("w2sb", [128, 16 * DM], BF16)
        ngc = sb("ngc", [128, NTT], F32)
        lg = sb("lg", [128, 8], F32)
        scr8 = sb("scr8", [128, 8], F32)
        gst = sb("gst", [128, 4], F32)
        xf = sb("xf", [128, DM], F32)
        xhat = sb("xhat", [128, DM], BF16)
        stats = sb("stats", [128, 4], F32)
        scr = sb("scr", [128, DM], F32)
        ysb = sb("ysb", [128, DM], F32)
        osb = sb("osb", [128, DM], F32)
        pbt = BB.psum("pbt", [128, DM], BF16)
        psg = BB.psum("psg", [128, 8], F32)
        ph = [BB.psum("ph", [128, 512], F32) for _ in range(2)]
        py = [BB.psum("py", [128, 512], F32) for _ in range(2)]

        for sw in range(NSW):
            r0 = sw * TS
            emit_load_ln_t(B, x_in, r0, NTT, xT, TS, xf, xhat, stats, scr, pbt)

            for tt in range(NTT):
                with nc.Block() as b:
                    @b.tensor
                    def _(t, _tt=tt):
                        for ic in range(8):
                            i1 = t.matmul(psg[:, :],
                                          xT[:, ic * TS + _tt * 128:
                                             ic * TS + _tt * 128 + 128],
                                          gwT[:, ic * 8:(ic + 1) * 8],
                                          start=(ic == 0), stop=(ic == 7))
                        B.et.inc(i1)

                    @b.vector
                    def _(v):
                        v.wait_ge(B.et.sem, B.et.v)
                        B.ev.inc(v.tensor_add(lg[:, :], gbb[:, :], psg[:, :]))

                    @b.scalar
                    def _(s):
                        s.wait_ge(B.ev.sem, B.ev.v)
                        B.es.inc(s.activation(scr8[:, :], lg[:, :], AF.Square,
                                              accum_out=gst[:, 0:1]))
                        s.wait_ge(B.es.sem, B.es.v)
                        B.es.inc(s.activation(
                            gst[:, 1:2], gst[:, 0:1], AF.Ln,
                            bias=B.kc[:, KC_RMSEPS:KC_RMSEPS + 1],
                            scale=B.kc[:, KC_ATT:KC_ATT + 1]))
                        s.wait_ge(B.es.sem, B.es.v)
                        B.es.inc(s.activation(
                            gst[:, 2:3], gst[:, 1:2], AF.Exp,
                            scale=B.kc[:, KC_MHALF:KC_MHALF + 1]))

                    @b.vector
                    def _(v):
                        v.wait_ge(B.es.sem, B.es.v)
                        B.ev.inc(v.tensor_scalar(lg[:, :], lg[:, :],
                                                 gst[:, 2:3], None, ALU.mult))
                        v.wait_ge(B.ev.sem, B.ev.v)
                        B.ev.inc(v.tensor_mul(lg[:, :], lg[:, :], rwb[:, :]))
                        v.wait_ge(B.ev.sem, B.ev.v)
                        B.ev.inc(v.reduce_max(gst[:, 0:1], lg[:, :],
                                              axis=AX.X, negate=True))
                        v.wait_ge(B.ev.sem, B.ev.v)
                        B.ev.inc(v.tensor_scalar(gst[:, 1:2], gst[:, 0:1],
                                                 2.0, None, ALU.mult))

                    @b.scalar
                    def _(s):
                        s.wait_ge(B.ev.sem, B.ev.v)
                        B.es.inc(s.activation(scr8[:, :], lg[:, :], AF.Exp,
                                              bias=gst[:, 1:2],
                                              scale=B.kc[:, KC_TWO:KC_TWO + 1],
                                              accum_out=gst[:, 2:3]))

                    @b.vector
                    def _(v, _tt=tt):
                        v.wait_ge(B.es.sem, B.es.v)
                        B.ev.inc(v.tensor_scalar(gst[:, 3:4], gst[:, 2:3],
                                                 1e-6, 1.0, ALU.mult, ALU.add))
                        v.wait_ge(B.ev.sem, B.ev.v)
                        B.ev.inc(v.reciprocal(ngc[:, _tt:_tt + 1],
                                              gst[:, 3:4]))
                B.bar()

            for e in range(8):
                with nc.Block() as b:
                    @b.sync
                    def _(sy, _e=e):
                        for ic in range(8):
                            B.ds.inc(sy.dma_start(
                                w1sb[:, ic * H:(ic + 1) * H],
                                w1b[_e, ic * 128:(ic + 1) * 128, :]), 16)
                        for hc in range(16):
                            B.ds.inc(sy.dma_start(
                                w2sb[:, hc * DM:(hc + 1) * DM],
                                w2b[_e, hc * 128:(hc + 1) * 128, :]), 16)
                et0, es0 = B.et.v, B.es.v
                with nc.Block() as b:
                    @b.tensor
                    def _(t, _e=e):
                        t.wait_ge(B.ds.sem, B.ds.v)
                        for j in range(32):
                            hc, tsub = j // 2, j % 2
                            if j >= 2:
                                t.wait_ge(B.es.sem, es0 + j - 1)
                            for ic in range(8):
                                i1 = t.matmul(
                                    ph[j % 2][:, :],
                                    w1sb[:, ic * H + hc * 128:
                                         ic * H + hc * 128 + 128],
                                    xT[:, ic * TS + tsub * 512:
                                       ic * TS + tsub * 512 + 512],
                                    start=(ic == 0), stop=(ic == 7))
                            B.et.inc(i1)

                    @b.scalar
                    def _(s, _e=e):
                        for j in range(32):
                            hc, tsub = j // 2, j % 2
                            s.wait_ge(B.et.sem, et0 + j + 1)
                            B.es.inc(s.activation(
                                hT[:, hc * TS + tsub * 512:
                                   hc * TS + tsub * 512 + 512],
                                ph[j % 2][:, :], AF.Gelu,
                                bias=b1c[:, _e * 16 + hc:_e * 16 + hc + 1]))
                B.bar()
                et0, ev0 = B.et.v, B.ev.v
                with nc.Block() as b:
                    @b.tensor
                    def _(t):
                        for j in range(16):
                            tt, ds_ = j // 2, j % 2
                            if j >= 2:
                                t.wait_ge(B.ev.sem, ev0 + j - 1)
                            for hc in range(16):
                                i1 = t.matmul(
                                    py[j % 2][:, :],
                                    hT[:, hc * TS + tt * 128:
                                       hc * TS + tt * 128 + 128],
                                    w2sb[:, hc * DM + ds_ * 512:
                                         hc * DM + ds_ * 512 + 512],
                                    start=(hc == 0), stop=(hc == 15))
                            B.et.inc(i1)

                    @b.vector
                    def _(v, _e=e):
                        for j in range(16):
                            tt, ds_ = j // 2, j % 2
                            v.wait_ge(B.et.sem, et0 + j + 1)
                            dst = yacc[:, tt * DM + ds_ * 512:
                                       tt * DM + ds_ * 512 + 512]
                            if _e == 0:
                                B.ev.inc(v.tensor_copy(dst, py[j % 2][:, :]))
                            else:
                                B.ev.inc(v.tensor_add(dst, dst,
                                                      py[j % 2][:, :]))
                B.bar()

            for tt in range(NTT):
                with nc.Block() as b:
                    @b.vector
                    def _(v, _tt=tt):
                        B.ev.inc(v.tensor_add(
                            ysb[:, :], yacc[:, _tt * DM:(_tt + 1) * DM],
                            b2b[:, :]))
                        v.wait_ge(B.ev.sem, B.ev.v)
                        B.ev.inc(v.tensor_scalar(ysb[:, :], ysb[:, :],
                                                 ngc[:, _tt:_tt + 1], None,
                                                 ALU.mult))

                    @b.scalar
                    def _(s):
                        s.wait_ge(B.ev.sem, B.ev.v)
                emit_epilogue_tile(B, ysb, x_in, x_out, r0 + tt * 128,
                                   xf, osb, stats, scr)



def emit_l0_attn(B, x_in, x_out, W):
    nc = B.nc
    NH, G, HD = 16, 4, 64

    with contextlib.ExitStack() as st:
        BB = Bld.__new__(Bld)
        BB.__dict__.update(B.__dict__)
        BB.st = st
        sb = BB.sbuf

        qTp = sb("qTp", [128, 8 * S], BF16)
        kTp = sb("kTp", [128, 4 * S], BF16)
        vsb = sb("vsb", [128, 32 * 256], BF16)
        band_sb = sb("band", [128, 640], F32)
        ohb = sb("ohb", [128, 32 * 4], BF16)
        rmc = sb("rmc", [128, 32], F32)
        gpb = sb("gpb", [128, 4], F32)
        dupaddb = sb("dupaddb", [128, 4], F32)
        keepbf = sb("keepbf", [128, 4], F32)
        gpi = sb("gpi", [1, 4], I32)
        gpf = sb("gpf", [1, 4], F32)
        dup01 = sb("dup01", [1, 12], F32)
        brow = sb("brow", [1, 2560], F32)
        bqb = sb("bqb", [128, DM], F32)
        bkb = sb("bkb", [128, 256], F32)
        bvb = sb("bvb", [128, 256], F32)
        bob = sb("bob", [128, DM], F32)
        qgT = sb("qgT", [64, 64], BF16)
        kgT = sb("kgT", [128, 16], BF16)
        vgg = sb("vgg", [4, 256], BF16)
        ogsb = sb("ogsb", [4, DM], F32)
        ogbf = sb("ogbf", [4, DM], BF16)

        with nc.Block() as b:
            @b.sync
            def _(sy):
                B.ds.inc(sy.dma_start(band_sb[:, :], W["band_tbl"]), 16)
                B.ds.inc(sy.dma_start(
                    gpi[0:1, :], W["gp"].rearrange("(a b) -> a b", a=1)), 16)
                off = 0
                for nm, ln_ in (("l0a_bq", DM), ("l0a_bk", 256),
                                ("l0a_bv", 256), ("l0a_bo", DM)):
                    B.ds.inc(sy.dma_start(
                        brow[0:1, off:off + ln_],
                        W[nm].rearrange("(a b) -> a b", a=1)), 16)
                    off += ln_

            @b.scalar
            def _(s):
                s.wait_ge(B.ds.sem, B.ds.v)
                B.es.inc(s.copy(gpf[0:1, :], gpi[0:1, :]))

            @b.vector
            def _(v):
                v.wait_ge(B.es.sem, B.es.v)
                B.ev.inc(v.memset(dup01[0:1, :], 0.0))
                for k in range(1, 4):
                    for k2 in range(k):
                        v.wait_ge(B.ev.sem, B.ev.v)
                        B.ev.inc(v.tensor_tensor(
                            dup01[0:1, 4 + k:5 + k], gpf[0:1, k:k + 1],
                            gpf[0:1, k2:k2 + 1], ALU.is_equal))
                        v.wait_ge(B.ev.sem, B.ev.v)
                        B.ev.inc(v.tensor_max(dup01[0:1, k:k + 1],
                                              dup01[0:1, k:k + 1],
                                              dup01[0:1, 4 + k:5 + k]))
                v.wait_ge(B.ev.sem, B.ev.v)
                B.ev.inc(v.tensor_scalar(dup01[0:1, 8:12], dup01[0:1, 0:4],
                                         -1.0, 1.0, ALU.mult, ALU.add))
                B.ev.inc(v.tensor_scalar(dup01[0:1, 0:4], dup01[0:1, 0:4],
                                         NEG, None, ALU.mult))
        with contextlib.ExitStack() as pst:
            psx = pst.enter_context(
                nc.psum_tensor(B.name("psx"), [128, 512], F32))
            bc_jobs = [(gpb[:, :], gpf[0:1, :]),
                       (dupaddb[:, :], dup01[0:1, 0:4]),
                       (keepbf[:, :], dup01[0:1, 8:12]),
                       (bkb[:, :], brow[0:1, DM:DM + 256]),
                       (bvb[:, :], brow[0:1, DM + 256:DM + 512]),
                       (bqb[:, 0:512], brow[0:1, 0:512]),
                       (bqb[:, 512:DM], brow[0:1, 512:DM]),
                       (bob[:, 0:512], brow[0:1, DM + 512:DM + 1024]),
                       (bob[:, 512:DM], brow[0:1, DM + 1024:DM + 1536])]
            for ji, (dst, srow) in enumerate(bc_jobs):
                with nc.Block() as b:
                    @b.tensor
                    def _(t, _d=dst, _s=srow):
                        t.wait_ge(B.ev.sem, B.ev.v)
                        B.et.inc(t.matmul(psx[:, 0:_s.shape[-1]],
                                          B.onesf[0:1, :], _s,
                                          start=True, stop=True))
                        t.wait_ge(B.ev.sem, B.ev.v + 1)

                    @b.vector
                    def _(v, _d=dst):
                        v.wait_ge(B.et.sem, B.et.v)
                        B.ev.inc(v.tensor_copy(_d, psx[:, 0:_d.shape[-1]]))
        B.bar()
        with nc.Block() as b:
            @b.vector
            def _(v):
                for b32 in range(32):
                    B.ev.inc(v.tensor_scalar(rmc[:, 0:4], gpb[:, :],
                                             B.iotac[:, 0:1],
                                             float(128 * b32),
                                             ALU.subtract, ALU.subtract))
                    v.wait_ge(B.ev.sem, B.ev.v)
                    B.ev.inc(v.tensor_scalar(rmc[:, 0:4], rmc[:, 0:4], 0.0,
                                             None, ALU.is_equal))
                    v.wait_ge(B.ev.sem, B.ev.v)
                    B.ev.inc(v.tensor_mul(ohb[:, b32 * 4:(b32 + 1) * 4],
                                          rmc[:, 0:4], keepbf[:, :]))
                    v.wait_ge(B.ev.sem, B.ev.v)
        with nc.Block() as b:
            @b.vector
            def _(v):
                for b32 in range(32):
                    B.ev.inc(v.reduce_sum(rmc[:, b32:b32 + 1],
                                          ohb[:, b32 * 4:(b32 + 1) * 4],
                                          axis=AX.X))
                v.wait_ge(B.ev.sem, B.ev.v)
                B.ev.inc(v.tensor_scalar(rmc[:, :], rmc[:, :], -1.0, 1.0,
                                         ALU.mult, ALU.add))
        B.bar()

        with contextlib.ExitStack() as ast:
            asb = lambda nm, sh, dt: ast.enter_context(
                nc.sbuf_tensor(B.name(nm), sh, dt))
            wqsb = asb("wqsb", [128, 8 * DM], BF16)
            wksb = asb("wksb", [128, 8 * 256], BF16)
            wvsb = asb("wvsb", [128, 8 * 256], BF16)
            xf = asb("xf", [128, DM], F32)
            xhat = asb("xhat", [128, DM], BF16)
            xTt = asb("xTt", [128, DM], BF16)
            stats = asb("stats", [128, 4], F32)
            scr = asb("scr", [128, DM], F32)
            sint = asb("sint", [128, DM], F32)
            cost = asb("cost", [128, DM], F32)
            qf = asb("qf", [128, DM], F32)
            qrot = asb("qrot", [128, DM], F32)
            qhat = asb("qhat", [128, DM], BF16)
            khat = asb("khat", [128, 256], BF16)
            pbt = ast.enter_context(
                nc.psum_tensor(B.name("pbt"), [128, DM], BF16))
            pq = [ast.enter_context(
                nc.psum_tensor(B.name("pq"), [128, 512], F32))
                for _ in range(2)]
            psqg = ast.enter_context(
                nc.psum_tensor(B.name("psqg"), [64, 64], F32))
            pskg = ast.enter_context(
                nc.psum_tensor(B.name("pskg"), [128, 16], F32))
            psvg = ast.enter_context(
                nc.psum_tensor(B.name("psvg"), [4, 256], F32))

            with nc.Block() as b:
                @b.gpsimd
                def _(g):
                    for ic in range(8):
                        B.dg.inc(g.dma_start(
                            wqsb[:, ic * DM:(ic + 1) * DM],
                            W["l0a_Wq"][ic * 128:(ic + 1) * 128, :]), 16)
                        B.dg.inc(g.dma_start(
                            wksb[:, ic * 256:(ic + 1) * 256],
                            W["l0a_Wk"][ic * 128:(ic + 1) * 128, :]), 16)
                        B.dg.inc(g.dma_start(
                            wvsb[:, ic * 256:(ic + 1) * 256],
                            W["l0a_Wv"][ic * 128:(ic + 1) * 128, :]), 16)

            for rt in range(32):
                r0 = rt * 128
                with nc.Block() as b:
                    @b.sync
                    def _(sy, _r0=r0):
                        B.ds.inc(sy.dma_start(xf[:, :],
                                              x_in[_r0:_r0 + 128, :]), 16)
                        B.ds.inc(sy.dma_start(
                            sint[:, :], W["sin_rep"][_r0:_r0 + 128, :]), 16)
                        B.ds.inc(sy.dma_start(
                            cost[:, :], W["cos_rep"][_r0:_r0 + 128, :]), 16)

                    @b.vector
                    def _(v):
                        v.wait_ge(B.ds.sem, B.ds.v)

                    @b.scalar
                    def _(s):
                        s.wait_ge(B.ds.sem, B.ds.v)
                emit_ln(B, xf, xhat, stats, scr)
                with nc.Block() as b:
                    @b.tensor
                    def _(t):
                        t.wait_ge(B.ev.sem, B.ev.v)
                        for c in range(8):
                            i1 = t.transpose(pbt[:, c * 128:(c + 1) * 128],
                                             xhat[:, c * 128:(c + 1) * 128],
                                             B.identb[:, :])
                        B.et.inc(i1)

                    @b.scalar
                    def _(s):
                        s.wait_ge(B.et.sem, B.et.v)
                        B.es.inc(s.copy(xTt[:, :], pbt[:, :]))
                B.bar()
                with nc.Block() as b:
                    @b.tensor
                    def _(t, _rt=rt):
                        t.wait_ge(B.dg.sem, B.dg.v)
                        for half in range(2):
                            for ic in range(8):
                                i1 = t.matmul(
                                    pq[half][:, :],
                                    xTt[:, ic * 128:(ic + 1) * 128],
                                    wqsb[:, ic * DM + half * 512:
                                         ic * DM + half * 512 + 512],
                                    start=(ic == 0), stop=(ic == 7))
                            B.et.inc(i1)

                    @b.vector
                    def _(v):
                        for half in range(2):
                            v.wait_ge(B.et.sem, B.et.v - 1 + half)
                            B.ev.inc(v.tensor_add(
                                qf[:, half * 512:half * 512 + 512],
                                pq[half][:, :],
                                bqb[:, half * 512:half * 512 + 512]))
                with nc.Block() as b:
                    @b.vector
                    def _(v):
                        v.wait_ge(B.ev.sem, B.ev.v)
                        src_o = strided_ap(qf[:, :], 1, [[DM, 128], [2, 512]])
                        dst_e = strided_ap(qrot[:, :], 0, [[DM, 128], [2, 512]])
                        B.ev.inc(v.tensor_scalar(dst_e, src_o, -1.0, None,
                                                 ALU.mult))
                        src_e = strided_ap(qf[:, :], 0, [[DM, 128], [2, 512]])
                        dst_o = strided_ap(qrot[:, :], 1, [[DM, 128], [2, 512]])
                        B.ev.inc(v.tensor_copy(dst_o, src_e))
                        v.wait_ge(B.ev.sem, B.ev.v)
                        B.ev.inc(v.tensor_mul(qf[:, :], qf[:, :], cost[:, :]))
                        B.ev.inc(v.tensor_mul(qrot[:, :], qrot[:, :],
                                              sint[:, :]))
                        v.wait_ge(B.ev.sem, B.ev.v)
                        B.ev.inc(v.tensor_add(qhat[:, :], qf[:, :],
                                              qrot[:, :]))
                with nc.Block() as b:
                    @b.tensor
                    def _(t, _rt=rt):
                        t.wait_ge(B.ev.sem, B.ev.v)
                        for c in range(8):
                            i1 = t.transpose(pbt[:, c * 128:(c + 1) * 128],
                                             qhat[:, c * 128:(c + 1) * 128],
                                             B.identb[:, :])
                        B.et.inc(i1)
                        for h in range(NH):
                            i1 = t.matmul(psqg[0:64, h * 4:(h + 1) * 4],
                                          qhat[:, h * 64:(h + 1) * 64],
                                          ohb[:, _rt * 4:(_rt + 1) * 4],
                                          start=(_rt == 0), stop=(_rt == 31),
                                          skip_group_check=True)
                        B.et.inc(i1)

                    @b.scalar
                    def _(s, _rt=rt):
                        s.wait_ge(B.et.sem, B.et.v - 1)
                        dst = strided_ap(qTp[:, :], _rt * 128,
                                         [[8 * S, 128], [S, 8], [1, 128]])
                        B.es.inc(s.copy(dst, pbt[:, :]))
                with nc.Block() as b:
                    @b.tensor
                    def _(t):
                        for ic in range(8):
                            i1 = t.matmul(pq[0][:, 0:256],
                                          xTt[:, ic * 128:(ic + 1) * 128],
                                          wksb[:, ic * 256:(ic + 1) * 256],
                                          start=(ic == 0), stop=(ic == 7))
                        B.et.inc(i1)
                        for ic in range(8):
                            i1 = t.matmul(pq[1][:, 0:256],
                                          xTt[:, ic * 128:(ic + 1) * 128],
                                          wvsb[:, ic * 256:(ic + 1) * 256],
                                          start=(ic == 0), stop=(ic == 7))
                        B.et.inc(i1)

                    @b.vector
                    def _(v, _rt=rt):
                        v.wait_ge(B.et.sem, B.et.v - 1)
                        B.ev.inc(v.tensor_add(qf[:, 0:256], pq[0][:, 0:256],
                                              bkb[:, :]))
                        src_o = strided_ap(qf[:, :], 1, [[DM, 128], [2, 128]])
                        dst_e = strided_ap(qrot[:, :], 0, [[DM, 128], [2, 128]])
                        v.wait_ge(B.ev.sem, B.ev.v)
                        B.ev.inc(v.tensor_scalar(dst_e, src_o, -1.0, None,
                                                 ALU.mult))
                        src_e = strided_ap(qf[:, :], 0, [[DM, 128], [2, 128]])
                        dst_o = strided_ap(qrot[:, :], 1, [[DM, 128], [2, 128]])
                        B.ev.inc(v.tensor_copy(dst_o, src_e))
                        v.wait_ge(B.ev.sem, B.ev.v)
                        B.ev.inc(v.tensor_mul(qf[:, 0:256], qf[:, 0:256],
                                              cost[:, 0:256]))
                        B.ev.inc(v.tensor_mul(qrot[:, 0:256], qrot[:, 0:256],
                                              sint[:, 0:256]))
                        v.wait_ge(B.ev.sem, B.ev.v)
                        B.ev.inc(v.tensor_add(khat[:, :], qf[:, 0:256],
                                              qrot[:, 0:256]))
                        v.wait_ge(B.et.sem, B.et.v)
                        B.ev.inc(v.tensor_add(
                            vsb[:, _rt * 256:(_rt + 1) * 256],
                            pq[1][:, 0:256], bvb[:, :]))
                with nc.Block() as b:
                    @b.tensor
                    def _(t, _rt=rt):
                        t.wait_ge(B.ev.sem, B.ev.v)
                        t.wait_ge(B.es.sem, B.es.v)
                        for g_ in range(4):
                            for bp in (0, 64):
                                i1 = t.transpose(
                                    pbt[bp:bp + 64,
                                        g_ * 128:g_ * 128 + 128],
                                    khat[:, g_ * 64:(g_ + 1) * 64],
                                    B.identb[:, :])
                        B.et.inc(i1)
                        for g_ in range(4):
                            for bp in (0, 64):
                                i1 = t.matmul(
                                    pskg[bp:bp + 64, g_ * 4:(g_ + 1) * 4],
                                    khat[:, g_ * 64:(g_ + 1) * 64],
                                    ohb[:, _rt * 4:(_rt + 1) * 4],
                                    start=(_rt == 0), stop=(_rt == 31),
                                    skip_group_check=True)
                        B.et.inc(i1)
                        for g_ in range(4):
                            i1 = t.matmul(psvg[0:4, g_ * 64:(g_ + 1) * 64],
                                          ohb[:, _rt * 4:(_rt + 1) * 4],
                                          vsb[:, _rt * 256 + g_ * 64:
                                              _rt * 256 + g_ * 64 + 64],
                                          start=(_rt == 0), stop=(_rt == 31),
                                          skip_group_check=True)
                        B.et.inc(i1)

                    @b.scalar
                    def _(s, _rt=rt):
                        s.wait_ge(B.et.sem, B.et.v - 2)
                        dst = strided_ap(kTp[:, :], _rt * 128,
                                         [[4 * S, 128], [S, 4], [1, 128]])
                        B.es.inc(s.copy(dst, pbt[:, 0:512]))
                B.bar()
            with nc.Block() as b:
                @b.vector
                def _(v):
                    B.ev.inc(v.tensor_copy(qgT[0:64, :], psqg[0:64, :]))
                    B.ev.inc(v.tensor_copy(kgT[:, :], pskg[:, :]))
                    B.ev.inc(v.tensor_copy(vgg[0:4, :], psvg[0:4, :]))
            B.bar()

        with contextlib.ExitStack() as gst:
            gsb = lambda nm, sh, dt: gst.enter_context(
                nc.sbuf_tensor(B.name(nm), sh, dt))
            scg = gsb("scg", [4, S], F32)
            pgbf = gsb("pgbf", [4, S], BF16)
            pTg = gsb("pTg", [128, 128], BF16)
            rg = gsb("rg", [4, 4], F32)
            psg = gst.enter_context(
                nc.psum_tensor(B.name("psg"), [4, 512], F32))
            ptg = gst.enter_context(
                nc.psum_tensor(B.name("ptg"), [128, 128], BF16))
            psog = gst.enter_context(
                nc.psum_tensor(B.name("psog"), [4, 64], F32))
            for h in range(NH):
                g_ = h // 4
                with nc.Block() as b:
                    @b.tensor
                    def _(t, _h=h, _g=g_):
                        for cg in range(8):
                            B.et.inc(t.matmul(
                                psg[0:4, :],
                                qgT[0:64, _h * 4:(_h + 1) * 4],
                                kTp[0:64, _g * S + cg * 512:
                                    _g * S + cg * 512 + 512],
                                start=True, stop=True))
                            t.wait_ge(B.es.sem, B.es.v + cg + 1)

                    @b.scalar
                    def _(s):
                        for cg in range(8):
                            s.wait_ge(B.et.sem, B.et.v - 8 + cg + 1)
                            B.es.inc(s.activation(
                                scg[0:4, cg * 512:(cg + 1) * 512],
                                psg[0:4, :], AF.Copy,
                                scale=B.kc[0:4, KC_ATT:KC_ATT + 1]))
                with nc.Block() as b:
                    @b.vector
                    def _(v):
                        v.wait_ge(B.es.sem, B.es.v)
                        B.ev.inc(v.reduce_max(rg[0:4, 0:1], scg[0:4, :],
                                              axis=AX.X, negate=True))

                    @b.scalar
                    def _(s):
                        s.wait_ge(B.ev.sem, B.ev.v)
                        B.es.inc(s.activation(pgbf[0:4, :], scg[0:4, :],
                                              AF.Exp, bias=rg[0:4, 0:1],
                                              accum_out=rg[0:4, 1:2]))

                    @b.vector
                    def _(v):
                        v.wait_ge(B.es.sem, B.es.v)
                        B.ev.inc(v.reciprocal(rg[0:4, 2:3], rg[0:4, 1:2]))
                with nc.Block() as b:
                    @b.tensor
                    def _(t):
                        t.wait_ge(B.ev.sem, B.ev.v)
                        for jb in range(32):
                            i1 = t.transpose(ptg[:, jb * 4:(jb + 1) * 4],
                                             pgbf[0:4, jb * 128:
                                                  (jb + 1) * 128],
                                             B.identb[0:4, 0:4])
                        B.et.inc(i1)

                    @b.vector
                    def _(v):
                        v.wait_ge(B.et.sem, B.et.v)
                        B.ev.inc(v.tensor_copy(pTg[:, :], ptg[:, :]))
                with nc.Block() as b:
                    @b.tensor
                    def _(t, _g=g_):
                        t.wait_ge(B.ev.sem, B.ev.v)
                        for jb in range(32):
                            i1 = t.matmul(psog[0:4, :],
                                          pTg[:, jb * 4:(jb + 1) * 4],
                                          vsb[:, jb * 256 + _g * 64:
                                              jb * 256 + _g * 64 + 64],
                                          start=(jb == 0), stop=(jb == 31))
                        B.et.inc(i1)

                    @b.vector
                    def _(v, _h=h):
                        v.wait_ge(B.et.sem, B.et.v)
                        B.ev.inc(v.tensor_scalar(
                            ogsb[0:4, _h * 64:(_h + 1) * 64],
                            psog[0:4, :], rg[0:4, 2:3], None, ALU.mult))
                B.bar()
            with nc.Block() as b:
                @b.vector
                def _(v):
                    B.ev.inc(v.tensor_copy(ogbf[0:4, :], ogsb[0:4, :]))
            B.bar()

        with contextlib.ExitStack() as mst:
            msb = lambda nm, sh, dt: mst.enter_context(
                nc.sbuf_tensor(B.name(nm), sh, dt))
            wosb = msb("wosb", [128, 8 * DM], BF16)
            sc = msb("sc", [128, 648], F32)
            pbf = msb("pbf", [128, 648], BF16)
            pT = msb("pT", [128, 640], BF16)
            pTg4 = msb("pTg4", [4, 128], BF16)
            ohT = msb("ohT", [4, 128], BF16)
            gadd = msb("gadd", [128, 4], F32)
            rv = msb("rv", [128, 8], F32)
            osb = msb("osb", [128, DM], F32)
            obf = msb("obf", [128, DM], BF16)
            oT = msb("oT", [128, DM], BF16)
            ysb = msb("ysb", [128, DM], F32)
            xf2 = msb("xf2", [128, DM], F32)
            stats2 = msb("stats2", [128, 4], F32)
            scr2 = msb("scr2", [128, DM], F32)
            osc = msb("osc", [128, DM], F32)
            psA = mst.enter_context(
                nc.psum_tensor(B.name("psA"), [128, 512], F32))
            psB = mst.enter_context(
                nc.psum_tensor(B.name("psB"), [128, 260], F32))
            ptpm = mst.enter_context(
                nc.psum_tensor(B.name("ptpm"), [128, 640], BF16))
            ptg4 = mst.enter_context(
                nc.psum_tensor(B.name("ptg4"), [4, 128], BF16))
            pso = mst.enter_context(
                nc.psum_tensor(B.name("pso"), [128, 64], F32))
            pssc = mst.enter_context(
                nc.psum_tensor(B.name("pssc"), [128, 512], F32))
            pbt2 = mst.enter_context(
                nc.psum_tensor(B.name("pbt2"), [128, DM], BF16))
            psw = mst.enter_context(
                nc.psum_tensor(B.name("psw"), [128, 512], F32))

            with nc.Block() as b:
                @b.gpsimd
                def _(g):
                    for oc in range(8):
                        B.dg.inc(g.dma_start(
                            wosb[:, oc * DM:(oc + 1) * DM],
                            W["l0a_Wo"][oc * 128:(oc + 1) * 128, :]), 16)

            for rt in range(32):
                ws = max(0, rt - 2)
                we = min(31, rt + 2)
                nb = we - ws + 1
                Wd = nb * 128
                toff = (ws - (rt - 2)) * 128
                with nc.Block() as b:
                    @b.vector
                    def _(v, _rt=rt):
                        B.ev.inc(v.tensor_scalar(gadd[:, :], gpb[:, :],
                                                 B.iotac[:, 0:1],
                                                 float(128 * _rt),
                                                 ALU.subtract, ALU.subtract))
                        v.wait_ge(B.ev.sem, B.ev.v)
                        B.ev.inc(v.tensor_scalar(rv[:, 4:8], gadd[:, :],
                                                 -256.0, None, ALU.is_lt))
                        B.ev.inc(v.tensor_scalar(gadd[:, :], gadd[:, :],
                                                 256.0, None, ALU.is_gt))
                        v.wait_ge(B.ev.sem, B.ev.v)
                        B.ev.inc(v.tensor_max(gadd[:, :], gadd[:, :],
                                              rv[:, 4:8]))
                        v.wait_ge(B.ev.sem, B.ev.v)
                        B.ev.inc(v.tensor_scalar(gadd[:, :], gadd[:, :],
                                                 -1.0, -NEG, ALU.add,
                                                 ALU.mult))
                        v.wait_ge(B.ev.sem, B.ev.v)
                        B.ev.inc(v.tensor_add(gadd[:, :], gadd[:, :],
                                              dupaddb[:, :]))
                B.bar()
                for h in range(NH):
                    g_ = h // 4
                    qsl = qTp[64 * (h % 2):64 * (h % 2) + 64,
                              (h // 2) * S + rt * 128:
                              (h // 2) * S + rt * 128 + 128]
                    kbase = 64 * (h % 2)
                    with nc.Block() as b:
                        @b.tensor
                        def _(t, _qsl=qsl, _g=g_, _kb=kbase, _ws=ws,
                              _Wd=Wd):
                            n1 = min(_Wd, 512)
                            B.et.inc(t.matmul(
                                psA[:, 0:n1], _qsl,
                                kTp[_kb:_kb + 64,
                                    _g * S + _ws * 128:
                                    _g * S + _ws * 128 + n1],
                                start=True, stop=True))
                            if _Wd > 512:
                                B.et.inc(t.matmul(
                                    psB[:, 0:_Wd - 512], _qsl,
                                    kTp[_kb:_kb + 64,
                                        _g * S + _ws * 128 + 512:
                                        _g * S + _ws * 128 + _Wd],
                                    start=True, stop=True))
                            B.et.inc(t.matmul(
                                psB[:, 128:132], _qsl,
                                kgT[_kb:_kb + 64, _g * 4:(_g + 1) * 4],
                                start=True, stop=True))

                        @b.scalar
                        def _(s, _Wd=Wd):
                            net = 3 if _Wd > 512 else 2
                            s.wait_ge(B.et.sem, B.et.v - net + 1)
                            n1 = min(_Wd, 512)
                            B.es.inc(s.activation(
                                sc[:, 0:n1], psA[:, 0:n1], AF.Copy,
                                scale=B.kc[:, KC_ATT:KC_ATT + 1]))
                            s.wait_ge(B.et.sem, B.et.v)
                            if _Wd > 512:
                                B.es.inc(s.activation(
                                    sc[:, 512:_Wd], psB[:, 0:_Wd - 512],
                                    AF.Copy,
                                    scale=B.kc[:, KC_ATT:KC_ATT + 1]))
                            B.es.inc(s.activation(
                                sc[:, _Wd:_Wd + 4], psB[:, 128:132],
                                AF.Copy, scale=B.kc[:, KC_ATT:KC_ATT + 1]))

                        @b.vector
                        def _(v, _Wd=Wd, _toff=toff):
                            v.wait_ge(B.es.sem, B.es.v)
                            B.ev.inc(v.tensor_add(
                                sc[:, 0:_Wd], sc[:, 0:_Wd],
                                band_sb[:, _toff:_toff + _Wd]))
                            B.ev.inc(v.tensor_add(sc[:, _Wd:_Wd + 4],
                                                  sc[:, _Wd:_Wd + 4],
                                                  gadd[:, :]))
                            v.wait_ge(B.ev.sem, B.ev.v)
                            B.ev.inc(v.reduce_max(rv[:, 0:1],
                                                  sc[:, 0:_Wd + 4],
                                                  axis=AX.X, negate=True))

                        @b.scalar
                        def _(s, _Wd=Wd):
                            s.wait_ge(B.ev.sem, B.ev.v)
                            B.es.inc(s.activation(
                                pbf[:, 0:_Wd + 4], sc[:, 0:_Wd + 4],
                                AF.Exp, bias=rv[:, 0:1],
                                accum_out=rv[:, 1:2]))

                        @b.vector
                        def _(v):
                            v.wait_ge(B.es.sem, B.es.v)
                            B.ev.inc(v.reciprocal(rv[:, 2:3], rv[:, 1:2]))
                    with nc.Block() as b:
                        @b.tensor
                        def _(t, _Wd=Wd, _nb=nb):
                            t.wait_ge(B.es.sem, B.es.v)
                            for cb in range(_nb):
                                i1 = t.transpose(
                                    ptpm[:, cb * 128:(cb + 1) * 128],
                                    pbf[:, cb * 128:(cb + 1) * 128],
                                    B.identb[:, :])
                            B.et.inc(i1)
                            B.et.inc(t.transpose(ptg4[0:4, :],
                                                 pbf[:, _Wd:_Wd + 4],
                                                 B.identb[:, :]))

                        @b.vector
                        def _(v, _Wd=Wd):
                            v.wait_ge(B.et.sem, B.et.v - 1)
                            B.ev.inc(v.tensor_copy(pT[:, 0:_Wd],
                                                   ptpm[:, 0:_Wd]))
                            v.wait_ge(B.et.sem, B.et.v)
                            B.ev.inc(v.tensor_copy(pTg4[0:4, :],
                                                   ptg4[0:4, :]))
                    with nc.Block() as b:
                        @b.tensor
                        def _(t, _nb=nb, _ws=ws, _g=g_):
                            t.wait_ge(B.ev.sem, B.ev.v)
                            for cb in range(_nb):
                                t.matmul(pso[:, :],
                                         pT[:, cb * 128:(cb + 1) * 128],
                                         vsb[:, (_ws + cb) * 256 + _g * 64:
                                             (_ws + cb) * 256 + _g * 64 + 64],
                                         start=(cb == 0), stop=False)
                            B.et.inc(t.matmul(
                                pso[:, :], pTg4[0:4, :],
                                vgg[0:4, _g * 64:(_g + 1) * 64],
                                start=False, stop=True))

                        @b.vector
                        def _(v, _h=h):
                            v.wait_ge(B.et.sem, B.et.v)
                            B.ev.inc(v.tensor_scalar(
                                osb[:, _h * 64:(_h + 1) * 64], pso[:, :],
                                rv[:, 2:3], None, ALU.mult))
                    B.bar()
                with nc.Block() as b:
                    @b.tensor
                    def _(t, _rt=rt):
                        B.et.inc(t.transpose(ptg4[0:4, :],
                                             ohb[:, _rt * 4:(_rt + 1) * 4],
                                             B.identb[:, :]))

                    @b.vector
                    def _(v):
                        v.wait_ge(B.et.sem, B.et.v)
                        B.ev.inc(v.tensor_copy(ohT[0:4, :], ptg4[0:4, :]))
                with nc.Block() as b:
                    @b.tensor
                    def _(t):
                        t.wait_ge(B.ev.sem, B.ev.v)
                        for half in range(2):
                            B.et.inc(t.matmul(
                                pssc[:, :], ohT[0:4, :],
                                ogbf[0:4, half * 512:half * 512 + 512],
                                start=True, stop=True))
                            t.wait_ge(B.ev.sem, B.ev.v + half + 1)

                    @b.vector
                    def _(v, _rt=rt):
                        for half in range(2):
                            v.wait_ge(B.et.sem, B.et.v - 1 + half)
                            sl = slice(half * 512, half * 512 + 512)
                            B.ev.inc(v.scalar_tensor_tensor(
                                osb[:, sl], osb[:, sl],
                                rmc[:, _rt:_rt + 1], pssc[:, :],
                                ALU.mult, ALU.add))
                        v.wait_ge(B.ev.sem, B.ev.v)
                        B.ev.inc(v.tensor_copy(obf[:, :], osb[:, :]))
                with nc.Block() as b:
                    @b.tensor
                    def _(t):
                        t.wait_ge(B.ev.sem, B.ev.v)
                        for c in range(8):
                            i1 = t.transpose(pbt2[:, c * 128:(c + 1) * 128],
                                             obf[:, c * 128:(c + 1) * 128],
                                             B.identb[:, :])
                        B.et.inc(i1)

                    @b.scalar
                    def _(s):
                        s.wait_ge(B.et.sem, B.et.v)
                        B.es.inc(s.copy(oT[:, :], pbt2[:, :]))
                with nc.Block() as b:
                    @b.tensor
                    def _(t):
                        t.wait_ge(B.es.sem, B.es.v)
                        t.wait_ge(B.dg.sem, B.dg.v)
                        for half in range(2):
                            for oc in range(8):
                                i1 = t.matmul(
                                    psw[:, :],
                                    oT[:, oc * 128:(oc + 1) * 128],
                                    wosb[:, oc * DM + half * 512:
                                         oc * DM + half * 512 + 512],
                                    start=(oc == 0), stop=(oc == 7))
                            B.et.inc(i1)
                            t.wait_ge(B.ev.sem, B.ev.v + half + 1)

                    @b.vector
                    def _(v):
                        for half in range(2):
                            v.wait_ge(B.et.sem, B.et.v - 1 + half)
                            sl = slice(half * 512, half * 512 + 512)
                            B.ev.inc(v.tensor_add(ysb[:, sl], psw[:, :],
                                                  bob[:, sl]))
                B.bar()
                emit_epilogue_tile(B, ysb, x_in, x_out, rt * 128,
                                   xf2, osc, stats2, scr2)



def emit_l1_attn(B, x_in, x_out, W):
    nc = B.nc
    NH, HD = 8, 64
    CL = 256
    NC_ = S // CL

    with contextlib.ExitStack() as st:
        BB = Bld.__new__(Bld)
        BB.__dict__.update(B.__dict__)
        BB.st = st
        sb = BB.sbuf

        wqsb = sb("wqsb", [128, 8 * 512], BF16)
        wksb = sb("wksb", [128, 8 * 512], BF16)
        wvsb = sb("wvsb", [128, 8 * 512], BF16)
        wosb = sb("wosb", [128, 4 * DM], BF16)
        gates = sb("gates", [128, 512], F32)
        bet = sb("bet", [1, 512], F32)
        memz = sb("memz", [64, 8 * 65], F32)
        membf = sb("membf", [64, 8 * 65], BF16)

        with nc.Block() as b:
            @b.gpsimd
            def _(g):
                for ic in range(8):
                    B.dg.inc(g.dma_start(
                        wqsb[:, ic * 512:(ic + 1) * 512],
                        W["l1a_Wq"][ic * 128:(ic + 1) * 128, :]), 16)
                    B.dg.inc(g.dma_start(
                        wksb[:, ic * 512:(ic + 1) * 512],
                        W["l1a_Wk"][ic * 128:(ic + 1) * 128, :]), 16)
                    B.dg.inc(g.dma_start(
                        wvsb[:, ic * 512:(ic + 1) * 512],
                        W["l1a_Wv"][ic * 128:(ic + 1) * 128, :]), 16)
                for oc in range(4):
                    B.dg.inc(g.dma_start(
                        wosb[:, oc * DM:(oc + 1) * DM],
                        W["l1a_Wo"][oc * 128:(oc + 1) * 128, :]), 16)
                g.memset(memz[0:64, :], 0.0)
                for h in range(NH):
                    B.eg.inc(g.memset(memz[0:64, h * 65 + 64:h * 65 + 65],
                                      1.0 / 64.0))

            @b.sync
            def _(sy):
                B.ds.inc(sy.dma_start(
                    bet[0:1, :],
                    W["l1a_betas"].rearrange("(a b) -> a b", a=1)), 16)

            @b.scalar
            def _(s):
                s.wait_ge(B.ds.sem, B.ds.v)
                B.es.inc(s.activation(bet[0:1, :], bet[0:1, :], AF.Sigmoid))
        with contextlib.ExitStack() as pst:
            psx = pst.enter_context(nc.psum_tensor(B.name("psx"),
                                                   [128, 512], F32))
            with nc.Block() as b:
                @b.tensor
                def _(t):
                    t.wait_ge(B.es.sem, B.es.v)
                    B.et.inc(t.matmul(psx[:, :], B.onesf[0:1, :], bet[0:1, :],
                                      start=True, stop=True))

                @b.vector
                def _(v):
                    v.wait_ge(B.et.sem, B.et.v)
                    B.ev.inc(v.tensor_copy(gates[:, :], psx[:, :]))
        B.bar()

        xTc = sb("xTc", [128, 8 * CL], BF16)
        qsb = sb("qsb", [128, 2 * 512], F32)
        ksb = sb("ksb", [128, 2 * 512], F32)
        qbf = sb("qbf", [128, 2 * 512], BF16)
        kbf = sb("kbf", [128, 2 * 512], BF16)
        vbf = sb("vbf", [128, 2 * 512], BF16)
        sqbf = sb("sqbf", [128, 2 * 512], BF16)
        skbf = sb("skbf", [128, 2 * 512], BF16)
        qTsb = sb("qTsb", [64, 8 * CL], BF16)
        kTsb = sb("kTsb", [64, 8 * CL], BF16)
        sqTsb = sb("sqTsb", [64, 8 * CL], BF16)
        pTsb = sb("pTsb", [128, 256], BF16)
        amsb = sb("amsb", [128, 2 * 512], F32)
        adsb = sb("adsb", [128, 2 * 512], F32)
        obf = sb("obf", [128, 2 * 512], BF16)
        oTsb = sb("oTsb", [128, 512], BF16)
        scrA = sb("scrA", [128, 512], F32)
        scrB = sb("scrB", [128, 512], F32)
        rc = sb("rc", [128, 8], F32)
        xf = sb("xf", [128, DM], F32)
        xhat = sb("xhat", [128, DM], BF16)
        stats = sb("stats", [128, 4], F32)
        scr = sb("scr", [128, DM], F32)
        ysb = sb("ysb", [128, DM], F32)
        osb = sb("osb", [128, DM], F32)

        pbt = BB.psum("pbt", [128, DM], BF16)
        ps_am = BB.psum("ps_am", [128, 65], F32)
        ps_s = BB.psum("ps_s", [128, 256], F32)
        ptp = BB.psum("ptp", [128, 256], BF16)
        ps_pv = BB.psum("ps_pv", [128, 64], F32)
        ps_up = BB.psum("ps_up", [64, 65], F32)
        ps_o = BB.psum("ps_o", [128, 512], F32)
        ps_ot = BB.psum("ps_ot", [128, 512], BF16)

        for n in range(NC_):
            c0 = n * CL
            emit_load_ln_t(B, x_in, c0, 2, xTc, CL, xf, xhat, stats, scr, pbt)

            for ti in range(2):
                ev0 = B.ev.v
                with nc.Block() as b:
                    @b.tensor
                    def _(t, _ti=ti):
                        t.wait_ge(B.dg.sem, B.dg.v)
                        evtgt = [ev0 + 2, ev0 + 4, ev0 + 5]
                        for _j, wsb_t in enumerate((wqsb, wksb, wvsb)):
                            for ic in range(8):
                                i1 = t.matmul(
                                    ps_o[:, :],
                                    xTc[:, ic * CL + _ti * 128:
                                        ic * CL + _ti * 128 + 128],
                                    wsb_t[:, ic * 512:(ic + 1) * 512],
                                    start=(ic == 0), stop=(ic == 7))
                            B.et.inc(i1)
                            t.wait_ge(B.ev.sem, evtgt[_j])

                    @b.vector
                    def _(v, _ti=ti):
                        for _j, (dst_f, dst_b) in enumerate(
                                ((qsb, qbf), (ksb, kbf), (None, vbf))):
                            v.wait_ge(B.et.sem, B.et.v - 2 + _j)
                            sl = slice(_ti * 512, _ti * 512 + 512)
                            if dst_f is not None:
                                B.ev.inc(v.tensor_copy(dst_f[:, sl],
                                                       ps_o[:, :]))
                                v.wait_ge(B.ev.sem, B.ev.v)
                            B.ev.inc(v.tensor_copy(dst_b[:, sl], ps_o[:, :]))
                B.bar()

            for (src_f, dst_b) in ((qsb, sqbf), (ksb, skbf)):
                for ti in range(2):
                    with nc.Block() as b:
                        @b.vector
                        def _(v, _s=src_f, _ti=ti):
                            sl = slice(_ti * 512, _ti * 512 + 512)
                            B.ev.inc(v.tensor_scalar(scrA[:, :], _s[:, sl],
                                                     0.0, None, ALU.min))

                        @b.scalar
                        def _(s):
                            s.wait_ge(B.ev.sem, B.ev.v)
                            B.es.inc(s.activation(scrB[:, :], scrA[:, :],
                                                  AF.Exp))

                        @b.vector
                        def _(v, _s=src_f, _d=dst_b, _ti=ti):
                            sl = slice(_ti * 512, _ti * 512 + 512)
                            v.wait_ge(B.es.sem, B.es.v)
                            B.ev.inc(v.tensor_scalar(scrA[:, :], _s[:, sl],
                                                     0.0, None, ALU.max))
                            v.wait_ge(B.ev.sem, B.ev.v)
                            B.ev.inc(v.tensor_add(_d[:, sl], scrB[:, :],
                                                  scrA[:, :]))
                    B.bar()

            for (srcb, dstT) in ((qbf, qTsb), (kbf, kTsb), (sqbf, sqTsb)):
                for ti in range(2):
                    with nc.Block() as b:
                        @b.tensor
                        def _(t, _s=srcb, _ti=ti):
                            for h in range(NH):
                                i1 = t.transpose(
                                    ptp[0:64, (h % 2) * 128:
                                        (h % 2) * 128 + 128] if False else
                                    ps_ot[0:64, h * 64:h * 64 + 128]
                                    if False else
                                    ptp[0:64, 0:128],
                                    _s[:, _ti * 512 + h * 64:
                                       _ti * 512 + h * 64 + 64],
                                    B.identb[:, :])
                                B.et.inc(i1)
                                t.wait_ge(B.ev.sem, B.ev.v + h + 1)

                        @b.vector
                        def _(v, _d=dstT, _ti=ti):
                            for h in range(NH):
                                v.wait_ge(B.et.sem, B.et.v - NH + h + 1)
                                dst = strided_ap(
                                    _d[0:64, :], h * CL + _ti * 128,
                                    [[8 * CL, 64], [1, 128]])
                                B.ev.inc(v.tensor_copy(dst, ptp[0:64, 0:128]))
                    B.bar()

            with nc.Block() as b:
                @b.vector
                def _(v):
                    B.ev.inc(v.tensor_copy(membf[0:64, :], memz[0:64, :]))
            B.bar()
            if _DBG and n == 1:
                with nc.Block() as b:
                    @b.sync
                    def _(sy):
                        B.ds.inc(sy.dma_start(
                            _DBG_T["dbg_mem"][:, :], memz[0:64, :]), 16)
                        sy.wait_ge(B.ds.sem, B.ds.v)

            for h in range(NH):
                for ti in range(2):
                    with nc.Block() as b:
                        @b.tensor
                        def _(t, _h=h, _ti=ti):
                            B.et.inc(t.matmul(
                                ps_am[:, :],
                                sqTsb[0:64, _h * CL + _ti * 128:
                                      _h * CL + _ti * 128 + 128],
                                membf[0:64, _h * 65:(_h + 1) * 65],
                                start=True, stop=True))
                            B.et.inc(t.matmul(
                                ps_s[:, :],
                                qTsb[0:64, _h * CL + _ti * 128:
                                     _h * CL + _ti * 128 + 128],
                                kTsb[0:64, _h * CL:(_h + 1) * CL],
                                start=True, stop=True))

                        @b.vector
                        def _(v, _h=h, _ti=ti):
                            v.wait_ge(B.et.sem, B.et.v - 1)
                            B.ev.inc(v.reciprocal(rc[:, 0:1],
                                                  ps_am[:, 64:65]))
                            v.wait_ge(B.ev.sem, B.ev.v)
                            B.ev.inc(v.tensor_scalar(
                                amsb[:, _ti * 512 + _h * 64:
                                     _ti * 512 + _h * 64 + 64],
                                ps_am[:, 0:64], rc[:, 0:1], None, ALU.mult))
                            v.wait_ge(B.et.sem, B.et.v)
                            B.ev.inc(v.reduce_max(rc[:, 1:2], ps_s[:, :],
                                                  axis=AX.X, negate=True))
                            v.wait_ge(B.ev.sem, B.ev.v)
                            B.ev.inc(v.tensor_scalar(rc[:, 2:3], rc[:, 1:2],
                                                     0.125, None, ALU.mult))

                        @b.scalar
                        def _(s):
                            s.wait_ge(B.ev.sem, B.ev.v)
                            B.es.inc(s.activation(
                                scrA[:, 0:256],
                                ps_s[:, :], AF.Exp, bias=rc[:, 2:3],
                                scale=B.kc[:, KC_ATT:KC_ATT + 1],
                                accum_out=rc[:, 3:4]))

                        @b.vector
                        def _(v):
                            v.wait_ge(B.es.sem, B.es.v)
                            B.ev.inc(v.tensor_copy(pTsb[:, 0:256],
                                                   scrA[:, 0:256]))
                            B.ev.inc(v.reciprocal(rc[:, 4:5], rc[:, 3:4]))
                    with nc.Block() as b:
                        @b.tensor
                        def _(t, _h=h, _ti=ti):
                            t.wait_ge(B.ev.sem, B.ev.v)
                            for jb in range(2):
                                i1 = t.transpose(
                                    ptp[:, jb * 128:(jb + 1) * 128],
                                    pTsb[:, jb * 128:(jb + 1) * 128],
                                    B.identb[:, :])
                            B.et.inc(i1)

                        @b.vector
                        def _(v):
                            v.wait_ge(B.et.sem, B.et.v)
                            B.ev.inc(v.tensor_copy(oTsb[:, 0:256],
                                                   ptp[:, :]))
                    with nc.Block() as b:
                        @b.tensor
                        def _(t, _h=h, _ti=ti):
                            t.wait_ge(B.ev.sem, B.ev.v)
                            for jb in range(2):
                                i1 = t.matmul(
                                    ps_pv[:, :],
                                    oTsb[:, jb * 128:(jb + 1) * 128],
                                    vbf[:, jb * 512 + _h * 64:
                                        jb * 512 + _h * 64 + 64],
                                    start=(jb == 0), stop=(jb == 1))
                            B.et.inc(i1)

                        @b.vector
                        def _(v, _h=h, _ti=ti):
                            v.wait_ge(B.et.sem, B.et.v)
                            B.ev.inc(v.tensor_scalar(
                                adsb[:, _ti * 512 + _h * 64:
                                     _ti * 512 + _h * 64 + 64],
                                ps_pv[:, :], rc[:, 4:5], None, ALU.mult))
                    B.bar()
                with nc.Block() as b:
                    @b.tensor
                    def _(t, _h=h):
                        for ti in range(2):
                            i1 = t.matmul(ps_up[0:64, 0:64],
                                          skbf[:, ti * 512 + _h * 64:
                                               ti * 512 + _h * 64 + 64],
                                          vbf[:, ti * 512 + _h * 64:
                                              ti * 512 + _h * 64 + 64],
                                          start=(ti == 0), stop=(ti == 1))
                        B.et.inc(i1)
                        for ti in range(2):
                            i1 = t.matmul(ps_up[0:64, 64:65],
                                          skbf[:, ti * 512 + _h * 64:
                                               ti * 512 + _h * 64 + 64],
                                          B.onesb[:, 0:1],
                                          start=(ti == 0), stop=(ti == 1))
                        B.et.inc(i1)

                    @b.vector
                    def _(v, _h=h):
                        v.wait_ge(B.et.sem, B.et.v)
                        B.ev.inc(v.tensor_add(memz[0:64, _h * 65:(_h + 1) * 65],
                                              memz[0:64, _h * 65:(_h + 1) * 65],
                                              ps_up[0:64, :]))
                B.bar()

            if _DBG and n == 1:
                with nc.Block() as b:
                    @b.sync
                    def _(sy):
                        B.ds.inc(sy.dma_start(_DBG_T["dbg_am"][:, :],
                                              amsb[:, :]), 16)
                        B.ds.inc(sy.dma_start(_DBG_T["dbg_ad"][:, :],
                                              adsb[:, :]), 16)
                        sy.wait_ge(B.ds.sem, B.ds.v)

            for ti in range(2):
                with nc.Block() as b:
                    @b.vector
                    def _(v, _ti=ti):
                        sl = slice(_ti * 512, _ti * 512 + 512)
                        B.ev.inc(v.tensor_sub(scrA[:, :], amsb[:, sl],
                                              adsb[:, sl]))
                        v.wait_ge(B.ev.sem, B.ev.v)
                        B.ev.inc(v.tensor_mul(scrA[:, :], scrA[:, :],
                                              gates[:, :]))
                        v.wait_ge(B.ev.sem, B.ev.v)
                        B.ev.inc(v.tensor_add(obf[:, sl], scrA[:, :],
                                              adsb[:, sl]))
                B.bar()
                with nc.Block() as b:
                    @b.tensor
                    def _(t, _ti=ti):
                        for oc in range(4):
                            i1 = t.transpose(
                                ps_ot[:, oc * 128:(oc + 1) * 128],
                                obf[:, _ti * 512 + oc * 128:
                                    _ti * 512 + oc * 128 + 128],
                                B.identb[:, :])
                        B.et.inc(i1)

                    @b.vector
                    def _(v):
                        v.wait_ge(B.et.sem, B.et.v)
                        B.ev.inc(v.tensor_copy(oTsb[:, :], ps_ot[:, :]))
                with nc.Block() as b:
                    @b.tensor
                    def _(t, _ti=ti):
                        t.wait_ge(B.ev.sem, B.ev.v)
                        for ds_ in range(2):
                            for oc in range(4):
                                i1 = t.matmul(
                                    ps_o[:, :] if ds_ == 0 else ps_s[:, :]
                                    if False else ps_o[:, :],
                                    oTsb[:, oc * 128:(oc + 1) * 128],
                                    wosb[:, oc * DM + ds_ * 512:
                                         oc * DM + ds_ * 512 + 512],
                                    start=(oc == 0), stop=(oc == 3))
                            B.et.inc(i1)
                            t.wait_ge(B.ev.sem, B.ev.v + ds_ + 1)

                    @b.vector
                    def _(v):
                        for ds_ in range(2):
                            v.wait_ge(B.et.sem, B.et.v - 1 + ds_)
                            B.ev.inc(v.tensor_copy(
                                ysb[:, ds_ * 512:(ds_ + 1) * 512],
                                ps_o[:, :]))
                B.bar()
                emit_epilogue_tile(B, ysb, x_in, x_out, c0 + ti * 128,
                                   xf, osb, stats, scr)



def _host_tables():
    out = {}
    p = np.arange(128)[:, None]
    n = np.arange(640)[None, :]
    band = np.where((n - p >= 0) & (n - p <= 512), 0.0, NEG).astype(np.float32)
    out["band_tbl"] = band
    j = np.arange(32)
    invf = (10000.0 ** (-(2.0 * j) / 64.0)).astype(np.float32)
    t = np.arange(S, dtype=np.float32)
    ang = t[:, None] * invf[None, :]
    emb = np.concatenate([ang, ang], axis=1)
    sin1 = np.sin(emb).astype(np.float32)
    cos1 = np.cos(emb).astype(np.float32)
    out["sin_rep"] = np.ascontiguousarray(np.tile(sin1, (1, 16)))
    out["cos_rep"] = np.ascontiguousarray(np.tile(cos1, (1, 16)))
    return out


def _prep_host(inputs):
    g = {}
    f32 = np.float32
    asc = np.ascontiguousarray

    def scol(v):
        v = np.asarray(v)
        return asc(v.reshape(-1, 128).T)

    g["l1f_w1T"] = asc(np.asarray(inputs["L1_ffn_w1_w"]).T)
    g["l1f_w2T"] = asc(np.asarray(inputs["L1_ffn_w2_w"]).T)
    g["l1f_h1T"] = asc(np.asarray(inputs["L1_ffn_h1_w"]).T)
    g["l1f_h2T"] = asc(np.asarray(inputs["L1_ffn_h2_w"]).T)
    g["l1f_w3T"] = asc(np.asarray(inputs["L1_ffn_w3_w"]).T)
    g["l1f_s1"] = scol(inputs["L1_ffn_w1_s"])
    g["l1f_s2"] = scol(inputs["L1_ffn_w2_s"])
    g["l1f_sh1"] = scol(inputs["L1_ffn_h1_s"])
    g["l1f_sh2"] = scol(inputs["L1_ffn_h2_s"])
    g["l1f_s3"] = asc(np.asarray(inputs["L1_ffn_w3_s"], dtype=f32))
    g["l0f_w1"] = asc(np.asarray(inputs["L0_ffn_w1"]))
    g["l0f_w2"] = asc(np.asarray(inputs["L0_ffn_w2"]))
    g["l0f_gwT"] = asc(np.asarray(inputs["L0_ffn_gate_w"]).T)
    g["l0f_gate_b"] = asc(np.asarray(inputs["L0_ffn_gate_b"]))
    g["l0f_rms_w"] = asc(np.asarray(inputs["L0_ffn_rms_w"]))
    g["l0f_b1c"] = asc(np.asarray(inputs["L0_ffn_b1"])
                       .reshape(8, 16, 128).transpose(0, 2, 1))
    g["l0f_b2"] = asc(np.asarray(inputs["L0_ffn_b2"]))
    for k_ in ("Wq", "bq", "Wk", "bk", "Wv", "bv", "Wo", "bo"):
        g["l0a_" + k_] = asc(np.asarray(inputs["L0_attn_" + k_]))
    g["l1a_Wq"] = asc(np.asarray(inputs["L1_attn_Wq"]))
    g["l1a_Wk"] = asc(np.asarray(inputs["L1_attn_Wk"]))
    g["l1a_Wv"] = asc(np.asarray(inputs["L1_attn_Wv"]))
    g["l1a_Wo"] = asc(np.asarray(inputs["L1_attn_Wo"]))
    g["l1a_betas"] = asc(np.asarray(inputs["L1_attn_betas"]).reshape(512))
    for k in list(g):
        g[k] = np.asarray(g[k], dtype=f32)
    g["gp"] = np.ascontiguousarray(
        np.asarray(inputs["L0_attn_global_positions"], dtype=np.int32))
    g.update(_host_tables())
    return g


BLOCK_INPUTS = {
    "l1f": ["l1f_w1T", "l1f_w2T", "l1f_h1T", "l1f_h2T", "l1f_w3T",
            "l1f_s1", "l1f_s2", "l1f_sh1", "l1f_sh2", "l1f_s3"],
    "l0f": ["l0f_w1", "l0f_w2", "l0f_gwT", "l0f_gate_b", "l0f_rms_w",
            "l0f_b1c", "l0f_b2"],
    "l1a": ["l1a_Wq", "l1a_Wk", "l1a_Wv", "l1a_Wo", "l1a_betas"],
    "l0a": ["l0a_Wq", "l0a_bq", "l0a_Wk", "l0a_bk", "l0a_Wv", "l0a_bv",
            "l0a_Wo", "l0a_bo", "gp", "band_tbl", "sin_rep", "cos_rep"],
}

EMITS = {"l1f": emit_l1_ffn, "l0f": emit_l0_ffn, "l1a": emit_l1_attn,
         "l0a": emit_l0_attn}

SHAPES = {
    "l1f_w1T": [DM, 4096], "l1f_w2T": [DM, 4096],
    "l1f_h1T": [4096, 4096], "l1f_h2T": [4096, 4096],
    "l1f_w3T": [4096, DM],
    "l1f_s1": [128, 32], "l1f_s2": [128, 32],
    "l1f_sh1": [128, 32], "l1f_sh2": [128, 32],
    "l1f_s3": [DM],
    "l0f_w1": [8, DM, 2048], "l0f_w2": [8, 2048, DM],
    "l0f_gwT": [DM, 8], "l0f_gate_b": [8], "l0f_rms_w": [8],
    "l0f_b1c": [8, 128, 16], "l0f_b2": [8, DM],
    "l1a_Wq": [DM, 512], "l1a_Wk": [DM, 512], "l1a_Wv": [DM, 512],
    "l1a_Wo": [512, DM], "l1a_betas": [512],
    "l0a_Wq": [DM, DM], "l0a_bq": [DM], "l0a_Wk": [DM, 256],
    "l0a_bk": [256], "l0a_Wv": [DM, 256], "l0a_bv": [256],
    "l0a_Wo": [DM, DM], "l0a_bo": [DM], "gp": [4],
    "band_tbl": [128, 640], "sin_rep": [S, DM], "cos_rep": [S, DM],
}


_DBG = False
_DBG_T = {}


def _build(block_names):
    nc = bass.Bass("TRN2", target_bir_lowering=False)
    xin = nc.dram_tensor("xin", [S, DM], F32, kind="ExternalInput")
    xout = nc.dram_tensor("xout", [S, DM], F32, kind="ExternalOutput")
    if _DBG:
        _DBG_T["dbg_mem"] = nc.dram_tensor("dbg_mem", [64, 520], F32,
                                           kind="ExternalOutput")
        _DBG_T["dbg_am"] = nc.dram_tensor("dbg_am", [128, 1024], F32,
                                          kind="ExternalOutput")
        _DBG_T["dbg_ad"] = nc.dram_tensor("dbg_ad", [128, 1024], F32,
                                          kind="ExternalOutput")
    needed = []
    for bn in block_names:
        needed += BLOCK_INPUTS[bn]
    W = {}
    for name in needed:
        dt = I32 if name == "gp" else F32
        W[name] = nc.dram_tensor(name, SHAPES[name], dt,
                                 kind="ExternalInput")[tuple(
                                     slice(None) for _ in SHAPES[name])]
    with contextlib.ExitStack() as st:
        B = Bld(nc, st)
        emit_consts(B)
        cur_in = xin[:, :]
        for i, bn in enumerate(block_names):
            last = (i == len(block_names) - 1)
            if last:
                cur_out = xout[:, :]
            else:
                h = B.dram(f"xh{i}", [S, DM], F32)
                cur_out = h[:, :]
            EMITS[bn](B, cur_in, cur_out, W)
            cur_in = cur_out
            B.bar()
            B.dma_fence()
        with nc.Block() as b:
            @b.sync
            def _(sy):
                sy.wait_ge(B.ds.sem, B.ds.v)
                sy.wait_ge(B.dg.sem, B.dg.v)
    return nc, needed


_PROG_CACHE = {}


def _run_blocks(block_names, x_full, inputs):
    key = tuple(block_names)
    if key not in _PROG_CACHE:
        _PROG_CACHE[key] = _build(block_names)
    nc, needed = _PROG_CACHE[key]
    shared = _prep_host(inputs)
    in_maps = []
    for c in range(8):
        m = {"xin": np.ascontiguousarray(x_full[c])}
        for name in needed:
            m[name] = shared[name]
        in_maps.append(m)
    res = run_bass_kernel_spmd(nc, in_maps, core_ids=list(range(8)))
    if _DBG:
        _DBG["last"] = res.results
    return np.stack([np.asarray(r["xout"]) for r in res.results])


GROUPS = [["l0a"], ["l0f"], ["l1a"], ["l1f"]]


def kernel(**inputs):
    x = np.asarray(inputs["L0_attn_x"], dtype=np.float32)
    for grp in GROUPS:
        x = _run_blocks(grp, x, inputs)
    return x


# version 03
# speedup vs baseline: 1.0152x; 1.0152x over previous
import contextlib
import math

import numpy as np
import concourse.bass as bass
import concourse.mybir as mybir
from concourse.bass_utils import run_bass_kernel_spmd

F32 = mybir.dt.float32
BF16 = mybir.dt.bfloat16
I32 = mybir.dt.int32
AF = mybir.ActivationFunctionType
ALU = mybir.AluOpType
AX = mybir.AxisListType

S = 4096
DM = 1024
NEG = -1.0e30
LN_EPS = 1e-5
RMS_EPS = float(np.finfo(np.float32).eps)

ENGS = ("sync", "tensor", "vector", "scalar", "gpsimd")


class Ctr:

    def __init__(self, sem):
        self.sem = sem
        self.v = 0

    def inc(self, inst, n=1):
        inst.then_inc(self.sem, n)
        self.v += n
        return self.v


_NAME_CTR = [0]


class Bld:

    def __init__(self, nc, st):
        self.nc = nc
        self.st = st
        self.bars = 0
        self.ds = self.mksem("ds")
        self.dg = self.mksem("dg")
        self.dc = self.mksem("dc")
        self.et = self.mksem("et")
        self.ev = self.mksem("ev")
        self.es = self.mksem("es")
        self.eg = self.mksem("eg")
        self.w0 = self.mksem("w0")
        self.w1 = self.mksem("w1")
        self.w2 = self.mksem("w2")
        self.barsem = self.mksem("bar")
        self.identb = None
        self.identf = None
        self.iotac = None
        self.onesf = None
        self.onesb = None
        self.kc = None

    def name(self, base):
        _NAME_CTR[0] += 1
        return f"{base}{_NAME_CTR[0]}"

    def mksem(self, base):
        return Ctr(self.st.enter_context(self.nc.semaphore(self.name(base))))

    def sbuf(self, base, shape, dtype):
        return self.st.enter_context(
            self.nc.sbuf_tensor(self.name(base), shape, dtype))

    def psum(self, base, shape, dtype):
        return self.st.enter_context(
            self.nc.psum_tensor(self.name(base), shape, dtype))

    def dram(self, base, shape, dtype):
        return self.nc.dram_tensor(self.name(base), shape, dtype, kind="Internal")

    def bar(self):
        tgt = self.bars + len(ENGS)
        self.bars = tgt
        with self.nc.Block() as b:
            for en in ENGS:
                def mk(e, _t=tgt):
                    e.sem_inc(self.barsem.sem, 1)
                    e.wait_ge(self.barsem.sem, _t)
                getattr(b, en)(mk)

    def dma_fence(self):
        with self.nc.Block() as b:
            @b.sync
            def _(sy):
                sy.wait_ge(self.ds.sem, self.ds.v)
                sy.wait_ge(self.dg.sem, self.dg.v)
                sy.wait_ge(self.dc.sem, self.dc.v)


KC_LNEPS = 0
KC_MHALF = 1
KC_RMSEPS = 2
KC_TWO = 3
KC_ATT = 4
KC_ONE = 5


def emit_consts(B):
    nc = B.nc
    B.identb = B.sbuf("identb", [128, 128], BF16)
    B.identf = B.sbuf("identf", [128, 128], F32)
    B.iotac = B.sbuf("iotac", [128, 1], F32)
    B.onesf = B.sbuf("onesf", [1, 128], F32)
    B.onescf = B.sbuf("onescf", [128, 1], F32)
    B.onesb = B.sbuf("onesb", [128, 1], BF16)
    B.kc = B.sbuf("kc", [128, 8], F32)
    with nc.Block() as b:
        @b.gpsimd
        def _(g):
            g.memset(B.identb[:, :], 0.0)
            g.affine_select(out=B.identb[:, :], in_=B.identb[:, :],
                            compare_op=ALU.not_equal, fill=1.0, base=0,
                            pattern=[[-1, 128]], channel_multiplier=1)
            g.memset(B.identf[:, :], 0.0)
            g.affine_select(out=B.identf[:, :], in_=B.identf[:, :],
                            compare_op=ALU.not_equal, fill=1.0, base=0,
                            pattern=[[-1, 128]], channel_multiplier=1)
            g.iota(B.iotac[:, :], pattern=[[0, 1]], base=0, channel_multiplier=1,
                   allow_small_or_imprecise_dtypes=True)
            g.memset(B.onesf[0:1, :], 1.0)
            g.memset(B.onescf[:, :], 1.0)
            g.memset(B.onesb[:, :], 1.0)
            g.memset(B.kc[:, KC_LNEPS:KC_LNEPS + 1], LN_EPS)
            g.memset(B.kc[:, KC_MHALF:KC_MHALF + 1], -0.5)
            g.memset(B.kc[:, KC_RMSEPS:KC_RMSEPS + 1], RMS_EPS)
            g.memset(B.kc[:, KC_TWO:KC_TWO + 1], 2.0)
            g.memset(B.kc[:, KC_ATT:KC_ATT + 1], 0.125)
            g.memset(B.kc[:, KC_ONE:KC_ONE + 1], 1.0)
    B.bar()


def _flat(ap):
    nd = len(ap.shape)
    if nd == 1:
        return ap
    if nd == 2:
        return ap.rearrange("a b -> (a b)")
    if nd == 3:
        return ap.rearrange("a b c -> (a b c)")
    raise ValueError(nd)


def emit_cast_dram(B, dst, src):
    nc = B.nc
    n = 1
    for d in src.shape:
        n *= d
    sf = _flat(src)
    df = _flat(dst)
    with nc.Block() as b:
        @b.gpsimd
        def _(g):
            off = 0
            while off < n:
                m = min(2 * 1024 * 1024, n - off)
                B.dg.inc(g.dma_start(
                    df[off:off + m].rearrange("(a b) -> a b", a=1),
                    sf[off:off + m].rearrange("(a b) -> a b", a=1)), 16)
                off += m


def strided_ap(base_ap, extra_off, dims):
    return bass.AP(base_ap.tensor, base_ap.offset + extra_off, dims)


def emit_ln(B, src, dst, stats, scr, dve="vector"):
    nc = B.nc
    if not isinstance(src, bass.AP):
        src = src[:, :]
    if not isinstance(dst, bass.AP):
        dst = dst[:, :]
    if not isinstance(scr, bass.AP):
        scr = scr[:, :]
    D = src.shape[-1] if hasattr(src, "shape") else DM
    rinv = 1.0 / D
    with nc.Block() as b:
        @b.vector
        def _(v):
            B.ev.inc(v.reduce_sum(stats[:, 0:1], src, axis=AX.X, negate=True))

        @b.scalar
        def _(s):
            B.es.inc(s.activation(scr, src, AF.Square,
                                  accum_out=stats[:, 1:2]))
    with nc.Block() as b:
        @b.vector
        def _(v):
            v.wait_ge(B.es.sem, B.es.v)
            v.wait_ge(B.ev.sem, B.ev.v)
            B.ev.inc(v.tensor_scalar(stats[:, 0:1], stats[:, 0:1], rinv, None,
                                     ALU.mult))
            v.wait_ge(B.ev.sem, B.ev.v)
            B.ev.inc(v.tensor_mul(stats[:, 2:3], stats[:, 0:1], stats[:, 0:1]))
            B.ev.inc(v.tensor_scalar(stats[:, 1:2], stats[:, 1:2], rinv, None,
                                     ALU.mult))
            v.wait_ge(B.ev.sem, B.ev.v)
            B.ev.inc(v.tensor_sub(stats[:, 1:2], stats[:, 1:2], stats[:, 2:3]))

        @b.scalar
        def _(s):
            s.wait_ge(B.ev.sem, B.ev.v)
            B.es.inc(s.activation(stats[:, 2:3], stats[:, 1:2], AF.Ln,
                                  bias=B.kc[:, KC_LNEPS:KC_LNEPS + 1]))
            s.wait_ge(B.es.sem, B.es.v)
            B.es.inc(s.activation(stats[:, 3:4], stats[:, 2:3], AF.Exp,
                                  scale=B.kc[:, KC_MHALF:KC_MHALF + 1]))
    with nc.Block() as b:
        @b.vector
        def _(v):
            v.wait_ge(B.es.sem, B.es.v)
            B.ev.inc(v.tensor_scalar(dst, src, stats[:, 0:1], stats[:, 3:4],
                                     ALU.add, ALU.mult))
    return B.ev.v


def emit_load_ln_t(B, x_in, r0, ntt, xT, ts_free, xf, xhat, stats, scr, pbt):
    nc = B.nc
    for tt in range(ntt):
        with nc.Block() as b:
            @b.sync
            def _(sy, _tt=tt):
                B.ds.inc(sy.dma_start(
                    xf[:, :], x_in[r0 + _tt * 128:r0 + _tt * 128 + 128, :]), 16)

            @b.vector
            def _(v):
                v.wait_ge(B.ds.sem, B.ds.v)

            @b.scalar
            def _(s):
                s.wait_ge(B.ds.sem, B.ds.v)
        emit_ln(B, xf[:, :], xhat[:, :], stats, scr)
        with nc.Block() as b:
            @b.tensor
            def _(t, _tt=tt):
                t.wait_ge(B.ev.sem, B.ev.v)
                for c in range(8):
                    inst = t.transpose(pbt[:, c * 128:(c + 1) * 128],
                                       xhat[:, c * 128:(c + 1) * 128],
                                       B.identb[:, :])
                B.et.inc(inst)

            @b.scalar
            def _(s, _tt=tt):
                s.wait_ge(B.et.sem, B.et.v)
                dst = strided_ap(xT[:, :], _tt * 128,
                                 [[xT.shape[1], 128], [ts_free, 8], [1, 128]])
                B.es.inc(s.copy(dst, pbt[:, :]))
        B.bar()


def emit_epilogue_tile(B, ysb, x_in, x_out, row0, xf, osb, stats, scr):
    nc = B.nc
    with nc.Block() as b:
        @b.sync
        def _(sy):
            B.ds.inc(sy.dma_start(xf[:, :], x_in[row0:row0 + 128, :]), 16)
    emit_ln(B, ysb[:, :], scr[:, :], stats, osb[:, :])
    with nc.Block() as b:
        @b.vector
        def _(v):
            v.wait_ge(B.ds.sem, B.ds.v)
            v.wait_ge(B.ev.sem, B.ev.v)
            B.ev.inc(v.tensor_add(osb[:, :], scr[:, :], xf[:, :]))

        @b.sync
        def _(sy):
            sy.wait_ge(B.ev.sem, B.ev.v)
            B.ds.inc(sy.dma_start(x_out[row0:row0 + 128, :], osb[:, :]), 16)
    B.bar()



def emit_l1_ffn(B, x_in, x_out, W):
    nc = B.nc
    DH = 4096
    TS = 512
    NSW = S // TS
    NTT = TS // 128

    w1b = B.dram("l1fw1b", [DM, DH], BF16)
    w2b = B.dram("l1fw2b", [DM, DH], BF16)
    h1b = B.dram("l1fh1b", [DH, DH], BF16)
    h2b = B.dram("l1fh2b", [DH, DH], BF16)
    w3b = B.dram("l1fw3b", [DH, DM], BF16)
    emit_cast_dram(B, w1b[:, :], W["l1f_w1T"])
    emit_cast_dram(B, w2b[:, :], W["l1f_w2T"])
    emit_cast_dram(B, h1b[:, :], W["l1f_h1T"])
    emit_cast_dram(B, h2b[:, :], W["l1f_h2T"])
    emit_cast_dram(B, w3b[:, :], W["l1f_w3T"])
    B.dma_fence()

    with contextlib.ExitStack() as st:
        BB = Bld.__new__(Bld)
        BB.__dict__.update(B.__dict__)
        BB.st = st
        sb = BB.sbuf
        s1c = sb("s1c", [128, 32], F32)
        s2c = sb("s2c", [128, 32], F32)
        sh1c = sb("sh1c", [128, 32], F32)
        sh2c = sb("sh2c", [128, 32], F32)
        s3b = sb("s3b", [128, DM], F32)
        s3row = sb("s3row", [1, DM], F32)
        with nc.Block() as b:
            @b.sync
            def _(sy):
                B.ds.inc(sy.dma_start(s1c[:, :], W["l1f_s1"]), 16)
                B.ds.inc(sy.dma_start(s2c[:, :], W["l1f_s2"]), 16)
                B.ds.inc(sy.dma_start(sh1c[:, :], W["l1f_sh1"]), 16)
                B.ds.inc(sy.dma_start(sh2c[:, :], W["l1f_sh2"]), 16)
                B.ds.inc(sy.dma_start(
                    s3row[0:1, :],
                    W["l1f_s3"].rearrange("(a b) -> a b", a=1)), 16)
        ps3 = BB.psum("ps3", [128, 512], F32)
        for half in range(2):
            with nc.Block() as b:
                @b.tensor
                def _(t, _h=half):
                    t.wait_ge(B.ds.sem, B.ds.v)
                    B.et.inc(t.matmul(ps3[:, :], B.onesf[0:1, :],
                                      s3row[0:1, _h * 512:(_h + 1) * 512],
                                      start=True, stop=True))

                @b.vector
                def _(v, _h=half):
                    v.wait_ge(B.et.sem, B.et.v)
                    B.ev.inc(v.tensor_copy(s3b[:, _h * 512:(_h + 1) * 512],
                                           ps3[:, :]))
        B.bar()

        xT = sb("xT", [128, 8 * TS], BF16)
        hT = sb("hT", [128, 32 * TS], BF16)
        g1T = sb("g1T", [128, 32 * TS], BF16)
        wsa = sb("wsa", [128, 8 * 512], BF16)
        wsb_ = sb("wsb", [128, 8 * 512], BF16)
        wbig = sb("wbig", [128, 32 * 512], BF16)
        us = sb("us", [128, TS], F32)
        ysb = sb("ysb", [128, NTT * DM], F32)
        xf = sb("xf", [128, DM], F32)
        xhat = sb("xhat", [128, DM], BF16)
        stats = sb("stats", [128, 4], F32)
        scr = sb("scr", [128, DM], F32)
        osb = sb("osb", [128, DM], F32)
        pbt = BB.psum("pbt", [128, DM], BF16)
        py1 = [BB.psum("py1", [128, TS], F32) for _ in range(2)]
        py2 = [BB.psum("py2", [128, TS], F32) for _ in range(2)]

        for sw in range(NSW):
            r0 = sw * TS
            emit_load_ln_t(B, x_in, r0, NTT, xT, TS, xf, xhat, stats, scr, pbt)

            for og in range(8):
                with nc.Block() as b:
                    @b.sync
                    def _(sy, _og=og):
                        for ic in range(8):
                            B.ds.inc(sy.dma_start(
                                wsa[:, ic * 512:(ic + 1) * 512],
                                w1b[ic * 128:(ic + 1) * 128,
                                    _og * 512:(_og + 1) * 512]), 16)
                        for ic in range(8):
                            B.ds.inc(sy.dma_start(
                                wsb_[:, ic * 512:(ic + 1) * 512],
                                w2b[ic * 128:(ic + 1) * 128,
                                    _og * 512:(_og + 1) * 512]), 16)
                for j in range(4):
                    oc = og * 4 + j
                    with nc.Block() as b:
                        @b.tensor
                        def _(t, _j=j, _oc=oc):
                            t.wait_ge(B.ds.sem, B.ds.v)
                            for ic in range(8):
                                i1 = t.matmul(py1[_j % 2][:, :],
                                              wsa[:, ic * 512 + _j * 128:
                                                  ic * 512 + _j * 128 + 128],
                                              xT[:, ic * TS:(ic + 1) * TS],
                                              start=(ic == 0), stop=(ic == 7))
                            B.et.inc(i1)
                            for ic in range(8):
                                i2 = t.matmul(py2[_j % 2][:, :],
                                              wsb_[:, ic * 512 + _j * 128:
                                                   ic * 512 + _j * 128 + 128],
                                              xT[:, ic * TS:(ic + 1) * TS],
                                              start=(ic == 0), stop=(ic == 7))
                            B.et.inc(i2)

                        @b.scalar
                        def _(s, _j=j, _oc=oc):
                            s.wait_ge(B.et.sem, B.et.v - 1)
                            B.es.inc(s.activation(us[:, :], py1[_j % 2][:, :],
                                                  AF.Silu,
                                                  scale=s1c[:, _oc:_oc + 1]))

                        @b.vector
                        def _(v, _j=j, _oc=oc):
                            v.wait_ge(B.et.sem, B.et.v)
                            v.wait_ge(B.es.sem, B.es.v)
                            B.ev.inc(v.scalar_tensor_tensor(
                                hT[:, _oc * TS:(_oc + 1) * TS],
                                py2[_j % 2][:, :], s2c[:, _oc:_oc + 1],
                                us[:, :], ALU.mult, ALU.mult))
                    B.bar()

            for (wdram, scol, srcT, dstT) in ((h1b, sh1c, hT, g1T),
                                              (h2b, sh2c, g1T, hT)):
                for og in range(8):
                    with nc.Block() as b:
                        @b.sync
                        def _(sy, _og=og, _w=wdram):
                            for hc in range(32):
                                B.ds.inc(sy.dma_start(
                                    wbig[:, hc * 512:(hc + 1) * 512],
                                    _w[hc * 128:(hc + 1) * 128,
                                       _og * 512:(_og + 1) * 512]), 16)
                    for j in range(4):
                        oc = og * 4 + j
                        with nc.Block() as b:
                            @b.tensor
                            def _(t, _j=j, _src=srcT):
                                t.wait_ge(B.ds.sem, B.ds.v)
                                for hc in range(32):
                                    i1 = t.matmul(
                                        py1[_j % 2][:, :],
                                        wbig[:, hc * 512 + _j * 128:
                                             hc * 512 + _j * 128 + 128],
                                        _src[:, hc * TS:(hc + 1) * TS],
                                        start=(hc == 0), stop=(hc == 31))
                                B.et.inc(i1)

                            @b.scalar
                            def _(s, _j=j, _oc=oc, _sc=scol, _dst=dstT):
                                s.wait_ge(B.et.sem, B.et.v)
                                B.es.inc(s.activation(
                                    _dst[:, _oc * TS:(_oc + 1) * TS],
                                    py1[_j % 2][:, :], AF.Gelu,
                                    scale=_sc[:, _oc:_oc + 1]))
                        B.bar()

            for do in range(2):
                with nc.Block() as b:
                    @b.sync
                    def _(sy, _do=do):
                        for hc in range(32):
                            B.ds.inc(sy.dma_start(
                                wbig[:, hc * 512:(hc + 1) * 512],
                                w3b[hc * 128:(hc + 1) * 128,
                                    _do * 512:(_do + 1) * 512]), 16)
                for tt in range(NTT):
                    with nc.Block() as b:
                        @b.tensor
                        def _(t, _tt=tt, _do=do):
                            t.wait_ge(B.ds.sem, B.ds.v)
                            for hc in range(32):
                                i1 = t.matmul(
                                    py1[_tt % 2][:, :],
                                    hT[:, hc * TS + _tt * 128:
                                       hc * TS + _tt * 128 + 128],
                                    wbig[:, hc * 512:(hc + 1) * 512],
                                    start=(hc == 0), stop=(hc == 31))
                            B.et.inc(i1)

                        @b.vector
                        def _(v, _tt=tt, _do=do):
                            v.wait_ge(B.et.sem, B.et.v)
                            B.ev.inc(v.tensor_mul(
                                ysb[:, _tt * DM + _do * 512:
                                    _tt * DM + _do * 512 + 512],
                                py1[_tt % 2][:, :],
                                s3b[:, _do * 512:(_do + 1) * 512]))
                    B.bar()

            for tt in range(NTT):
                emit_epilogue_tile(B, ysb[:, tt * DM:(tt + 1) * DM],
                                   x_in, x_out, r0 + tt * 128,
                                   xf, osb, stats, scr)



def emit_l0_ffn(B, x_in, x_out, W):
    nc = B.nc
    E, H = 8, 2048
    TS = 1024
    NSW = S // TS
    NTT = TS // 128

    w1b = B.dram("l0fw1b", [E, DM, H], BF16)
    w2b = B.dram("l0fw2b", [E, H, DM], BF16)
    emit_cast_dram(B, w1b[:, :, :], W["l0f_w1"])
    emit_cast_dram(B, w2b[:, :, :], W["l0f_w2"])

    with contextlib.ExitStack() as st:
        BB = Bld.__new__(Bld)
        BB.__dict__.update(B.__dict__)
        BB.st = st
        sb = BB.sbuf

        gwT = sb("gwT", [128, 64], BF16)
        gbb = sb("gbb", [128, 8], F32)
        rwb = sb("rwb", [128, 8], F32)
        b2b = sb("b2b", [128, DM], F32)
        b1c = sb("b1c", [128, 128], F32)
        row8 = sb("row8", [1, 16], F32)
        b2sb = sb("b2sb", [8, DM], F32)
        b2row = sb("b2row", [1, DM], F32)
        psx = BB.psum("psx", [128, 512], F32)

        with nc.Block() as b:
            @b.gpsimd
            def _(g):
                for ic in range(8):
                    B.dg.inc(g.dma_start(
                        gwT[:, ic * 8:(ic + 1) * 8],
                        W["l0f_gwT"][ic * 128:(ic + 1) * 128, :]), 16)

            @b.sync
            def _(sy):
                B.ds.inc(sy.dma_start(
                    row8[0:1, 0:8],
                    W["l0f_gate_b"].rearrange("(a b) -> a b", a=1)), 16)
                B.ds.inc(sy.dma_start(
                    row8[0:1, 8:16],
                    W["l0f_rms_w"].rearrange("(a b) -> a b", a=1)), 16)
                B.ds.inc(sy.dma_start(b2sb[:, :], W["l0f_b2"]), 16)
                B.ds.inc(sy.dma_start(
                    b1c[:, :].rearrange("p (e c) -> p e c", e=8),
                    W["l0f_b1c"].rearrange("e p c -> p e c")), 16)
        with nc.Block() as b:
            @b.tensor
            def _(t):
                t.wait_ge(B.ds.sem, B.ds.v)
                B.et.inc(t.matmul(psx[:, 0:8], B.onesf[0:1, :],
                                  row8[0:1, 0:8], start=True, stop=True))
                B.et.inc(t.matmul(psx[:, 8:16], B.onesf[0:1, :],
                                  row8[0:1, 8:16], start=True, stop=True))

            @b.vector
            def _(v):
                v.wait_ge(B.et.sem, B.et.v)
                B.ev.inc(v.tensor_copy(gbb[:, :], psx[:, 0:8]))
                B.ev.inc(v.tensor_copy(rwb[:, :], psx[:, 8:16]))
        with nc.Block() as b:
            @b.tensor
            def _(t):
                t.wait_ge(B.ev.sem, B.ev.v)
                for hf in range(2):
                    B.et.inc(t.matmul(psx[0:1, 0:512], B.onescf[0:8, 0:1],
                                      b2sb[:, hf * 512:(hf + 1) * 512],
                                      start=True, stop=True))
                    t.wait_ge(B.ev.sem, B.ev.v + hf + 1)

            @b.vector
            def _(v):
                for hf in range(2):
                    v.wait_ge(B.et.sem, B.et.v - 1 + hf)
                    B.ev.inc(v.tensor_copy(
                        b2row[0:1, hf * 512:(hf + 1) * 512], psx[0:1, 0:512]))
        B.bar()
        with nc.Block() as b:
            @b.tensor
            def _(t):
                for hf in range(2):
                    B.et.inc(t.matmul(psx[:, 0:512], B.onesf[0:1, :],
                                      b2row[0:1, hf * 512:(hf + 1) * 512],
                                      start=True, stop=True))
                    t.wait_ge(B.ev.sem, B.ev.v + hf + 1)

            @b.vector
            def _(v):
                for hf in range(2):
                    v.wait_ge(B.et.sem, B.et.v - 1 + hf)
                    B.ev.inc(v.tensor_copy(
                        b2b[:, hf * 512:(hf + 1) * 512], psx[:, 0:512]))
        B.bar()

        xT = sb("xT", [128, 8 * TS], BF16)
        hT = sb("hT", [128, 16 * TS], BF16)
        yacc = sb("yacc", [128, NTT * DM], F32)
        w1sb = sb("w1sb", [128, 8 * H], BF16)
        w2sb = sb("w2sb", [128, 16 * DM], BF16)
        ngc = sb("ngc", [128, NTT], F32)
        lg = sb("lg", [128, 8], F32)
        scr8 = sb("scr8", [128, 8], F32)
        gst = sb("gst", [128, 4], F32)
        xf = sb("xf", [128, DM], F32)
        xhat = sb("xhat", [128, DM], BF16)
        stats = sb("stats", [128, 4], F32)
        scr = sb("scr", [128, DM], F32)
        ysb = sb("ysb", [128, DM], F32)
        osb = sb("osb", [128, DM], F32)
        pbt = BB.psum("pbt", [128, DM], BF16)
        psg = BB.psum("psg", [128, 8], F32)
        ph = [BB.psum("ph", [128, 512], F32) for _ in range(2)]
        py = [BB.psum("py", [128, 512], F32) for _ in range(2)]

        for sw in range(NSW):
            r0 = sw * TS
            emit_load_ln_t(B, x_in, r0, NTT, xT, TS, xf, xhat, stats, scr, pbt)

            for tt in range(NTT):
                with nc.Block() as b:
                    @b.tensor
                    def _(t, _tt=tt):
                        for ic in range(8):
                            i1 = t.matmul(psg[:, :],
                                          xT[:, ic * TS + _tt * 128:
                                             ic * TS + _tt * 128 + 128],
                                          gwT[:, ic * 8:(ic + 1) * 8],
                                          start=(ic == 0), stop=(ic == 7))
                        B.et.inc(i1)

                    @b.vector
                    def _(v):
                        v.wait_ge(B.et.sem, B.et.v)
                        B.ev.inc(v.tensor_add(lg[:, :], gbb[:, :], psg[:, :]))

                    @b.scalar
                    def _(s):
                        s.wait_ge(B.ev.sem, B.ev.v)
                        B.es.inc(s.activation(scr8[:, :], lg[:, :], AF.Square,
                                              accum_out=gst[:, 0:1]))
                        s.wait_ge(B.es.sem, B.es.v)
                        B.es.inc(s.activation(
                            gst[:, 1:2], gst[:, 0:1], AF.Ln,
                            bias=B.kc[:, KC_RMSEPS:KC_RMSEPS + 1],
                            scale=B.kc[:, KC_ATT:KC_ATT + 1]))
                        s.wait_ge(B.es.sem, B.es.v)
                        B.es.inc(s.activation(
                            gst[:, 2:3], gst[:, 1:2], AF.Exp,
                            scale=B.kc[:, KC_MHALF:KC_MHALF + 1]))

                    @b.vector
                    def _(v):
                        v.wait_ge(B.es.sem, B.es.v)
                        B.ev.inc(v.tensor_scalar(lg[:, :], lg[:, :],
                                                 gst[:, 2:3], None, ALU.mult))
                        v.wait_ge(B.ev.sem, B.ev.v)
                        B.ev.inc(v.tensor_mul(lg[:, :], lg[:, :], rwb[:, :]))
                        v.wait_ge(B.ev.sem, B.ev.v)
                        B.ev.inc(v.reduce_max(gst[:, 0:1], lg[:, :],
                                              axis=AX.X, negate=True))
                        v.wait_ge(B.ev.sem, B.ev.v)
                        B.ev.inc(v.tensor_scalar(gst[:, 1:2], gst[:, 0:1],
                                                 2.0, None, ALU.mult))

                    @b.scalar
                    def _(s):
                        s.wait_ge(B.ev.sem, B.ev.v)
                        B.es.inc(s.activation(scr8[:, :], lg[:, :], AF.Exp,
                                              bias=gst[:, 1:2],
                                              scale=B.kc[:, KC_TWO:KC_TWO + 1],
                                              accum_out=gst[:, 2:3]))

                    @b.vector
                    def _(v, _tt=tt):
                        v.wait_ge(B.es.sem, B.es.v)
                        B.ev.inc(v.tensor_scalar(gst[:, 3:4], gst[:, 2:3],
                                                 1e-6, 1.0, ALU.mult, ALU.add))
                        v.wait_ge(B.ev.sem, B.ev.v)
                        B.ev.inc(v.reciprocal(ngc[:, _tt:_tt + 1],
                                              gst[:, 3:4]))
                B.bar()

            for e in range(8):
                with nc.Block() as b:
                    @b.sync
                    def _(sy, _e=e):
                        for ic in range(8):
                            B.ds.inc(sy.dma_start(
                                w1sb[:, ic * H:(ic + 1) * H],
                                w1b[_e, ic * 128:(ic + 1) * 128, :]), 16)
                        for hc in range(16):
                            B.ds.inc(sy.dma_start(
                                w2sb[:, hc * DM:(hc + 1) * DM],
                                w2b[_e, hc * 128:(hc + 1) * 128, :]), 16)
                et0, es0 = B.et.v, B.es.v
                with nc.Block() as b:
                    @b.tensor
                    def _(t, _e=e):
                        t.wait_ge(B.ds.sem, B.ds.v)
                        for j in range(32):
                            hc, tsub = j // 2, j % 2
                            if j >= 2:
                                t.wait_ge(B.es.sem, es0 + j - 1)
                            for ic in range(8):
                                i1 = t.matmul(
                                    ph[j % 2][:, :],
                                    w1sb[:, ic * H + hc * 128:
                                         ic * H + hc * 128 + 128],
                                    xT[:, ic * TS + tsub * 512:
                                       ic * TS + tsub * 512 + 512],
                                    start=(ic == 0), stop=(ic == 7))
                            B.et.inc(i1)

                    @b.scalar
                    def _(s, _e=e):
                        for j in range(32):
                            hc, tsub = j // 2, j % 2
                            s.wait_ge(B.et.sem, et0 + j + 1)
                            B.es.inc(s.activation(
                                hT[:, hc * TS + tsub * 512:
                                   hc * TS + tsub * 512 + 512],
                                ph[j % 2][:, :], AF.Gelu,
                                bias=b1c[:, _e * 16 + hc:_e * 16 + hc + 1]))
                B.bar()
                et0, ev0 = B.et.v, B.ev.v
                with nc.Block() as b:
                    @b.tensor
                    def _(t):
                        for j in range(16):
                            tt, ds_ = j // 2, j % 2
                            if j >= 2:
                                t.wait_ge(B.ev.sem, ev0 + j - 1)
                            for hc in range(16):
                                i1 = t.matmul(
                                    py[j % 2][:, :],
                                    hT[:, hc * TS + tt * 128:
                                       hc * TS + tt * 128 + 128],
                                    w2sb[:, hc * DM + ds_ * 512:
                                         hc * DM + ds_ * 512 + 512],
                                    start=(hc == 0), stop=(hc == 15))
                            B.et.inc(i1)

                    @b.vector
                    def _(v, _e=e):
                        for j in range(16):
                            tt, ds_ = j // 2, j % 2
                            v.wait_ge(B.et.sem, et0 + j + 1)
                            dst = yacc[:, tt * DM + ds_ * 512:
                                       tt * DM + ds_ * 512 + 512]
                            if _e == 0:
                                B.ev.inc(v.tensor_copy(dst, py[j % 2][:, :]))
                            else:
                                B.ev.inc(v.tensor_add(dst, dst,
                                                      py[j % 2][:, :]))
                B.bar()

            for tt in range(NTT):
                with nc.Block() as b:
                    @b.vector
                    def _(v, _tt=tt):
                        B.ev.inc(v.tensor_add(
                            ysb[:, :], yacc[:, _tt * DM:(_tt + 1) * DM],
                            b2b[:, :]))
                        v.wait_ge(B.ev.sem, B.ev.v)
                        B.ev.inc(v.tensor_scalar(ysb[:, :], ysb[:, :],
                                                 ngc[:, _tt:_tt + 1], None,
                                                 ALU.mult))

                    @b.scalar
                    def _(s):
                        s.wait_ge(B.ev.sem, B.ev.v)
                emit_epilogue_tile(B, ysb, x_in, x_out, r0 + tt * 128,
                                   xf, osb, stats, scr)



def emit_l0_attn(B, x_in, x_out, W):
    nc = B.nc
    NH, G, HD = 16, 4, 64

    with contextlib.ExitStack() as st:
        BB = Bld.__new__(Bld)
        BB.__dict__.update(B.__dict__)
        BB.st = st
        sb = BB.sbuf

        qTp = sb("qTp", [128, 8 * S], BF16)
        kTp = sb("kTp", [128, 4 * S], BF16)
        vsb = sb("vsb", [128, 32 * 256], BF16)
        band_sb = sb("band", [128, 640], F32)
        ohb = sb("ohb", [128, 32 * 4], BF16)
        rmc = sb("rmc", [128, 32], F32)
        gpb = sb("gpb", [128, 4], F32)
        dupaddb = sb("dupaddb", [128, 4], F32)
        keepbf = sb("keepbf", [128, 4], F32)
        gpi = sb("gpi", [1, 4], I32)
        gpf = sb("gpf", [1, 4], F32)
        dup01 = sb("dup01", [1, 12], F32)
        brow = sb("brow", [1, 2560], F32)
        bqb = sb("bqb", [128, DM], F32)
        bkb = sb("bkb", [128, 256], F32)
        bvb = sb("bvb", [128, 256], F32)
        bob = sb("bob", [128, DM], F32)
        qgT = sb("qgT", [64, 64], BF16)
        kgT = sb("kgT", [128, 16], BF16)
        vgg = sb("vgg", [4, 256], BF16)
        ogsb = sb("ogsb", [4, DM], F32)
        ogbf = sb("ogbf", [4, DM], BF16)

        with nc.Block() as b:
            @b.sync
            def _(sy):
                B.ds.inc(sy.dma_start(band_sb[:, :], W["band_tbl"]), 16)
                B.ds.inc(sy.dma_start(
                    gpi[0:1, :], W["gp"].rearrange("(a b) -> a b", a=1)), 16)
                off = 0
                for nm, ln_ in (("l0a_bq", DM), ("l0a_bk", 256),
                                ("l0a_bv", 256), ("l0a_bo", DM)):
                    B.ds.inc(sy.dma_start(
                        brow[0:1, off:off + ln_],
                        W[nm].rearrange("(a b) -> a b", a=1)), 16)
                    off += ln_

            @b.scalar
            def _(s):
                s.wait_ge(B.ds.sem, B.ds.v)
                B.es.inc(s.copy(gpf[0:1, :], gpi[0:1, :]))

            @b.vector
            def _(v):
                v.wait_ge(B.es.sem, B.es.v)
                B.ev.inc(v.memset(dup01[0:1, :], 0.0))
                for k in range(1, 4):
                    for k2 in range(k):
                        v.wait_ge(B.ev.sem, B.ev.v)
                        B.ev.inc(v.tensor_tensor(
                            dup01[0:1, 4 + k:5 + k], gpf[0:1, k:k + 1],
                            gpf[0:1, k2:k2 + 1], ALU.is_equal))
                        v.wait_ge(B.ev.sem, B.ev.v)
                        B.ev.inc(v.tensor_max(dup01[0:1, k:k + 1],
                                              dup01[0:1, k:k + 1],
                                              dup01[0:1, 4 + k:5 + k]))
                v.wait_ge(B.ev.sem, B.ev.v)
                B.ev.inc(v.tensor_scalar(dup01[0:1, 8:12], dup01[0:1, 0:4],
                                         -1.0, 1.0, ALU.mult, ALU.add))
                B.ev.inc(v.tensor_scalar(dup01[0:1, 0:4], dup01[0:1, 0:4],
                                         NEG, None, ALU.mult))
        with contextlib.ExitStack() as pst:
            psx = pst.enter_context(
                nc.psum_tensor(B.name("psx"), [128, 512], F32))
            bc_jobs = [(gpb[:, :], gpf[0:1, :]),
                       (dupaddb[:, :], dup01[0:1, 0:4]),
                       (keepbf[:, :], dup01[0:1, 8:12]),
                       (bkb[:, :], brow[0:1, DM:DM + 256]),
                       (bvb[:, :], brow[0:1, DM + 256:DM + 512]),
                       (bqb[:, 0:512], brow[0:1, 0:512]),
                       (bqb[:, 512:DM], brow[0:1, 512:DM]),
                       (bob[:, 0:512], brow[0:1, DM + 512:DM + 1024]),
                       (bob[:, 512:DM], brow[0:1, DM + 1024:DM + 1536])]
            for ji, (dst, srow) in enumerate(bc_jobs):
                with nc.Block() as b:
                    @b.tensor
                    def _(t, _d=dst, _s=srow):
                        t.wait_ge(B.ev.sem, B.ev.v)
                        B.et.inc(t.matmul(psx[:, 0:_s.shape[-1]],
                                          B.onesf[0:1, :], _s,
                                          start=True, stop=True))
                        t.wait_ge(B.ev.sem, B.ev.v + 1)

                    @b.vector
                    def _(v, _d=dst):
                        v.wait_ge(B.et.sem, B.et.v)
                        B.ev.inc(v.tensor_copy(_d, psx[:, 0:_d.shape[-1]]))
        B.bar()
        with nc.Block() as b:
            @b.vector
            def _(v):
                for b32 in range(32):
                    B.ev.inc(v.tensor_scalar(rmc[:, 0:4], gpb[:, :],
                                             B.iotac[:, 0:1],
                                             float(128 * b32),
                                             ALU.subtract, ALU.subtract))
                    v.wait_ge(B.ev.sem, B.ev.v)
                    B.ev.inc(v.tensor_scalar(rmc[:, 0:4], rmc[:, 0:4], 0.0,
                                             None, ALU.is_equal))
                    v.wait_ge(B.ev.sem, B.ev.v)
                    B.ev.inc(v.tensor_mul(ohb[:, b32 * 4:(b32 + 1) * 4],
                                          rmc[:, 0:4], keepbf[:, :]))
                    v.wait_ge(B.ev.sem, B.ev.v)
        with nc.Block() as b:
            @b.vector
            def _(v):
                for b32 in range(32):
                    B.ev.inc(v.reduce_sum(rmc[:, b32:b32 + 1],
                                          ohb[:, b32 * 4:(b32 + 1) * 4],
                                          axis=AX.X))
                v.wait_ge(B.ev.sem, B.ev.v)
                B.ev.inc(v.tensor_scalar(rmc[:, :], rmc[:, :], -1.0, 1.0,
                                         ALU.mult, ALU.add))
        B.bar()

        with contextlib.ExitStack() as ast:
            asb = lambda nm, sh, dt: ast.enter_context(
                nc.sbuf_tensor(B.name(nm), sh, dt))
            wqsb = asb("wqsb", [128, 8 * DM], BF16)
            wksb = asb("wksb", [128, 8 * 256], BF16)
            wvsb = asb("wvsb", [128, 8 * 256], BF16)
            xf = asb("xf", [128, DM], F32)
            xhat = asb("xhat", [128, DM], BF16)
            xTt = asb("xTt", [128, DM], BF16)
            stats = asb("stats", [128, 4], F32)
            scr = asb("scr", [128, DM], F32)
            sint = asb("sint", [128, DM], F32)
            cost = asb("cost", [128, DM], F32)
            qf = asb("qf", [128, DM], F32)
            qrot = asb("qrot", [128, DM], F32)
            qhat = asb("qhat", [128, DM], BF16)
            khat = asb("khat", [128, 256], BF16)
            pbt = ast.enter_context(
                nc.psum_tensor(B.name("pbt"), [128, DM], BF16))
            pq = [ast.enter_context(
                nc.psum_tensor(B.name("pq"), [128, 512], F32))
                for _ in range(2)]
            psqg = ast.enter_context(
                nc.psum_tensor(B.name("psqg"), [64, 64], F32))
            pskg = ast.enter_context(
                nc.psum_tensor(B.name("pskg"), [128, 16], F32))
            psvg = ast.enter_context(
                nc.psum_tensor(B.name("psvg"), [4, 256], F32))

            with nc.Block() as b:
                @b.gpsimd
                def _(g):
                    for ic in range(8):
                        B.dg.inc(g.dma_start(
                            wqsb[:, ic * DM:(ic + 1) * DM],
                            W["l0a_Wq"][ic * 128:(ic + 1) * 128, :]), 16)
                        B.dg.inc(g.dma_start(
                            wksb[:, ic * 256:(ic + 1) * 256],
                            W["l0a_Wk"][ic * 128:(ic + 1) * 128, :]), 16)
                        B.dg.inc(g.dma_start(
                            wvsb[:, ic * 256:(ic + 1) * 256],
                            W["l0a_Wv"][ic * 128:(ic + 1) * 128, :]), 16)

            for rt in range(32):
                r0 = rt * 128
                with nc.Block() as b:
                    @b.sync
                    def _(sy, _r0=r0):
                        B.ds.inc(sy.dma_start(xf[:, :],
                                              x_in[_r0:_r0 + 128, :]), 16)
                        B.ds.inc(sy.dma_start(
                            sint[:, :], W["sin_rep"][_r0:_r0 + 128, :]), 16)
                        B.ds.inc(sy.dma_start(
                            cost[:, :], W["cos_rep"][_r0:_r0 + 128, :]), 16)

                    @b.vector
                    def _(v):
                        v.wait_ge(B.ds.sem, B.ds.v)

                    @b.scalar
                    def _(s):
                        s.wait_ge(B.ds.sem, B.ds.v)
                emit_ln(B, xf, xhat, stats, scr)
                with nc.Block() as b:
                    @b.tensor
                    def _(t):
                        t.wait_ge(B.ev.sem, B.ev.v)
                        for c in range(8):
                            i1 = t.transpose(pbt[:, c * 128:(c + 1) * 128],
                                             xhat[:, c * 128:(c + 1) * 128],
                                             B.identb[:, :])
                        B.et.inc(i1)

                    @b.scalar
                    def _(s):
                        s.wait_ge(B.et.sem, B.et.v)
                        B.es.inc(s.copy(xTt[:, :], pbt[:, :]))
                B.bar()
                with nc.Block() as b:
                    @b.tensor
                    def _(t, _rt=rt):
                        t.wait_ge(B.dg.sem, B.dg.v)
                        for half in range(2):
                            for ic in range(8):
                                i1 = t.matmul(
                                    pq[half][:, :],
                                    xTt[:, ic * 128:(ic + 1) * 128],
                                    wqsb[:, ic * DM + half * 512:
                                         ic * DM + half * 512 + 512],
                                    start=(ic == 0), stop=(ic == 7))
                            B.et.inc(i1)

                    @b.vector
                    def _(v):
                        for half in range(2):
                            v.wait_ge(B.et.sem, B.et.v - 1 + half)
                            B.ev.inc(v.tensor_add(
                                qf[:, half * 512:half * 512 + 512],
                                pq[half][:, :],
                                bqb[:, half * 512:half * 512 + 512]))
                with nc.Block() as b:
                    @b.vector
                    def _(v):
                        v.wait_ge(B.ev.sem, B.ev.v)
                        src_o = strided_ap(qf[:, :], 1, [[DM, 128], [2, 512]])
                        dst_e = strided_ap(qrot[:, :], 0, [[DM, 128], [2, 512]])
                        B.ev.inc(v.tensor_scalar(dst_e, src_o, -1.0, None,
                                                 ALU.mult))
                        src_e = strided_ap(qf[:, :], 0, [[DM, 128], [2, 512]])
                        dst_o = strided_ap(qrot[:, :], 1, [[DM, 128], [2, 512]])
                        B.ev.inc(v.tensor_copy(dst_o, src_e))
                        v.wait_ge(B.ev.sem, B.ev.v)
                        B.ev.inc(v.tensor_mul(qf[:, :], qf[:, :], cost[:, :]))
                        B.ev.inc(v.tensor_mul(qrot[:, :], qrot[:, :],
                                              sint[:, :]))
                        v.wait_ge(B.ev.sem, B.ev.v)
                        B.ev.inc(v.tensor_add(qhat[:, :], qf[:, :],
                                              qrot[:, :]))
                with nc.Block() as b:
                    @b.tensor
                    def _(t, _rt=rt):
                        t.wait_ge(B.ev.sem, B.ev.v)
                        for c in range(8):
                            i1 = t.transpose(pbt[:, c * 128:(c + 1) * 128],
                                             qhat[:, c * 128:(c + 1) * 128],
                                             B.identb[:, :])
                        B.et.inc(i1)
                        for h in range(NH):
                            i1 = t.matmul(psqg[0:64, h * 4:(h + 1) * 4],
                                          qhat[:, h * 64:(h + 1) * 64],
                                          ohb[:, _rt * 4:(_rt + 1) * 4],
                                          start=(_rt == 0), stop=(_rt == 31),
                                          skip_group_check=True)
                        B.et.inc(i1)

                    @b.scalar
                    def _(s, _rt=rt):
                        s.wait_ge(B.et.sem, B.et.v - 1)
                        dst = strided_ap(qTp[:, :], _rt * 128,
                                         [[8 * S, 128], [S, 8], [1, 128]])
                        B.es.inc(s.copy(dst, pbt[:, :]))
                with nc.Block() as b:
                    @b.tensor
                    def _(t):
                        for ic in range(8):
                            i1 = t.matmul(pq[0][:, 0:256],
                                          xTt[:, ic * 128:(ic + 1) * 128],
                                          wksb[:, ic * 256:(ic + 1) * 256],
                                          start=(ic == 0), stop=(ic == 7))
                        B.et.inc(i1)
                        for ic in range(8):
                            i1 = t.matmul(pq[1][:, 0:256],
                                          xTt[:, ic * 128:(ic + 1) * 128],
                                          wvsb[:, ic * 256:(ic + 1) * 256],
                                          start=(ic == 0), stop=(ic == 7))
                        B.et.inc(i1)

                    @b.vector
                    def _(v, _rt=rt):
                        v.wait_ge(B.et.sem, B.et.v - 1)
                        B.ev.inc(v.tensor_add(qf[:, 0:256], pq[0][:, 0:256],
                                              bkb[:, :]))
                        src_o = strided_ap(qf[:, :], 1, [[DM, 128], [2, 128]])
                        dst_e = strided_ap(qrot[:, :], 0, [[DM, 128], [2, 128]])
                        v.wait_ge(B.ev.sem, B.ev.v)
                        B.ev.inc(v.tensor_scalar(dst_e, src_o, -1.0, None,
                                                 ALU.mult))
                        src_e = strided_ap(qf[:, :], 0, [[DM, 128], [2, 128]])
                        dst_o = strided_ap(qrot[:, :], 1, [[DM, 128], [2, 128]])
                        B.ev.inc(v.tensor_copy(dst_o, src_e))
                        v.wait_ge(B.ev.sem, B.ev.v)
                        B.ev.inc(v.tensor_mul(qf[:, 0:256], qf[:, 0:256],
                                              cost[:, 0:256]))
                        B.ev.inc(v.tensor_mul(qrot[:, 0:256], qrot[:, 0:256],
                                              sint[:, 0:256]))
                        v.wait_ge(B.ev.sem, B.ev.v)
                        B.ev.inc(v.tensor_add(khat[:, :], qf[:, 0:256],
                                              qrot[:, 0:256]))
                        v.wait_ge(B.et.sem, B.et.v)
                        B.ev.inc(v.tensor_add(
                            vsb[:, _rt * 256:(_rt + 1) * 256],
                            pq[1][:, 0:256], bvb[:, :]))
                with nc.Block() as b:
                    @b.tensor
                    def _(t, _rt=rt):
                        t.wait_ge(B.ev.sem, B.ev.v)
                        t.wait_ge(B.es.sem, B.es.v)
                        for g_ in range(4):
                            for bp in (0, 64):
                                i1 = t.transpose(
                                    pbt[bp:bp + 64,
                                        g_ * 128:g_ * 128 + 128],
                                    khat[:, g_ * 64:(g_ + 1) * 64],
                                    B.identb[:, :])
                        B.et.inc(i1)
                        for g_ in range(4):
                            for bp in (0, 64):
                                i1 = t.matmul(
                                    pskg[bp:bp + 64, g_ * 4:(g_ + 1) * 4],
                                    khat[:, g_ * 64:(g_ + 1) * 64],
                                    ohb[:, _rt * 4:(_rt + 1) * 4],
                                    start=(_rt == 0), stop=(_rt == 31),
                                    skip_group_check=True)
                        B.et.inc(i1)
                        for g_ in range(4):
                            i1 = t.matmul(psvg[0:4, g_ * 64:(g_ + 1) * 64],
                                          ohb[:, _rt * 4:(_rt + 1) * 4],
                                          vsb[:, _rt * 256 + g_ * 64:
                                              _rt * 256 + g_ * 64 + 64],
                                          start=(_rt == 0), stop=(_rt == 31),
                                          skip_group_check=True)
                        B.et.inc(i1)

                    @b.scalar
                    def _(s, _rt=rt):
                        s.wait_ge(B.et.sem, B.et.v - 2)
                        dst = strided_ap(kTp[:, :], _rt * 128,
                                         [[4 * S, 128], [S, 4], [1, 128]])
                        B.es.inc(s.copy(dst, pbt[:, 0:512]))
                B.bar()
            with nc.Block() as b:
                @b.vector
                def _(v):
                    B.ev.inc(v.tensor_copy(qgT[0:64, :], psqg[0:64, :]))
                    B.ev.inc(v.tensor_copy(kgT[:, :], pskg[:, :]))
                    B.ev.inc(v.tensor_copy(vgg[0:4, :], psvg[0:4, :]))
            B.bar()

        with contextlib.ExitStack() as gst:
            gsb = lambda nm, sh, dt: gst.enter_context(
                nc.sbuf_tensor(B.name(nm), sh, dt))
            scg = gsb("scg", [4, S], F32)
            pgbf = gsb("pgbf", [4, S], BF16)
            pTg = gsb("pTg", [128, 128], BF16)
            rg = gsb("rg", [4, 4], F32)
            psg = gst.enter_context(
                nc.psum_tensor(B.name("psg"), [4, 512], F32))
            ptg = gst.enter_context(
                nc.psum_tensor(B.name("ptg"), [128, 128], BF16))
            psog = gst.enter_context(
                nc.psum_tensor(B.name("psog"), [4, 64], F32))
            for h in range(NH):
                g_ = h // 4
                with nc.Block() as b:
                    @b.tensor
                    def _(t, _h=h, _g=g_):
                        for cg in range(8):
                            B.et.inc(t.matmul(
                                psg[0:4, :],
                                qgT[0:64, _h * 4:(_h + 1) * 4],
                                kTp[0:64, _g * S + cg * 512:
                                    _g * S + cg * 512 + 512],
                                start=True, stop=True))
                            t.wait_ge(B.es.sem, B.es.v + cg + 1)

                    @b.scalar
                    def _(s):
                        for cg in range(8):
                            s.wait_ge(B.et.sem, B.et.v - 8 + cg + 1)
                            B.es.inc(s.activation(
                                scg[0:4, cg * 512:(cg + 1) * 512],
                                psg[0:4, :], AF.Copy,
                                scale=B.kc[0:4, KC_ATT:KC_ATT + 1]))
                with nc.Block() as b:
                    @b.vector
                    def _(v):
                        v.wait_ge(B.es.sem, B.es.v)
                        B.ev.inc(v.reduce_max(rg[0:4, 0:1], scg[0:4, :],
                                              axis=AX.X, negate=True))

                    @b.scalar
                    def _(s):
                        s.wait_ge(B.ev.sem, B.ev.v)
                        B.es.inc(s.activation(pgbf[0:4, :], scg[0:4, :],
                                              AF.Exp, bias=rg[0:4, 0:1],
                                              accum_out=rg[0:4, 1:2]))

                    @b.vector
                    def _(v):
                        v.wait_ge(B.es.sem, B.es.v)
                        B.ev.inc(v.reciprocal(rg[0:4, 2:3], rg[0:4, 1:2]))
                with nc.Block() as b:
                    @b.tensor
                    def _(t):
                        t.wait_ge(B.ev.sem, B.ev.v)
                        for jb in range(32):
                            i1 = t.transpose(ptg[:, jb * 4:(jb + 1) * 4],
                                             pgbf[0:4, jb * 128:
                                                  (jb + 1) * 128],
                                             B.identb[0:4, 0:4])
                        B.et.inc(i1)

                    @b.vector
                    def _(v):
                        v.wait_ge(B.et.sem, B.et.v)
                        B.ev.inc(v.tensor_copy(pTg[:, :], ptg[:, :]))
                with nc.Block() as b:
                    @b.tensor
                    def _(t, _g=g_):
                        t.wait_ge(B.ev.sem, B.ev.v)
                        for jb in range(32):
                            i1 = t.matmul(psog[0:4, :],
                                          pTg[:, jb * 4:(jb + 1) * 4],
                                          vsb[:, jb * 256 + _g * 64:
                                              jb * 256 + _g * 64 + 64],
                                          start=(jb == 0), stop=(jb == 31))
                        B.et.inc(i1)

                    @b.vector
                    def _(v, _h=h):
                        v.wait_ge(B.et.sem, B.et.v)
                        B.ev.inc(v.tensor_scalar(
                            ogsb[0:4, _h * 64:(_h + 1) * 64],
                            psog[0:4, :], rg[0:4, 2:3], None, ALU.mult))
                B.bar()
            with nc.Block() as b:
                @b.vector
                def _(v):
                    B.ev.inc(v.tensor_copy(ogbf[0:4, :], ogsb[0:4, :]))
            B.bar()

        with contextlib.ExitStack() as mst:
            msb = lambda nm, sh, dt: mst.enter_context(
                nc.sbuf_tensor(B.name(nm), sh, dt))
            wosb = msb("wosb", [128, 8 * DM], BF16)
            sc = msb("sc", [128, 648], F32)
            pbf = msb("pbf", [128, 648], BF16)
            pT = msb("pT", [128, 640], BF16)
            pTg4 = msb("pTg4", [4, 128], BF16)
            ohT = msb("ohT", [4, 128], BF16)
            gadd = msb("gadd", [128, 4], F32)
            rv = msb("rv", [128, 8], F32)
            osb = msb("osb", [128, DM], F32)
            obf = msb("obf", [128, DM], BF16)
            oT = msb("oT", [128, DM], BF16)
            ysb = msb("ysb", [128, DM], F32)
            xf2 = msb("xf2", [128, DM], F32)
            stats2 = msb("stats2", [128, 4], F32)
            scr2 = msb("scr2", [128, DM], F32)
            osc = msb("osc", [128, DM], F32)
            psA = mst.enter_context(
                nc.psum_tensor(B.name("psA"), [128, 512], F32))
            psB = mst.enter_context(
                nc.psum_tensor(B.name("psB"), [128, 260], F32))
            ptpm = mst.enter_context(
                nc.psum_tensor(B.name("ptpm"), [128, 640], BF16))
            ptg4 = mst.enter_context(
                nc.psum_tensor(B.name("ptg4"), [4, 128], BF16))
            pso = mst.enter_context(
                nc.psum_tensor(B.name("pso"), [128, 64], F32))
            pssc = mst.enter_context(
                nc.psum_tensor(B.name("pssc"), [128, 512], F32))
            pbt2 = mst.enter_context(
                nc.psum_tensor(B.name("pbt2"), [128, DM], BF16))
            psw = mst.enter_context(
                nc.psum_tensor(B.name("psw"), [128, 512], F32))

            with nc.Block() as b:
                @b.gpsimd
                def _(g):
                    for oc in range(8):
                        B.dg.inc(g.dma_start(
                            wosb[:, oc * DM:(oc + 1) * DM],
                            W["l0a_Wo"][oc * 128:(oc + 1) * 128, :]), 16)

            for rt in range(32):
                ws = max(0, rt - 2)
                we = min(31, rt + 2)
                nb = we - ws + 1
                Wd = nb * 128
                toff = (ws - (rt - 2)) * 128
                with nc.Block() as b:
                    @b.vector
                    def _(v, _rt=rt):
                        B.ev.inc(v.tensor_scalar(gadd[:, :], gpb[:, :],
                                                 B.iotac[:, 0:1],
                                                 float(128 * _rt),
                                                 ALU.subtract, ALU.subtract))
                        v.wait_ge(B.ev.sem, B.ev.v)
                        B.ev.inc(v.tensor_scalar(rv[:, 4:8], gadd[:, :],
                                                 -256.0, None, ALU.is_lt))
                        B.ev.inc(v.tensor_scalar(gadd[:, :], gadd[:, :],
                                                 256.0, None, ALU.is_gt))
                        v.wait_ge(B.ev.sem, B.ev.v)
                        B.ev.inc(v.tensor_max(gadd[:, :], gadd[:, :],
                                              rv[:, 4:8]))
                        v.wait_ge(B.ev.sem, B.ev.v)
                        B.ev.inc(v.tensor_scalar(gadd[:, :], gadd[:, :],
                                                 -1.0, -NEG, ALU.add,
                                                 ALU.mult))
                        v.wait_ge(B.ev.sem, B.ev.v)
                        B.ev.inc(v.tensor_add(gadd[:, :], gadd[:, :],
                                              dupaddb[:, :]))
                B.bar()
                for h in range(NH):
                    g_ = h // 4
                    qsl = qTp[64 * (h % 2):64 * (h % 2) + 64,
                              (h // 2) * S + rt * 128:
                              (h // 2) * S + rt * 128 + 128]
                    kbase = 64 * (h % 2)
                    with nc.Block() as b:
                        @b.tensor
                        def _(t, _qsl=qsl, _g=g_, _kb=kbase, _ws=ws,
                              _Wd=Wd):
                            n1 = min(_Wd, 512)
                            B.et.inc(t.matmul(
                                psA[:, 0:n1], _qsl,
                                kTp[_kb:_kb + 64,
                                    _g * S + _ws * 128:
                                    _g * S + _ws * 128 + n1],
                                start=True, stop=True))
                            if _Wd > 512:
                                B.et.inc(t.matmul(
                                    psB[:, 0:_Wd - 512], _qsl,
                                    kTp[_kb:_kb + 64,
                                        _g * S + _ws * 128 + 512:
                                        _g * S + _ws * 128 + _Wd],
                                    start=True, stop=True))
                            B.et.inc(t.matmul(
                                psB[:, 128:132], _qsl,
                                kgT[_kb:_kb + 64, _g * 4:(_g + 1) * 4],
                                start=True, stop=True))

                        @b.scalar
                        def _(s, _Wd=Wd):
                            net = 3 if _Wd > 512 else 2
                            s.wait_ge(B.et.sem, B.et.v - net + 1)
                            n1 = min(_Wd, 512)
                            B.es.inc(s.activation(
                                sc[:, 0:n1], psA[:, 0:n1], AF.Copy,
                                scale=B.kc[:, KC_ATT:KC_ATT + 1]))
                            s.wait_ge(B.et.sem, B.et.v)
                            if _Wd > 512:
                                B.es.inc(s.activation(
                                    sc[:, 512:_Wd], psB[:, 0:_Wd - 512],
                                    AF.Copy,
                                    scale=B.kc[:, KC_ATT:KC_ATT + 1]))
                            B.es.inc(s.activation(
                                sc[:, _Wd:_Wd + 4], psB[:, 128:132],
                                AF.Copy, scale=B.kc[:, KC_ATT:KC_ATT + 1]))

                        @b.vector
                        def _(v, _Wd=Wd, _toff=toff):
                            v.wait_ge(B.es.sem, B.es.v)
                            B.ev.inc(v.tensor_add(
                                sc[:, 0:_Wd], sc[:, 0:_Wd],
                                band_sb[:, _toff:_toff + _Wd]))
                            B.ev.inc(v.tensor_add(sc[:, _Wd:_Wd + 4],
                                                  sc[:, _Wd:_Wd + 4],
                                                  gadd[:, :]))
                            v.wait_ge(B.ev.sem, B.ev.v)
                            B.ev.inc(v.reduce_max(rv[:, 0:1],
                                                  sc[:, 0:_Wd + 4],
                                                  axis=AX.X, negate=True))

                        @b.scalar
                        def _(s, _Wd=Wd):
                            s.wait_ge(B.ev.sem, B.ev.v)
                            B.es.inc(s.activation(
                                pbf[:, 0:_Wd + 4], sc[:, 0:_Wd + 4],
                                AF.Exp, bias=rv[:, 0:1],
                                accum_out=rv[:, 1:2]))

                        @b.vector
                        def _(v):
                            v.wait_ge(B.es.sem, B.es.v)
                            B.ev.inc(v.reciprocal(rv[:, 2:3], rv[:, 1:2]))
                    with nc.Block() as b:
                        @b.tensor
                        def _(t, _Wd=Wd, _nb=nb):
                            t.wait_ge(B.es.sem, B.es.v)
                            for cb in range(_nb):
                                i1 = t.transpose(
                                    ptpm[:, cb * 128:(cb + 1) * 128],
                                    pbf[:, cb * 128:(cb + 1) * 128],
                                    B.identb[:, :])
                            B.et.inc(i1)
                            B.et.inc(t.transpose(ptg4[0:4, :],
                                                 pbf[:, _Wd:_Wd + 4],
                                                 B.identb[:, :]))

                        @b.vector
                        def _(v, _Wd=Wd):
                            v.wait_ge(B.et.sem, B.et.v - 1)
                            B.ev.inc(v.tensor_copy(pT[:, 0:_Wd],
                                                   ptpm[:, 0:_Wd]))
                            v.wait_ge(B.et.sem, B.et.v)
                            B.ev.inc(v.tensor_copy(pTg4[0:4, :],
                                                   ptg4[0:4, :]))
                    with nc.Block() as b:
                        @b.tensor
                        def _(t, _nb=nb, _ws=ws, _g=g_):
                            t.wait_ge(B.ev.sem, B.ev.v)
                            for cb in range(_nb):
                                t.matmul(pso[:, :],
                                         pT[:, cb * 128:(cb + 1) * 128],
                                         vsb[:, (_ws + cb) * 256 + _g * 64:
                                             (_ws + cb) * 256 + _g * 64 + 64],
                                         start=(cb == 0), stop=False)
                            B.et.inc(t.matmul(
                                pso[:, :], pTg4[0:4, :],
                                vgg[0:4, _g * 64:(_g + 1) * 64],
                                start=False, stop=True))

                        @b.vector
                        def _(v, _h=h):
                            v.wait_ge(B.et.sem, B.et.v)
                            B.ev.inc(v.tensor_scalar(
                                osb[:, _h * 64:(_h + 1) * 64], pso[:, :],
                                rv[:, 2:3], None, ALU.mult))
                    B.bar()
                with nc.Block() as b:
                    @b.tensor
                    def _(t, _rt=rt):
                        B.et.inc(t.transpose(ptg4[0:4, :],
                                             ohb[:, _rt * 4:(_rt + 1) * 4],
                                             B.identb[:, :]))

                    @b.vector
                    def _(v):
                        v.wait_ge(B.et.sem, B.et.v)
                        B.ev.inc(v.tensor_copy(ohT[0:4, :], ptg4[0:4, :]))
                with nc.Block() as b:
                    @b.tensor
                    def _(t):
                        t.wait_ge(B.ev.sem, B.ev.v)
                        for half in range(2):
                            B.et.inc(t.matmul(
                                pssc[:, :], ohT[0:4, :],
                                ogbf[0:4, half * 512:half * 512 + 512],
                                start=True, stop=True))
                            t.wait_ge(B.ev.sem, B.ev.v + half + 1)

                    @b.vector
                    def _(v, _rt=rt):
                        for half in range(2):
                            v.wait_ge(B.et.sem, B.et.v - 1 + half)
                            sl = slice(half * 512, half * 512 + 512)
                            B.ev.inc(v.scalar_tensor_tensor(
                                osb[:, sl], osb[:, sl],
                                rmc[:, _rt:_rt + 1], pssc[:, :],
                                ALU.mult, ALU.add))
                        v.wait_ge(B.ev.sem, B.ev.v)
                        B.ev.inc(v.tensor_copy(obf[:, :], osb[:, :]))
                with nc.Block() as b:
                    @b.tensor
                    def _(t):
                        t.wait_ge(B.ev.sem, B.ev.v)
                        for c in range(8):
                            i1 = t.transpose(pbt2[:, c * 128:(c + 1) * 128],
                                             obf[:, c * 128:(c + 1) * 128],
                                             B.identb[:, :])
                        B.et.inc(i1)

                    @b.scalar
                    def _(s):
                        s.wait_ge(B.et.sem, B.et.v)
                        B.es.inc(s.copy(oT[:, :], pbt2[:, :]))
                with nc.Block() as b:
                    @b.tensor
                    def _(t):
                        t.wait_ge(B.es.sem, B.es.v)
                        t.wait_ge(B.dg.sem, B.dg.v)
                        for half in range(2):
                            for oc in range(8):
                                i1 = t.matmul(
                                    psw[:, :],
                                    oT[:, oc * 128:(oc + 1) * 128],
                                    wosb[:, oc * DM + half * 512:
                                         oc * DM + half * 512 + 512],
                                    start=(oc == 0), stop=(oc == 7))
                            B.et.inc(i1)
                            t.wait_ge(B.ev.sem, B.ev.v + half + 1)

                    @b.vector
                    def _(v):
                        for half in range(2):
                            v.wait_ge(B.et.sem, B.et.v - 1 + half)
                            sl = slice(half * 512, half * 512 + 512)
                            B.ev.inc(v.tensor_add(ysb[:, sl], psw[:, :],
                                                  bob[:, sl]))
                B.bar()
                emit_epilogue_tile(B, ysb, x_in, x_out, rt * 128,
                                   xf2, osc, stats2, scr2)



def emit_l1_attn(B, x_in, x_out, W):
    nc = B.nc
    NH, HD = 8, 64
    CL = 256
    NC_ = S // CL

    with contextlib.ExitStack() as st:
        BB = Bld.__new__(Bld)
        BB.__dict__.update(B.__dict__)
        BB.st = st
        sb = BB.sbuf

        wqsb = sb("wqsb", [128, 8 * 512], BF16)
        wksb = sb("wksb", [128, 8 * 512], BF16)
        wvsb = sb("wvsb", [128, 8 * 512], BF16)
        wosb = sb("wosb", [128, 4 * DM], BF16)
        gates = sb("gates", [128, 512], F32)
        bet = sb("bet", [1, 512], F32)
        memz = sb("memz", [64, 8 * 65], F32)
        membf = sb("membf", [64, 8 * 65], BF16)

        with nc.Block() as b:
            @b.gpsimd
            def _(g):
                for ic in range(8):
                    B.dg.inc(g.dma_start(
                        wqsb[:, ic * 512:(ic + 1) * 512],
                        W["l1a_Wq"][ic * 128:(ic + 1) * 128, :]), 16)
                    B.dg.inc(g.dma_start(
                        wksb[:, ic * 512:(ic + 1) * 512],
                        W["l1a_Wk"][ic * 128:(ic + 1) * 128, :]), 16)
                    B.dg.inc(g.dma_start(
                        wvsb[:, ic * 512:(ic + 1) * 512],
                        W["l1a_Wv"][ic * 128:(ic + 1) * 128, :]), 16)
                for oc in range(4):
                    B.dg.inc(g.dma_start(
                        wosb[:, oc * DM:(oc + 1) * DM],
                        W["l1a_Wo"][oc * 128:(oc + 1) * 128, :]), 16)
                g.memset(memz[0:64, :], 0.0)
                for h in range(NH):
                    B.eg.inc(g.memset(memz[0:64, h * 65 + 64:h * 65 + 65],
                                      1.0 / 64.0))

            @b.sync
            def _(sy):
                B.ds.inc(sy.dma_start(
                    bet[0:1, :],
                    W["l1a_betas"].rearrange("(a b) -> a b", a=1)), 16)

            @b.scalar
            def _(s):
                s.wait_ge(B.ds.sem, B.ds.v)
                B.es.inc(s.activation(bet[0:1, :], bet[0:1, :], AF.Sigmoid))
        with contextlib.ExitStack() as pst:
            psx = pst.enter_context(nc.psum_tensor(B.name("psx"),
                                                   [128, 512], F32))
            with nc.Block() as b:
                @b.tensor
                def _(t):
                    t.wait_ge(B.es.sem, B.es.v)
                    B.et.inc(t.matmul(psx[:, :], B.onesf[0:1, :], bet[0:1, :],
                                      start=True, stop=True))

                @b.vector
                def _(v):
                    v.wait_ge(B.et.sem, B.et.v)
                    B.ev.inc(v.tensor_copy(gates[:, :], psx[:, :]))
        B.bar()

        xTc = sb("xTc", [128, 8 * CL], BF16)
        qsb = sb("qsb", [128, 2 * 512], F32)
        ksb = sb("ksb", [128, 2 * 512], F32)
        qbf = sb("qbf", [128, 2 * 512], BF16)
        kbf = sb("kbf", [128, 2 * 512], BF16)
        vbf = sb("vbf", [128, 2 * 512], BF16)
        sqbf = sb("sqbf", [128, 2 * 512], BF16)
        skbf = sb("skbf", [128, 2 * 512], BF16)
        qTsb = sb("qTsb", [64, 8 * CL], BF16)
        kTsb = sb("kTsb", [64, 8 * CL], BF16)
        sqTsb = sb("sqTsb", [64, 8 * CL], BF16)
        pTsb = sb("pTsb", [128, 256], BF16)
        amsb = sb("amsb", [128, 2 * 512], F32)
        adsb = sb("adsb", [128, 2 * 512], F32)
        obf = sb("obf", [128, 2 * 512], BF16)
        oTsb = sb("oTsb", [128, 512], BF16)
        scrA = sb("scrA", [128, 512], F32)
        scrB = sb("scrB", [128, 512], F32)
        rc = sb("rc", [128, 8], F32)
        xf = sb("xf", [128, DM], F32)
        xhat = sb("xhat", [128, DM], BF16)
        stats = sb("stats", [128, 4], F32)
        scr = sb("scr", [128, DM], F32)
        ysb = sb("ysb", [128, DM], F32)
        osb = sb("osb", [128, DM], F32)

        pbt = BB.psum("pbt", [128, DM], BF16)
        ps_am = BB.psum("ps_am", [128, 65], F32)
        ps_s = BB.psum("ps_s", [128, 256], F32)
        ptp = BB.psum("ptp", [128, 256], BF16)
        ps_pv = BB.psum("ps_pv", [128, 64], F32)
        ps_up = BB.psum("ps_up", [64, 65], F32)
        ps_o = BB.psum("ps_o", [128, 512], F32)
        ps_ot = BB.psum("ps_ot", [128, 512], BF16)

        for n in range(NC_):
            c0 = n * CL
            emit_load_ln_t(B, x_in, c0, 2, xTc, CL, xf, xhat, stats, scr, pbt)

            for ti in range(2):
                ev0 = B.ev.v
                with nc.Block() as b:
                    @b.tensor
                    def _(t, _ti=ti):
                        t.wait_ge(B.dg.sem, B.dg.v)
                        evtgt = [ev0 + 2, ev0 + 4, ev0 + 5]
                        for _j, wsb_t in enumerate((wqsb, wksb, wvsb)):
                            for ic in range(8):
                                i1 = t.matmul(
                                    ps_o[:, :],
                                    xTc[:, ic * CL + _ti * 128:
                                        ic * CL + _ti * 128 + 128],
                                    wsb_t[:, ic * 512:(ic + 1) * 512],
                                    start=(ic == 0), stop=(ic == 7))
                            B.et.inc(i1)
                            t.wait_ge(B.ev.sem, evtgt[_j])

                    @b.vector
                    def _(v, _ti=ti):
                        for _j, (dst_f, dst_b) in enumerate(
                                ((qsb, qbf), (ksb, kbf), (None, vbf))):
                            v.wait_ge(B.et.sem, B.et.v - 2 + _j)
                            sl = slice(_ti * 512, _ti * 512 + 512)
                            if dst_f is not None:
                                B.ev.inc(v.tensor_copy(dst_f[:, sl],
                                                       ps_o[:, :]))
                                v.wait_ge(B.ev.sem, B.ev.v)
                            B.ev.inc(v.tensor_copy(dst_b[:, sl], ps_o[:, :]))
                B.bar()

            for (src_f, dst_b) in ((qsb, sqbf), (ksb, skbf)):
                for ti in range(2):
                    with nc.Block() as b:
                        @b.vector
                        def _(v, _s=src_f, _ti=ti):
                            sl = slice(_ti * 512, _ti * 512 + 512)
                            B.ev.inc(v.tensor_scalar(scrA[:, :], _s[:, sl],
                                                     0.0, None, ALU.min))

                        @b.scalar
                        def _(s):
                            s.wait_ge(B.ev.sem, B.ev.v)
                            B.es.inc(s.activation(scrB[:, :], scrA[:, :],
                                                  AF.Exp))

                        @b.vector
                        def _(v, _s=src_f, _d=dst_b, _ti=ti):
                            sl = slice(_ti * 512, _ti * 512 + 512)
                            v.wait_ge(B.es.sem, B.es.v)
                            B.ev.inc(v.tensor_scalar(scrA[:, :], _s[:, sl],
                                                     0.0, None, ALU.max))
                            v.wait_ge(B.ev.sem, B.ev.v)
                            B.ev.inc(v.tensor_add(_d[:, sl], scrB[:, :],
                                                  scrA[:, :]))
                    B.bar()

            for (srcb, dstT) in ((qbf, qTsb), (kbf, kTsb), (sqbf, sqTsb)):
                for ti in range(2):
                    with nc.Block() as b:
                        @b.tensor
                        def _(t, _s=srcb, _ti=ti):
                            for h in range(NH):
                                i1 = t.transpose(
                                    ptp[0:64, (h % 2) * 128:
                                        (h % 2) * 128 + 128] if False else
                                    ps_ot[0:64, h * 64:h * 64 + 128]
                                    if False else
                                    ptp[0:64, 0:128],
                                    _s[:, _ti * 512 + h * 64:
                                       _ti * 512 + h * 64 + 64],
                                    B.identb[:, :])
                                B.et.inc(i1)
                                t.wait_ge(B.ev.sem, B.ev.v + h + 1)

                        @b.vector
                        def _(v, _d=dstT, _ti=ti):
                            for h in range(NH):
                                v.wait_ge(B.et.sem, B.et.v - NH + h + 1)
                                dst = strided_ap(
                                    _d[0:64, :], h * CL + _ti * 128,
                                    [[8 * CL, 64], [1, 128]])
                                B.ev.inc(v.tensor_copy(dst, ptp[0:64, 0:128]))
                    B.bar()

            with nc.Block() as b:
                @b.vector
                def _(v):
                    B.ev.inc(v.tensor_copy(membf[0:64, :], memz[0:64, :]))
            B.bar()
            if _DBG and n == 1:
                with nc.Block() as b:
                    @b.sync
                    def _(sy):
                        B.ds.inc(sy.dma_start(
                            _DBG_T["dbg_mem"][:, :], memz[0:64, :]), 16)
                        sy.wait_ge(B.ds.sem, B.ds.v)

            for h in range(NH):
                for ti in range(2):
                    with nc.Block() as b:
                        @b.tensor
                        def _(t, _h=h, _ti=ti):
                            B.et.inc(t.matmul(
                                ps_am[:, :],
                                sqTsb[0:64, _h * CL + _ti * 128:
                                      _h * CL + _ti * 128 + 128],
                                membf[0:64, _h * 65:(_h + 1) * 65],
                                start=True, stop=True))
                            B.et.inc(t.matmul(
                                ps_s[:, :],
                                qTsb[0:64, _h * CL + _ti * 128:
                                     _h * CL + _ti * 128 + 128],
                                kTsb[0:64, _h * CL:(_h + 1) * CL],
                                start=True, stop=True))

                        @b.vector
                        def _(v, _h=h, _ti=ti):
                            v.wait_ge(B.et.sem, B.et.v - 1)
                            B.ev.inc(v.reciprocal(rc[:, 0:1],
                                                  ps_am[:, 64:65]))
                            v.wait_ge(B.ev.sem, B.ev.v)
                            B.ev.inc(v.tensor_scalar(
                                amsb[:, _ti * 512 + _h * 64:
                                     _ti * 512 + _h * 64 + 64],
                                ps_am[:, 0:64], rc[:, 0:1], None, ALU.mult))
                            v.wait_ge(B.et.sem, B.et.v)
                            B.ev.inc(v.reduce_max(rc[:, 1:2], ps_s[:, :],
                                                  axis=AX.X, negate=True))
                            v.wait_ge(B.ev.sem, B.ev.v)
                            B.ev.inc(v.tensor_scalar(rc[:, 2:3], rc[:, 1:2],
                                                     0.125, None, ALU.mult))

                        @b.scalar
                        def _(s):
                            s.wait_ge(B.ev.sem, B.ev.v)
                            B.es.inc(s.activation(
                                scrA[:, 0:256],
                                ps_s[:, :], AF.Exp, bias=rc[:, 2:3],
                                scale=B.kc[:, KC_ATT:KC_ATT + 1],
                                accum_out=rc[:, 3:4]))

                        @b.vector
                        def _(v):
                            v.wait_ge(B.es.sem, B.es.v)
                            B.ev.inc(v.tensor_copy(pTsb[:, 0:256],
                                                   scrA[:, 0:256]))
                            B.ev.inc(v.reciprocal(rc[:, 4:5], rc[:, 3:4]))
                    with nc.Block() as b:
                        @b.tensor
                        def _(t, _h=h, _ti=ti):
                            t.wait_ge(B.ev.sem, B.ev.v)
                            for jb in range(2):
                                i1 = t.transpose(
                                    ptp[:, jb * 128:(jb + 1) * 128],
                                    pTsb[:, jb * 128:(jb + 1) * 128],
                                    B.identb[:, :])
                            B.et.inc(i1)

                        @b.vector
                        def _(v):
                            v.wait_ge(B.et.sem, B.et.v)
                            B.ev.inc(v.tensor_copy(oTsb[:, 0:256],
                                                   ptp[:, :]))
                    with nc.Block() as b:
                        @b.tensor
                        def _(t, _h=h, _ti=ti):
                            t.wait_ge(B.ev.sem, B.ev.v)
                            for jb in range(2):
                                i1 = t.matmul(
                                    ps_pv[:, :],
                                    oTsb[:, jb * 128:(jb + 1) * 128],
                                    vbf[:, jb * 512 + _h * 64:
                                        jb * 512 + _h * 64 + 64],
                                    start=(jb == 0), stop=(jb == 1))
                            B.et.inc(i1)

                        @b.vector
                        def _(v, _h=h, _ti=ti):
                            v.wait_ge(B.et.sem, B.et.v)
                            B.ev.inc(v.tensor_scalar(
                                adsb[:, _ti * 512 + _h * 64:
                                     _ti * 512 + _h * 64 + 64],
                                ps_pv[:, :], rc[:, 4:5], None, ALU.mult))
                    B.bar()
                with nc.Block() as b:
                    @b.tensor
                    def _(t, _h=h):
                        for ti in range(2):
                            i1 = t.matmul(ps_up[0:64, 0:64],
                                          skbf[:, ti * 512 + _h * 64:
                                               ti * 512 + _h * 64 + 64],
                                          vbf[:, ti * 512 + _h * 64:
                                              ti * 512 + _h * 64 + 64],
                                          start=(ti == 0), stop=(ti == 1))
                        B.et.inc(i1)
                        for ti in range(2):
                            i1 = t.matmul(ps_up[0:64, 64:65],
                                          skbf[:, ti * 512 + _h * 64:
                                               ti * 512 + _h * 64 + 64],
                                          B.onesb[:, 0:1],
                                          start=(ti == 0), stop=(ti == 1))
                        B.et.inc(i1)

                    @b.vector
                    def _(v, _h=h):
                        v.wait_ge(B.et.sem, B.et.v)
                        B.ev.inc(v.tensor_add(memz[0:64, _h * 65:(_h + 1) * 65],
                                              memz[0:64, _h * 65:(_h + 1) * 65],
                                              ps_up[0:64, :]))
                B.bar()

            if _DBG and n == 1:
                with nc.Block() as b:
                    @b.sync
                    def _(sy):
                        B.ds.inc(sy.dma_start(_DBG_T["dbg_am"][:, :],
                                              amsb[:, :]), 16)
                        B.ds.inc(sy.dma_start(_DBG_T["dbg_ad"][:, :],
                                              adsb[:, :]), 16)
                        sy.wait_ge(B.ds.sem, B.ds.v)

            for ti in range(2):
                with nc.Block() as b:
                    @b.vector
                    def _(v, _ti=ti):
                        sl = slice(_ti * 512, _ti * 512 + 512)
                        B.ev.inc(v.tensor_sub(scrA[:, :], amsb[:, sl],
                                              adsb[:, sl]))
                        v.wait_ge(B.ev.sem, B.ev.v)
                        B.ev.inc(v.tensor_mul(scrA[:, :], scrA[:, :],
                                              gates[:, :]))
                        v.wait_ge(B.ev.sem, B.ev.v)
                        B.ev.inc(v.tensor_add(obf[:, sl], scrA[:, :],
                                              adsb[:, sl]))
                B.bar()
                with nc.Block() as b:
                    @b.tensor
                    def _(t, _ti=ti):
                        for oc in range(4):
                            i1 = t.transpose(
                                ps_ot[:, oc * 128:(oc + 1) * 128],
                                obf[:, _ti * 512 + oc * 128:
                                    _ti * 512 + oc * 128 + 128],
                                B.identb[:, :])
                        B.et.inc(i1)

                    @b.vector
                    def _(v):
                        v.wait_ge(B.et.sem, B.et.v)
                        B.ev.inc(v.tensor_copy(oTsb[:, :], ps_ot[:, :]))
                with nc.Block() as b:
                    @b.tensor
                    def _(t, _ti=ti):
                        t.wait_ge(B.ev.sem, B.ev.v)
                        for ds_ in range(2):
                            for oc in range(4):
                                i1 = t.matmul(
                                    ps_o[:, :] if ds_ == 0 else ps_s[:, :]
                                    if False else ps_o[:, :],
                                    oTsb[:, oc * 128:(oc + 1) * 128],
                                    wosb[:, oc * DM + ds_ * 512:
                                         oc * DM + ds_ * 512 + 512],
                                    start=(oc == 0), stop=(oc == 3))
                            B.et.inc(i1)
                            t.wait_ge(B.ev.sem, B.ev.v + ds_ + 1)

                    @b.vector
                    def _(v):
                        for ds_ in range(2):
                            v.wait_ge(B.et.sem, B.et.v - 1 + ds_)
                            B.ev.inc(v.tensor_copy(
                                ysb[:, ds_ * 512:(ds_ + 1) * 512],
                                ps_o[:, :]))
                B.bar()
                emit_epilogue_tile(B, ysb, x_in, x_out, c0 + ti * 128,
                                   xf, osb, stats, scr)



def _host_tables():
    out = {}
    p = np.arange(128)[:, None]
    n = np.arange(640)[None, :]
    band = np.where((n - p >= 0) & (n - p <= 512), 0.0, NEG).astype(np.float32)
    out["band_tbl"] = band
    j = np.arange(32)
    invf = (10000.0 ** (-(2.0 * j) / 64.0)).astype(np.float32)
    t = np.arange(S, dtype=np.float32)
    ang = t[:, None] * invf[None, :]
    emb = np.concatenate([ang, ang], axis=1)
    sin1 = np.sin(emb).astype(np.float32)
    cos1 = np.cos(emb).astype(np.float32)
    out["sin_rep"] = np.ascontiguousarray(np.tile(sin1, (1, 16)))
    out["cos_rep"] = np.ascontiguousarray(np.tile(cos1, (1, 16)))
    return out


def _prep_host(inputs):
    g = {}
    f32 = np.float32
    asc = np.ascontiguousarray

    def scol(v):
        v = np.asarray(v)
        return asc(v.reshape(-1, 128).T)

    g["l1f_w1T"] = asc(np.asarray(inputs["L1_ffn_w1_w"]).T)
    g["l1f_w2T"] = asc(np.asarray(inputs["L1_ffn_w2_w"]).T)
    g["l1f_h1T"] = asc(np.asarray(inputs["L1_ffn_h1_w"]).T)
    g["l1f_h2T"] = asc(np.asarray(inputs["L1_ffn_h2_w"]).T)
    g["l1f_w3T"] = asc(np.asarray(inputs["L1_ffn_w3_w"]).T)
    g["l1f_s1"] = scol(inputs["L1_ffn_w1_s"])
    g["l1f_s2"] = scol(inputs["L1_ffn_w2_s"])
    g["l1f_sh1"] = scol(inputs["L1_ffn_h1_s"])
    g["l1f_sh2"] = scol(inputs["L1_ffn_h2_s"])
    g["l1f_s3"] = asc(np.asarray(inputs["L1_ffn_w3_s"], dtype=f32))
    g["l0f_w1"] = asc(np.asarray(inputs["L0_ffn_w1"]))
    g["l0f_w2"] = asc(np.asarray(inputs["L0_ffn_w2"]))
    g["l0f_gwT"] = asc(np.asarray(inputs["L0_ffn_gate_w"]).T)
    g["l0f_gate_b"] = asc(np.asarray(inputs["L0_ffn_gate_b"]))
    g["l0f_rms_w"] = asc(np.asarray(inputs["L0_ffn_rms_w"]))
    g["l0f_b1c"] = asc(np.asarray(inputs["L0_ffn_b1"])
                       .reshape(8, 16, 128).transpose(0, 2, 1))
    g["l0f_b2"] = asc(np.asarray(inputs["L0_ffn_b2"]))
    for k_ in ("Wq", "bq", "Wk", "bk", "Wv", "bv", "Wo", "bo"):
        g["l0a_" + k_] = asc(np.asarray(inputs["L0_attn_" + k_]))
    g["l1a_Wq"] = asc(np.asarray(inputs["L1_attn_Wq"]))
    g["l1a_Wk"] = asc(np.asarray(inputs["L1_attn_Wk"]))
    g["l1a_Wv"] = asc(np.asarray(inputs["L1_attn_Wv"]))
    g["l1a_Wo"] = asc(np.asarray(inputs["L1_attn_Wo"]))
    g["l1a_betas"] = asc(np.asarray(inputs["L1_attn_betas"]).reshape(512))
    for k in list(g):
        g[k] = np.asarray(g[k], dtype=f32)
    g["gp"] = np.ascontiguousarray(
        np.asarray(inputs["L0_attn_global_positions"], dtype=np.int32))
    g.update(_host_tables())
    return g


BLOCK_INPUTS = {
    "l1f": ["l1f_w1T", "l1f_w2T", "l1f_h1T", "l1f_h2T", "l1f_w3T",
            "l1f_s1", "l1f_s2", "l1f_sh1", "l1f_sh2", "l1f_s3"],
    "l0f": ["l0f_w1", "l0f_w2", "l0f_gwT", "l0f_gate_b", "l0f_rms_w",
            "l0f_b1c", "l0f_b2"],
    "l1a": ["l1a_Wq", "l1a_Wk", "l1a_Wv", "l1a_Wo", "l1a_betas"],
    "l0a": ["l0a_Wq", "l0a_bq", "l0a_Wk", "l0a_bk", "l0a_Wv", "l0a_bv",
            "l0a_Wo", "l0a_bo", "gp", "band_tbl", "sin_rep", "cos_rep"],
}

EMITS = {"l1f": emit_l1_ffn, "l0f": emit_l0_ffn, "l1a": emit_l1_attn,
         "l0a": emit_l0_attn}

SHAPES = {
    "l1f_w1T": [DM, 4096], "l1f_w2T": [DM, 4096],
    "l1f_h1T": [4096, 4096], "l1f_h2T": [4096, 4096],
    "l1f_w3T": [4096, DM],
    "l1f_s1": [128, 32], "l1f_s2": [128, 32],
    "l1f_sh1": [128, 32], "l1f_sh2": [128, 32],
    "l1f_s3": [DM],
    "l0f_w1": [8, DM, 2048], "l0f_w2": [8, 2048, DM],
    "l0f_gwT": [DM, 8], "l0f_gate_b": [8], "l0f_rms_w": [8],
    "l0f_b1c": [8, 128, 16], "l0f_b2": [8, DM],
    "l1a_Wq": [DM, 512], "l1a_Wk": [DM, 512], "l1a_Wv": [DM, 512],
    "l1a_Wo": [512, DM], "l1a_betas": [512],
    "l0a_Wq": [DM, DM], "l0a_bq": [DM], "l0a_Wk": [DM, 256],
    "l0a_bk": [256], "l0a_Wv": [DM, 256], "l0a_bv": [256],
    "l0a_Wo": [DM, DM], "l0a_bo": [DM], "gp": [4],
    "band_tbl": [128, 640], "sin_rep": [S, DM], "cos_rep": [S, DM],
}


_DBG = False
_DBG_T = {}


def _build(block_names):
    nc = bass.Bass("TRN2", target_bir_lowering=False)
    xin = nc.dram_tensor("xin", [S, DM], F32, kind="ExternalInput")
    xout = nc.dram_tensor("xout", [S, DM], F32, kind="ExternalOutput")
    if _DBG:
        _DBG_T["dbg_mem"] = nc.dram_tensor("dbg_mem", [64, 520], F32,
                                           kind="ExternalOutput")
        _DBG_T["dbg_am"] = nc.dram_tensor("dbg_am", [128, 1024], F32,
                                          kind="ExternalOutput")
        _DBG_T["dbg_ad"] = nc.dram_tensor("dbg_ad", [128, 1024], F32,
                                          kind="ExternalOutput")
    needed = []
    for bn in block_names:
        needed += BLOCK_INPUTS[bn]
    W = {}
    for name in needed:
        dt = I32 if name == "gp" else F32
        W[name] = nc.dram_tensor(name, SHAPES[name], dt,
                                 kind="ExternalInput")[tuple(
                                     slice(None) for _ in SHAPES[name])]
    with contextlib.ExitStack() as st:
        B = Bld(nc, st)
        emit_consts(B)
        cur_in = xin[:, :]
        for i, bn in enumerate(block_names):
            last = (i == len(block_names) - 1)
            if last:
                cur_out = xout[:, :]
            else:
                h = B.dram(f"xh{i}", [S, DM], F32)
                cur_out = h[:, :]
            EMITS[bn](B, cur_in, cur_out, W)
            cur_in = cur_out
            B.bar()
            B.dma_fence()
        with nc.Block() as b:
            @b.sync
            def _(sy):
                sy.wait_ge(B.ds.sem, B.ds.v)
                sy.wait_ge(B.dg.sem, B.dg.v)
    return nc, needed


_PROG_CACHE = {}


def _run_blocks(block_names, x_full, inputs):
    key = tuple(block_names)
    if key not in _PROG_CACHE:
        _PROG_CACHE[key] = _build(block_names)
    nc, needed = _PROG_CACHE[key]
    shared = _prep_host(inputs)
    in_maps = []
    for c in range(8):
        m = {"xin": np.ascontiguousarray(x_full[c])}
        for name in needed:
            m[name] = shared[name]
        in_maps.append(m)
    res = run_bass_kernel_spmd(nc, in_maps, core_ids=list(range(8)))
    if _DBG:
        _DBG["last"] = res.results
    return np.stack([np.asarray(r["xout"]) for r in res.results])


GROUPS = [["l0a", "l0f", "l1a", "l1f"]]


def kernel(**inputs):
    x = np.asarray(inputs["L0_attn_x"], dtype=np.float32)
    for grp in GROUPS:
        x = _run_blocks(grp, x, inputs)
    return x


# version 04
# speedup vs baseline: 1.4794x; 1.4573x over previous
import contextlib
import math

import numpy as np
import concourse.bass as bass
import concourse.mybir as mybir
from concourse.bass_utils import run_bass_kernel_spmd

F32 = mybir.dt.float32
BF16 = mybir.dt.bfloat16
I32 = mybir.dt.int32
AF = mybir.ActivationFunctionType
ALU = mybir.AluOpType
AX = mybir.AxisListType

S = 4096
DM = 1024
NEG = -1.0e30
LN_EPS = 1e-5
RMS_EPS = float(np.finfo(np.float32).eps)

ENGS = ("sync", "tensor", "vector", "scalar", "gpsimd")
_L1F_NEWY = True
_L1F_NEWH = True


class Ctr:

    def __init__(self, sem):
        self.sem = sem
        self.v = 0

    def inc(self, inst, n=1):
        inst.then_inc(self.sem, n)
        self.v += n
        return self.v


_NAME_CTR = [0]


class Bld:

    def __init__(self, nc, st):
        self.nc = nc
        self.st = st
        self.bars = 0
        self.ds = self.mksem("ds")
        self.dg = self.mksem("dg")
        self.dc = self.mksem("dc")
        self.et = self.mksem("et")
        self.ev = self.mksem("ev")
        self.es = self.mksem("es")
        self.eg = self.mksem("eg")
        self.w0 = self.mksem("w0")
        self.w1 = self.mksem("w1")
        self.w2 = self.mksem("w2")
        self.barsem = self.mksem("bar")
        self.identb = None
        self.identf = None
        self.iotac = None
        self.onesf = None
        self.onesb = None
        self.kc = None

    def name(self, base):
        _NAME_CTR[0] += 1
        return f"{base}{_NAME_CTR[0]}"

    def mksem(self, base):
        return Ctr(self.st.enter_context(self.nc.semaphore(self.name(base))))

    def sbuf(self, base, shape, dtype):
        return self.st.enter_context(
            self.nc.sbuf_tensor(self.name(base), shape, dtype))

    def psum(self, base, shape, dtype):
        return self.st.enter_context(
            self.nc.psum_tensor(self.name(base), shape, dtype))

    def dram(self, base, shape, dtype):
        return self.nc.dram_tensor(self.name(base), shape, dtype, kind="Internal")

    def bar(self):
        return

    def dma_fence(self):
        with self.nc.Block() as b:
            @b.sync
            def _(sy):
                sy.wait_ge(self.ds.sem, self.ds.v)
                sy.wait_ge(self.dg.sem, self.dg.v)
                sy.wait_ge(self.dc.sem, self.dc.v)


KC_LNEPS = 0
KC_MHALF = 1
KC_RMSEPS = 2
KC_TWO = 3
KC_ATT = 4
KC_ONE = 5


def emit_consts(B):
    nc = B.nc
    B.identb = B.sbuf("identb", [128, 128], BF16)
    B.identf = B.sbuf("identf", [128, 128], F32)
    B.iotac = B.sbuf("iotac", [128, 1], F32)
    B.onesf = B.sbuf("onesf", [1, 128], F32)
    B.onescf = B.sbuf("onescf", [128, 1], F32)
    B.onesb = B.sbuf("onesb", [128, 1], BF16)
    B.kc = B.sbuf("kc", [128, 8], F32)
    with nc.Block() as b:
        @b.gpsimd
        def _(g):
            g.memset(B.identb[:, :], 0.0)
            g.affine_select(out=B.identb[:, :], in_=B.identb[:, :],
                            compare_op=ALU.not_equal, fill=1.0, base=0,
                            pattern=[[-1, 128]], channel_multiplier=1)
            g.memset(B.identf[:, :], 0.0)
            g.affine_select(out=B.identf[:, :], in_=B.identf[:, :],
                            compare_op=ALU.not_equal, fill=1.0, base=0,
                            pattern=[[-1, 128]], channel_multiplier=1)
            g.iota(B.iotac[:, :], pattern=[[0, 1]], base=0, channel_multiplier=1,
                   allow_small_or_imprecise_dtypes=True)
            g.memset(B.onesf[0:1, :], 1.0)
            g.memset(B.onescf[:, :], 1.0)
            g.memset(B.onesb[:, :], 1.0)
            g.memset(B.kc[:, KC_LNEPS:KC_LNEPS + 1], LN_EPS)
            g.memset(B.kc[:, KC_MHALF:KC_MHALF + 1], -0.5)
            g.memset(B.kc[:, KC_RMSEPS:KC_RMSEPS + 1], RMS_EPS)
            g.memset(B.kc[:, KC_TWO:KC_TWO + 1], 2.0)
            g.memset(B.kc[:, KC_ATT:KC_ATT + 1], 0.125)
            g.memset(B.kc[:, KC_ONE:KC_ONE + 1], 1.0)
    B.bar()


def _flat(ap):
    nd = len(ap.shape)
    if nd == 1:
        return ap
    if nd == 2:
        return ap.rearrange("a b -> (a b)")
    if nd == 3:
        return ap.rearrange("a b c -> (a b c)")
    raise ValueError(nd)


def emit_cast_dram(B, dst, src):
    nc = B.nc
    n = 1
    for d in src.shape:
        n *= d
    sf = _flat(src)
    df = _flat(dst)
    with nc.Block() as b:
        @b.gpsimd
        def _(g):
            off = 0
            while off < n:
                m = min(2 * 1024 * 1024, n - off)
                B.dg.inc(g.dma_start(
                    df[off:off + m].rearrange("(a b) -> a b", a=1),
                    sf[off:off + m].rearrange("(a b) -> a b", a=1)), 16)
                off += m


def strided_ap(base_ap, extra_off, dims):
    return bass.AP(base_ap.tensor, base_ap.offset + extra_off, dims)


def emit_ln(B, src, dst, stats, scr, dve="vector"):
    nc = B.nc
    if not isinstance(src, bass.AP):
        src = src[:, :]
    if not isinstance(dst, bass.AP):
        dst = dst[:, :]
    if not isinstance(scr, bass.AP):
        scr = scr[:, :]
    D = src.shape[-1] if hasattr(src, "shape") else DM
    rinv = 1.0 / D
    with nc.Block() as b:
        @b.vector
        def _(v):
            B.ev.inc(v.reduce_sum(stats[:, 0:1], src, axis=AX.X, negate=True))

        @b.scalar
        def _(s):
            B.es.inc(s.activation(scr, src, AF.Square,
                                  accum_out=stats[:, 1:2]))
    with nc.Block() as b:
        @b.vector
        def _(v):
            v.wait_ge(B.es.sem, B.es.v)
            v.wait_ge(B.ev.sem, B.ev.v)
            B.ev.inc(v.tensor_scalar(stats[:, 0:1], stats[:, 0:1], rinv, None,
                                     ALU.mult))
            v.wait_ge(B.ev.sem, B.ev.v)
            B.ev.inc(v.tensor_mul(stats[:, 2:3], stats[:, 0:1], stats[:, 0:1]))
            B.ev.inc(v.tensor_scalar(stats[:, 1:2], stats[:, 1:2], rinv, None,
                                     ALU.mult))
            v.wait_ge(B.ev.sem, B.ev.v)
            B.ev.inc(v.tensor_sub(stats[:, 1:2], stats[:, 1:2], stats[:, 2:3]))

        @b.scalar
        def _(s):
            s.wait_ge(B.ev.sem, B.ev.v)
            B.es.inc(s.activation(stats[:, 2:3], stats[:, 1:2], AF.Ln,
                                  bias=B.kc[:, KC_LNEPS:KC_LNEPS + 1]))
            s.wait_ge(B.es.sem, B.es.v)
            B.es.inc(s.activation(stats[:, 3:4], stats[:, 2:3], AF.Exp,
                                  scale=B.kc[:, KC_MHALF:KC_MHALF + 1]))
    with nc.Block() as b:
        @b.vector
        def _(v):
            v.wait_ge(B.es.sem, B.es.v)
            B.ev.inc(v.tensor_scalar(dst, src, stats[:, 0:1], stats[:, 3:4],
                                     ALU.add, ALU.mult))
    return B.ev.v


def emit_load_ln_t(B, x_in, r0, ntt, xT, ts_free, xf, xhat, stats, scr, pbt):
    nc = B.nc
    for tt in range(ntt):
        with nc.Block() as b:
            @b.sync
            def _(sy, _tt=tt):
                B.ds.inc(sy.dma_start(
                    xf[:, :], x_in[r0 + _tt * 128:r0 + _tt * 128 + 128, :]), 16)

            @b.vector
            def _(v):
                v.wait_ge(B.ds.sem, B.ds.v)

            @b.scalar
            def _(s):
                s.wait_ge(B.ds.sem, B.ds.v)
        emit_ln(B, xf[:, :], xhat[:, :], stats, scr)
        with nc.Block() as b:
            @b.tensor
            def _(t, _tt=tt):
                t.wait_ge(B.ev.sem, B.ev.v)
                for c in range(8):
                    inst = t.transpose(pbt[:, c * 128:(c + 1) * 128],
                                       xhat[:, c * 128:(c + 1) * 128],
                                       B.identb[:, :])
                B.et.inc(inst)

            @b.scalar
            def _(s, _tt=tt):
                s.wait_ge(B.et.sem, B.et.v)
                dst = strided_ap(xT[:, :], _tt * 128,
                                 [[xT.shape[1], 128], [ts_free, 8], [1, 128]])
                B.es.inc(s.copy(dst, pbt[:, :]))
        B.bar()


def emit_epilogue_tile(B, ysb, x_in, x_out, row0, xf, osb, stats, scr):
    nc = B.nc
    with nc.Block() as b:
        @b.sync
        def _(sy):
            B.ds.inc(sy.dma_start(xf[:, :], x_in[row0:row0 + 128, :]), 16)
    emit_ln(B, ysb[:, :], scr[:, :], stats, osb[:, :])
    with nc.Block() as b:
        @b.vector
        def _(v):
            v.wait_ge(B.ds.sem, B.ds.v)
            v.wait_ge(B.ev.sem, B.ev.v)
            B.ev.inc(v.tensor_add(osb[:, :], scr[:, :], xf[:, :]))

        @b.sync
        def _(sy):
            sy.wait_ge(B.ev.sem, B.ev.v)
            B.ds.inc(sy.dma_start(x_out[row0:row0 + 128, :], osb[:, :]), 16)
    B.bar()



def emit_l1_ffn(B, x_in, x_out, W):
    nc = B.nc
    DH = 4096
    TS = 512
    NSW = S // TS
    NTT = TS // 128

    w1b = B.dram("l1fw1b", [DM, DH], BF16)
    w2b = B.dram("l1fw2b", [DM, DH], BF16)
    h1b = B.dram("l1fh1b", [DH, DH], BF16)
    h2b = B.dram("l1fh2b", [DH, DH], BF16)
    w3b = B.dram("l1fw3b", [DH, DM], BF16)
    emit_cast_dram(B, w1b[:, :], W["l1f_w1T"])
    emit_cast_dram(B, w2b[:, :], W["l1f_w2T"])
    emit_cast_dram(B, h1b[:, :], W["l1f_h1T"])
    emit_cast_dram(B, h2b[:, :], W["l1f_h2T"])
    emit_cast_dram(B, w3b[:, :], W["l1f_w3T"])
    B.dma_fence()

    with contextlib.ExitStack() as st:
        BB = Bld.__new__(Bld)
        BB.__dict__.update(B.__dict__)
        BB.st = st
        sb = BB.sbuf
        s1c = sb("s1c", [128, 32], F32)
        s2c = sb("s2c", [128, 32], F32)
        sh1c = sb("sh1c", [128, 32], F32)
        sh2c = sb("sh2c", [128, 32], F32)
        s3b = sb("s3b", [128, DM], F32)
        s3row = sb("s3row", [1, DM], F32)
        with nc.Block() as b:
            @b.sync
            def _(sy):
                B.ds.inc(sy.dma_start(s1c[:, :], W["l1f_s1"]), 16)
                B.ds.inc(sy.dma_start(s2c[:, :], W["l1f_s2"]), 16)
                B.ds.inc(sy.dma_start(sh1c[:, :], W["l1f_sh1"]), 16)
                B.ds.inc(sy.dma_start(sh2c[:, :], W["l1f_sh2"]), 16)
                B.ds.inc(sy.dma_start(
                    s3row[0:1, :],
                    W["l1f_s3"].rearrange("(a b) -> a b", a=1)), 16)
        ps3 = BB.psum("ps3", [128, 512], F32)
        for half in range(2):
            with nc.Block() as b:
                @b.tensor
                def _(t, _h=half):
                    t.wait_ge(B.ds.sem, B.ds.v)
                    B.et.inc(t.matmul(ps3[:, :], B.onesf[0:1, :],
                                      s3row[0:1, _h * 512:(_h + 1) * 512],
                                      start=True, stop=True))

                @b.vector
                def _(v, _h=half):
                    v.wait_ge(B.et.sem, B.et.v)
                    B.ev.inc(v.tensor_copy(s3b[:, _h * 512:(_h + 1) * 512],
                                           ps3[:, :]))
        B.bar()

        xT = sb("xT", [128, 8 * TS], BF16)
        hT = sb("hT", [128, 32 * TS], BF16)
        g1T = sb("g1T", [128, 32 * TS], BF16)
        wsa = [sb("wsa", [128, 8 * 512], BF16) for _ in range(2)]
        wsb_ = [sb("wsb", [128, 8 * 512], BF16) for _ in range(2)]
        wh2 = [sb("wh2", [128, 32 * 256], BF16) for _ in range(2)]
        us = sb("us", [128, 2 * TS], F32)
        ysb = sb("ysb", [128, NTT * DM], F32)
        xf = sb("xf", [128, DM], F32)
        xhat = sb("xhat", [128, DM], BF16)
        stats = sb("stats", [128, 4], F32)
        scr = sb("scr", [128, DM], F32)
        osb = sb("osb", [128, DM], F32)
        pbt = BB.psum("pbt", [128, DM], BF16)
        py1 = [BB.psum("py1", [128, TS], F32) for _ in range(2)]
        py2 = [BB.psum("py2", [128, TS], F32) for _ in range(2)]

        for sw in range(NSW):
            r0 = sw * TS
            emit_load_ln_t(B, x_in, r0, NTT, xT, TS, xf, xhat, stats, scr, pbt)

            et_y = B.et.v
            es_y = B.es.v
            ev_y = B.ev.v
            ds0_y = B.ds.v
            ds_y = [ds0_y + 32 * (og + 1) for og in range(8)]
            with nc.Block() as b:
                @b.sync
                def _(sy):
                    for og in range(8):
                        if og >= 2:
                            sy.wait_ge(B.et.sem, et_y + (og - 1) * 8)
                        for (dst, wsrc) in ((wsa[og % 2], w1b),
                                            (wsb_[og % 2], w2b)):
                            src = wsrc[:, og * 512:(og + 1) * 512] \
                                .rearrange("(c p) n -> p c n", p=128)
                            B.ds.inc(sy.dma_start(
                                dst[:, :].rearrange("p (c n) -> p c n", c=8),
                                src), 16)

                @b.tensor
                def _(t):
                    for og in range(8):
                        t.wait_ge(B.ds.sem, ds_y[og])
                        for j in range(4):
                            jj = og * 4 + j
                            if jj >= 2:
                                t.wait_ge(B.es.sem, es_y + jj - 1)
                            for ic in range(8):
                                i1 = t.matmul(py1[jj % 2][:, :],
                                              wsa[og % 2][:, ic * 512 + j * 128:
                                                  ic * 512 + j * 128 + 128],
                                              xT[:, ic * TS:(ic + 1) * TS],
                                              start=(ic == 0), stop=(ic == 7))
                            B.et.inc(i1)
                            if jj >= 2:
                                t.wait_ge(B.ev.sem, ev_y + jj - 1)
                            for ic in range(8):
                                i2 = t.matmul(py2[jj % 2][:, :],
                                              wsb_[og % 2][:, ic * 512 + j * 128:
                                                   ic * 512 + j * 128 + 128],
                                              xT[:, ic * TS:(ic + 1) * TS],
                                              start=(ic == 0), stop=(ic == 7))
                            B.et.inc(i2)

                @b.scalar
                def _(s):
                    for jj in range(32):
                        s.wait_ge(B.et.sem, et_y + 2 * jj + 1)
                        if jj >= 2:
                            s.wait_ge(B.ev.sem, ev_y + jj - 1)
                        B.es.inc(s.activation(us[:, jj % 2 * TS:
                                                 (jj % 2 + 1) * TS],
                                              py1[jj % 2][:, :], AF.Silu,
                                              scale=s1c[:, jj:jj + 1]))

                @b.vector
                def _(v):
                    for jj in range(32):
                        v.wait_ge(B.et.sem, et_y + 2 * jj + 2)
                        v.wait_ge(B.es.sem, es_y + jj + 1)
                        B.ev.inc(v.scalar_tensor_tensor(
                            hT[:, jj * TS:(jj + 1) * TS],
                            py2[jj % 2][:, :], s2c[:, jj:jj + 1],
                            us[:, jj % 2 * TS:(jj % 2 + 1) * TS],
                            ALU.mult, ALU.mult))

            for (wdram, scol, srcT, dstT) in ((h1b, sh1c, hT, g1T),
                                              (h2b, sh2c, g1T, hT)):
                et_h = B.et.v
                es_h = B.es.v
                ds0_h = B.ds.v
                ds_h = [ds0_h + 32 * (sg + 1) for sg in range(16)]
                with nc.Block() as b:
                    @b.sync
                    def _(sy, _w=wdram):
                        for sg in range(16):
                            if sg >= 2:
                                sy.wait_ge(B.et.sem, et_h + (sg - 1) * 2)
                            src = _w[:, sg * 256:(sg + 1) * 256] \
                                .rearrange("(c p) n -> p c n", p=128)
                            dst = wh2[sg % 2][:, :].rearrange(
                                "p (c n) -> p c n", c=32)
                            for hlf in range(2):
                                B.ds.inc(sy.dma_start(
                                    dst[:, hlf * 16:(hlf + 1) * 16, :],
                                    src[:, hlf * 16:(hlf + 1) * 16, :]), 16)

                    @b.tensor
                    def _(t, _src=srcT):
                        for sg in range(16):
                            t.wait_ge(B.ds.sem, ds_h[sg])
                            for j in range(2):
                                jj = sg * 2 + j
                                if jj >= 2:
                                    t.wait_ge(B.es.sem, es_h + jj - 1)
                                for hc in range(32):
                                    i1 = t.matmul(
                                        py1[jj % 2][:, :],
                                        wh2[sg % 2][:, hc * 256 + j * 128:
                                            hc * 256 + j * 128 + 128],
                                        _src[:, hc * TS:(hc + 1) * TS],
                                        start=(hc == 0), stop=(hc == 31))
                                B.et.inc(i1)

                    @b.scalar
                    def _(s, _sc=scol, _dst=dstT):
                        for jj in range(32):
                            s.wait_ge(B.et.sem, et_h + jj + 1)
                            B.es.inc(s.activation(
                                _dst[:, jj * TS:(jj + 1) * TS],
                                py1[jj % 2][:, :], AF.Gelu,
                                scale=_sc[:, jj:jj + 1]))

            et_w = B.et.v
            ev_w = B.ev.v
            ds0_w = B.ds.v
            ds_w = [ds0_w + 32 * (dg + 1) for dg in range(4)]
            with nc.Block() as b:
                @b.sync
                def _(sy):
                    for dg in range(4):
                        if dg >= 2:
                            sy.wait_ge(B.et.sem, et_w + (dg - 1) * NTT)
                        src = w3b[:, dg * 256:(dg + 1) * 256] \
                            .rearrange("(c p) n -> p c n", p=128)
                        dst = wh2[dg % 2][:, :].rearrange(
                            "p (c n) -> p c n", c=32)
                        for hlf in range(2):
                            B.ds.inc(sy.dma_start(
                                dst[:, hlf * 16:(hlf + 1) * 16, :],
                                src[:, hlf * 16:(hlf + 1) * 16, :]), 16)

                @b.tensor
                def _(t):
                    for dg in range(4):
                        t.wait_ge(B.ds.sem, ds_w[dg])
                        for tt in range(NTT):
                            jj = dg * NTT + tt
                            if jj >= 2:
                                t.wait_ge(B.ev.sem, ev_w + jj - 1)
                            for hc in range(32):
                                i1 = t.matmul(
                                    py1[jj % 2][:, 0:256],
                                    hT[:, hc * TS + tt * 128:
                                       hc * TS + tt * 128 + 128],
                                    wh2[dg % 2][:, hc * 256:(hc + 1) * 256],
                                    start=(hc == 0), stop=(hc == 31))
                            B.et.inc(i1)

                @b.vector
                def _(v):
                    for dg in range(4):
                        for tt in range(NTT):
                            jj = dg * NTT + tt
                            v.wait_ge(B.et.sem, et_w + jj + 1)
                            B.ev.inc(v.tensor_mul(
                                ysb[:, tt * DM + dg * 256:
                                    tt * DM + dg * 256 + 256],
                                py1[jj % 2][:, 0:256],
                                s3b[:, dg * 256:(dg + 1) * 256]))

            for tt in range(NTT):
                emit_epilogue_tile(B, ysb[:, tt * DM:(tt + 1) * DM],
                                   x_in, x_out, r0 + tt * 128,
                                   xf, osb, stats, scr)



def emit_l0_ffn(B, x_in, x_out, W):
    nc = B.nc
    E, H = 8, 2048
    TS = 512
    NSW = S // TS
    NTT = TS // 128

    w1b = B.dram("l0fw1b", [E, DM, H], BF16)
    w2b = B.dram("l0fw2b", [E, H, DM], BF16)
    emit_cast_dram(B, w1b[:, :, :], W["l0f_w1"])
    emit_cast_dram(B, w2b[:, :, :], W["l0f_w2"])

    with contextlib.ExitStack() as st:
        BB = Bld.__new__(Bld)
        BB.__dict__.update(B.__dict__)
        BB.st = st
        sb = BB.sbuf

        gwT = sb("gwT", [128, 64], BF16)
        gbb = sb("gbb", [128, 8], F32)
        rwb = sb("rwb", [128, 8], F32)
        b2b = sb("b2b", [128, DM], F32)
        b1c = sb("b1c", [128, 128], F32)
        row8 = sb("row8", [1, 16], F32)
        b2sb = sb("b2sb", [8, DM], F32)
        b2row = sb("b2row", [1, DM], F32)
        psx = BB.psum("psx", [128, 512], F32)

        with nc.Block() as b:
            @b.gpsimd
            def _(g):
                for ic in range(8):
                    B.dg.inc(g.dma_start(
                        gwT[:, ic * 8:(ic + 1) * 8],
                        W["l0f_gwT"][ic * 128:(ic + 1) * 128, :]), 16)

            @b.sync
            def _(sy):
                B.ds.inc(sy.dma_start(
                    row8[0:1, 0:8],
                    W["l0f_gate_b"].rearrange("(a b) -> a b", a=1)), 16)
                B.ds.inc(sy.dma_start(
                    row8[0:1, 8:16],
                    W["l0f_rms_w"].rearrange("(a b) -> a b", a=1)), 16)
                B.ds.inc(sy.dma_start(b2sb[:, :], W["l0f_b2"]), 16)
                B.ds.inc(sy.dma_start(
                    b1c[:, :].rearrange("p (e c) -> p e c", e=8),
                    W["l0f_b1c"].rearrange("e p c -> p e c")), 16)
        with nc.Block() as b:
            @b.tensor
            def _(t):
                t.wait_ge(B.ds.sem, B.ds.v)
                B.et.inc(t.matmul(psx[:, 0:8], B.onesf[0:1, :],
                                  row8[0:1, 0:8], start=True, stop=True))
                B.et.inc(t.matmul(psx[:, 8:16], B.onesf[0:1, :],
                                  row8[0:1, 8:16], start=True, stop=True))

            @b.vector
            def _(v):
                v.wait_ge(B.et.sem, B.et.v)
                B.ev.inc(v.tensor_copy(gbb[:, :], psx[:, 0:8]))
                B.ev.inc(v.tensor_copy(rwb[:, :], psx[:, 8:16]))
        with nc.Block() as b:
            @b.tensor
            def _(t):
                t.wait_ge(B.ev.sem, B.ev.v)
                for hf in range(2):
                    B.et.inc(t.matmul(psx[0:1, 0:512], B.onescf[0:8, 0:1],
                                      b2sb[:, hf * 512:(hf + 1) * 512],
                                      start=True, stop=True))
                    t.wait_ge(B.ev.sem, B.ev.v + hf + 1)

            @b.vector
            def _(v):
                for hf in range(2):
                    v.wait_ge(B.et.sem, B.et.v - 1 + hf)
                    B.ev.inc(v.tensor_copy(
                        b2row[0:1, hf * 512:(hf + 1) * 512], psx[0:1, 0:512]))
        B.bar()
        with nc.Block() as b:
            @b.tensor
            def _(t):
                for hf in range(2):
                    B.et.inc(t.matmul(psx[:, 0:512], B.onesf[0:1, :],
                                      b2row[0:1, hf * 512:(hf + 1) * 512],
                                      start=True, stop=True))
                    t.wait_ge(B.ev.sem, B.ev.v + hf + 1)

            @b.vector
            def _(v):
                for hf in range(2):
                    v.wait_ge(B.et.sem, B.et.v - 1 + hf)
                    B.ev.inc(v.tensor_copy(
                        b2b[:, hf * 512:(hf + 1) * 512], psx[:, 0:512]))
        B.bar()

        xT = sb("xT", [128, 8 * TS], BF16)
        hT = sb("hT", [128, 16 * TS], BF16)
        yacc = sb("yacc", [128, NTT * DM], F32)
        w1sb = [sb("w1sb", [128, 8 * H], BF16) for _ in range(2)]
        w2sb = sb("w2sb", [128, 16 * DM], BF16)
        ngc = sb("ngc", [128, NTT], F32)
        lg = sb("lg", [128, 8], F32)
        scr8 = sb("scr8", [128, 8], F32)
        gst = sb("gst", [128, 4], F32)
        xf = sb("xf", [128, DM], F32)
        xhat = sb("xhat", [128, DM], BF16)
        stats = sb("stats", [128, 4], F32)
        scr = sb("scr", [128, DM], F32)
        ysb = sb("ysb", [128, DM], F32)
        osb = sb("osb", [128, DM], F32)
        pbt = BB.psum("pbt", [128, DM], BF16)
        psg = BB.psum("psg", [128, 8], F32)
        ph = [BB.psum("ph", [128, 512], F32) for _ in range(2)]
        py = [BB.psum("py", [128, 512], F32) for _ in range(2)]

        for sw in range(NSW):
            r0 = sw * TS
            emit_load_ln_t(B, x_in, r0, NTT, xT, TS, xf, xhat, stats, scr, pbt)

            for tt in range(NTT):
                with nc.Block() as b:
                    @b.tensor
                    def _(t, _tt=tt):
                        for ic in range(8):
                            i1 = t.matmul(psg[:, :],
                                          xT[:, ic * TS + _tt * 128:
                                             ic * TS + _tt * 128 + 128],
                                          gwT[:, ic * 8:(ic + 1) * 8],
                                          start=(ic == 0), stop=(ic == 7))
                        B.et.inc(i1)

                    @b.vector
                    def _(v):
                        v.wait_ge(B.et.sem, B.et.v)
                        B.ev.inc(v.tensor_add(lg[:, :], gbb[:, :], psg[:, :]))

                    @b.scalar
                    def _(s):
                        s.wait_ge(B.ev.sem, B.ev.v)
                        B.es.inc(s.activation(scr8[:, :], lg[:, :], AF.Square,
                                              accum_out=gst[:, 0:1]))
                        s.wait_ge(B.es.sem, B.es.v)
                        B.es.inc(s.activation(
                            gst[:, 1:2], gst[:, 0:1], AF.Ln,
                            bias=B.kc[:, KC_RMSEPS:KC_RMSEPS + 1],
                            scale=B.kc[:, KC_ATT:KC_ATT + 1]))
                        s.wait_ge(B.es.sem, B.es.v)
                        B.es.inc(s.activation(
                            gst[:, 2:3], gst[:, 1:2], AF.Exp,
                            scale=B.kc[:, KC_MHALF:KC_MHALF + 1]))

                    @b.vector
                    def _(v):
                        v.wait_ge(B.es.sem, B.es.v)
                        B.ev.inc(v.tensor_scalar(lg[:, :], lg[:, :],
                                                 gst[:, 2:3], None, ALU.mult))
                        v.wait_ge(B.ev.sem, B.ev.v)
                        B.ev.inc(v.tensor_mul(lg[:, :], lg[:, :], rwb[:, :]))
                        v.wait_ge(B.ev.sem, B.ev.v)
                        B.ev.inc(v.reduce_max(gst[:, 0:1], lg[:, :],
                                              axis=AX.X, negate=True))
                        v.wait_ge(B.ev.sem, B.ev.v)
                        B.ev.inc(v.tensor_scalar(gst[:, 1:2], gst[:, 0:1],
                                                 2.0, None, ALU.mult))

                    @b.scalar
                    def _(s):
                        s.wait_ge(B.ev.sem, B.ev.v)
                        B.es.inc(s.activation(scr8[:, :], lg[:, :], AF.Exp,
                                              bias=gst[:, 1:2],
                                              scale=B.kc[:, KC_TWO:KC_TWO + 1],
                                              accum_out=gst[:, 2:3]))

                    @b.vector
                    def _(v, _tt=tt):
                        v.wait_ge(B.es.sem, B.es.v)
                        B.ev.inc(v.tensor_scalar(gst[:, 3:4], gst[:, 2:3],
                                                 1e-6, 1.0, ALU.mult, ALU.add))
                        v.wait_ge(B.ev.sem, B.ev.v)
                        B.ev.inc(v.reciprocal(ngc[:, _tt:_tt + 1],
                                              gst[:, 3:4]))
                B.bar()

            ds_w1 = [0] * 8
            ds_w2 = [0] * 8
            def w1_load(sy, e):
                src = w1b[e, :, :].rearrange("(c p) n -> p c n", p=128)
                dst = w1sb[e % 2][:, :].rearrange("p (c n) -> p c n", c=8)
                for hf in range(2):
                    B.ds.inc(sy.dma_start(dst[:, hf * 4:(hf + 1) * 4, :],
                                          src[:, hf * 4:(hf + 1) * 4, :]), 16)
                ds_w1[e] = B.ds.v

            def w2_load(sy, e):
                src = w2b[e, :, :].rearrange("(c p) n -> p c n", p=128)
                dst = w2sb[:, :].rearrange("p (c n) -> p c n", c=16)
                for hf in range(2):
                    B.ds.inc(sy.dma_start(dst[:, hf * 8:(hf + 1) * 8, :],
                                          src[:, hf * 8:(hf + 1) * 8, :]), 16)
                ds_w2[e] = B.ds.v

            with nc.Block() as b:
                @b.sync
                def _(sy):
                    w1_load(sy, 0)
            for e in range(8):
                with nc.Block() as b:
                    @b.sync
                    def _(sy, _e=e):
                        w2_load(sy, _e)
                        if _e < 7:
                            w1_load(sy, _e + 1)
                et0, es0 = B.et.v, B.es.v
                wb = w1sb[e % 2]
                with nc.Block() as b:
                    @b.tensor
                    def _(t, _e=e, _wb=wb):
                        t.wait_ge(B.ds.sem, ds_w1[_e])
                        for hc in range(16):
                            if hc >= 2:
                                t.wait_ge(B.es.sem, es0 + hc - 1)
                            for ic in range(8):
                                i1 = t.matmul(
                                    ph[hc % 2][:, :],
                                    _wb[:, ic * H + hc * 128:
                                        ic * H + hc * 128 + 128],
                                    xT[:, ic * TS:ic * TS + TS],
                                    start=(ic == 0), stop=(ic == 7))
                            B.et.inc(i1)

                    @b.scalar
                    def _(s, _e=e):
                        for hc in range(16):
                            s.wait_ge(B.et.sem, et0 + hc + 1)
                            B.es.inc(s.activation(
                                hT[:, hc * TS:(hc + 1) * TS],
                                ph[hc % 2][:, :], AF.Gelu,
                                bias=b1c[:, _e * 16 + hc:_e * 16 + hc + 1]))
                B.bar()
                et0, ev0 = B.et.v, B.ev.v
                with nc.Block() as b:
                    @b.tensor
                    def _(t, _e=e):
                        t.wait_ge(B.ds.sem, ds_w2[_e])
                        for j in range(8):
                            tt, ds_ = j // 2, j % 2
                            if j >= 2:
                                t.wait_ge(B.ev.sem, ev0 + j - 1)
                            for hc in range(16):
                                i1 = t.matmul(
                                    py[j % 2][:, :],
                                    hT[:, hc * TS + tt * 128:
                                       hc * TS + tt * 128 + 128],
                                    w2sb[:, hc * DM + ds_ * 512:
                                         hc * DM + ds_ * 512 + 512],
                                    start=(hc == 0), stop=(hc == 15))
                            B.et.inc(i1)

                    @b.vector
                    def _(v, _e=e):
                        for j in range(8):
                            tt, ds_ = j // 2, j % 2
                            v.wait_ge(B.et.sem, et0 + j + 1)
                            dst = yacc[:, tt * DM + ds_ * 512:
                                       tt * DM + ds_ * 512 + 512]
                            if _e == 0:
                                B.ev.inc(v.tensor_copy(dst, py[j % 2][:, :]))
                            else:
                                B.ev.inc(v.tensor_add(dst, dst,
                                                      py[j % 2][:, :]))
                B.bar()

            for tt in range(NTT):
                with nc.Block() as b:
                    @b.vector
                    def _(v, _tt=tt):
                        B.ev.inc(v.tensor_add(
                            ysb[:, :], yacc[:, _tt * DM:(_tt + 1) * DM],
                            b2b[:, :]))
                        v.wait_ge(B.ev.sem, B.ev.v)
                        B.ev.inc(v.tensor_scalar(ysb[:, :], ysb[:, :],
                                                 ngc[:, _tt:_tt + 1], None,
                                                 ALU.mult))

                    @b.scalar
                    def _(s):
                        s.wait_ge(B.ev.sem, B.ev.v)
                emit_epilogue_tile(B, ysb, x_in, x_out, r0 + tt * 128,
                                   xf, osb, stats, scr)



def emit_l0_attn(B, x_in, x_out, W):
    nc = B.nc
    NH, G, HD = 16, 4, 64

    with contextlib.ExitStack() as st:
        BB = Bld.__new__(Bld)
        BB.__dict__.update(B.__dict__)
        BB.st = st
        sb = BB.sbuf

        qTp = sb("qTp", [128, 8 * S], BF16)
        kTp = sb("kTp", [128, 4 * S], BF16)
        vsb = sb("vsb", [128, 32 * 256], BF16)
        band_sb = sb("band", [128, 640], F32)
        ohb = sb("ohb", [128, 32 * 4], BF16)
        rmc = sb("rmc", [128, 32], F32)
        gpb = sb("gpb", [128, 4], F32)
        dupaddb = sb("dupaddb", [128, 4], F32)
        keepbf = sb("keepbf", [128, 4], F32)
        gpi = sb("gpi", [1, 4], I32)
        gpf = sb("gpf", [1, 4], F32)
        dup01 = sb("dup01", [1, 12], F32)
        brow = sb("brow", [1, 2560], F32)
        bqb = sb("bqb", [128, DM], F32)
        bkb = sb("bkb", [128, 256], F32)
        bvb = sb("bvb", [128, 256], F32)
        bob = sb("bob", [128, DM], F32)
        qgT = sb("qgT", [64, 64], BF16)
        kgT = sb("kgT", [128, 16], BF16)
        vgg = sb("vgg", [4, 256], BF16)
        ogsb = sb("ogsb", [4, DM], F32)
        ogbf = sb("ogbf", [4, DM], BF16)

        with nc.Block() as b:
            @b.sync
            def _(sy):
                B.ds.inc(sy.dma_start(band_sb[:, :], W["band_tbl"]), 16)
                B.ds.inc(sy.dma_start(
                    gpi[0:1, :], W["gp"].rearrange("(a b) -> a b", a=1)), 16)
                off = 0
                for nm, ln_ in (("l0a_bq", DM), ("l0a_bk", 256),
                                ("l0a_bv", 256), ("l0a_bo", DM)):
                    B.ds.inc(sy.dma_start(
                        brow[0:1, off:off + ln_],
                        W[nm].rearrange("(a b) -> a b", a=1)), 16)
                    off += ln_

            @b.scalar
            def _(s):
                s.wait_ge(B.ds.sem, B.ds.v)
                B.es.inc(s.copy(gpf[0:1, :], gpi[0:1, :]))

            @b.vector
            def _(v):
                v.wait_ge(B.es.sem, B.es.v)
                B.ev.inc(v.memset(dup01[0:1, :], 0.0))
                for k in range(1, 4):
                    for k2 in range(k):
                        v.wait_ge(B.ev.sem, B.ev.v)
                        B.ev.inc(v.tensor_tensor(
                            dup01[0:1, 4 + k:5 + k], gpf[0:1, k:k + 1],
                            gpf[0:1, k2:k2 + 1], ALU.is_equal))
                        v.wait_ge(B.ev.sem, B.ev.v)
                        B.ev.inc(v.tensor_max(dup01[0:1, k:k + 1],
                                              dup01[0:1, k:k + 1],
                                              dup01[0:1, 4 + k:5 + k]))
                v.wait_ge(B.ev.sem, B.ev.v)
                B.ev.inc(v.tensor_scalar(dup01[0:1, 8:12], dup01[0:1, 0:4],
                                         -1.0, 1.0, ALU.mult, ALU.add))
                B.ev.inc(v.tensor_scalar(dup01[0:1, 0:4], dup01[0:1, 0:4],
                                         NEG, None, ALU.mult))
        with contextlib.ExitStack() as pst:
            psx = pst.enter_context(
                nc.psum_tensor(B.name("psx"), [128, 512], F32))
            bc_jobs = [(gpb[:, :], gpf[0:1, :]),
                       (dupaddb[:, :], dup01[0:1, 0:4]),
                       (keepbf[:, :], dup01[0:1, 8:12]),
                       (bkb[:, :], brow[0:1, DM:DM + 256]),
                       (bvb[:, :], brow[0:1, DM + 256:DM + 512]),
                       (bqb[:, 0:512], brow[0:1, 0:512]),
                       (bqb[:, 512:DM], brow[0:1, 512:DM]),
                       (bob[:, 0:512], brow[0:1, DM + 512:DM + 1024]),
                       (bob[:, 512:DM], brow[0:1, DM + 1024:DM + 1536])]
            for ji, (dst, srow) in enumerate(bc_jobs):
                with nc.Block() as b:
                    @b.tensor
                    def _(t, _d=dst, _s=srow):
                        t.wait_ge(B.ev.sem, B.ev.v)
                        B.et.inc(t.matmul(psx[:, 0:_s.shape[-1]],
                                          B.onesf[0:1, :], _s,
                                          start=True, stop=True))
                        t.wait_ge(B.ev.sem, B.ev.v + 1)

                    @b.vector
                    def _(v, _d=dst):
                        v.wait_ge(B.et.sem, B.et.v)
                        B.ev.inc(v.tensor_copy(_d, psx[:, 0:_d.shape[-1]]))
        B.bar()
        with nc.Block() as b:
            @b.vector
            def _(v):
                for b32 in range(32):
                    B.ev.inc(v.tensor_scalar(rmc[:, 0:4], gpb[:, :],
                                             B.iotac[:, 0:1],
                                             float(128 * b32),
                                             ALU.subtract, ALU.subtract))
                    v.wait_ge(B.ev.sem, B.ev.v)
                    B.ev.inc(v.tensor_scalar(rmc[:, 0:4], rmc[:, 0:4], 0.0,
                                             None, ALU.is_equal))
                    v.wait_ge(B.ev.sem, B.ev.v)
                    B.ev.inc(v.tensor_mul(ohb[:, b32 * 4:(b32 + 1) * 4],
                                          rmc[:, 0:4], keepbf[:, :]))
                    v.wait_ge(B.ev.sem, B.ev.v)
        with nc.Block() as b:
            @b.vector
            def _(v):
                for b32 in range(32):
                    B.ev.inc(v.reduce_sum(rmc[:, b32:b32 + 1],
                                          ohb[:, b32 * 4:(b32 + 1) * 4],
                                          axis=AX.X))
                v.wait_ge(B.ev.sem, B.ev.v)
                B.ev.inc(v.tensor_scalar(rmc[:, :], rmc[:, :], -1.0, 1.0,
                                         ALU.mult, ALU.add))
        B.bar()

        with contextlib.ExitStack() as ast:
            asb = lambda nm, sh, dt: ast.enter_context(
                nc.sbuf_tensor(B.name(nm), sh, dt))
            wqsb = asb("wqsb", [128, 8 * DM], BF16)
            wksb = asb("wksb", [128, 8 * 256], BF16)
            wvsb = asb("wvsb", [128, 8 * 256], BF16)
            xf = asb("xf", [128, DM], F32)
            xhat = asb("xhat", [128, DM], BF16)
            xTt = asb("xTt", [128, DM], BF16)
            stats = asb("stats", [128, 4], F32)
            scr = asb("scr", [128, DM], F32)
            sint = asb("sint", [128, DM], F32)
            cost = asb("cost", [128, DM], F32)
            qf = asb("qf", [128, DM], F32)
            qrot = asb("qrot", [128, DM], F32)
            qhat = asb("qhat", [128, DM], BF16)
            khat = asb("khat", [128, 256], BF16)
            pbt = ast.enter_context(
                nc.psum_tensor(B.name("pbt"), [128, DM], BF16))
            pq = [ast.enter_context(
                nc.psum_tensor(B.name("pq"), [128, 512], F32))
                for _ in range(2)]
            psqg = ast.enter_context(
                nc.psum_tensor(B.name("psqg"), [64, 64], F32))
            pskg = ast.enter_context(
                nc.psum_tensor(B.name("pskg"), [128, 16], F32))
            psvg = ast.enter_context(
                nc.psum_tensor(B.name("psvg"), [4, 256], F32))

            with nc.Block() as b:
                @b.gpsimd
                def _(g):
                    for ic in range(8):
                        B.dg.inc(g.dma_start(
                            wqsb[:, ic * DM:(ic + 1) * DM],
                            W["l0a_Wq"][ic * 128:(ic + 1) * 128, :]), 16)
                        B.dg.inc(g.dma_start(
                            wksb[:, ic * 256:(ic + 1) * 256],
                            W["l0a_Wk"][ic * 128:(ic + 1) * 128, :]), 16)
                        B.dg.inc(g.dma_start(
                            wvsb[:, ic * 256:(ic + 1) * 256],
                            W["l0a_Wv"][ic * 128:(ic + 1) * 128, :]), 16)

            for rt in range(32):
                r0 = rt * 128
                with nc.Block() as b:
                    @b.sync
                    def _(sy, _r0=r0):
                        B.ds.inc(sy.dma_start(xf[:, :],
                                              x_in[_r0:_r0 + 128, :]), 16)
                        B.ds.inc(sy.dma_start(
                            sint[:, :], W["sin_rep"][_r0:_r0 + 128, :]), 16)
                        B.ds.inc(sy.dma_start(
                            cost[:, :], W["cos_rep"][_r0:_r0 + 128, :]), 16)

                    @b.vector
                    def _(v):
                        v.wait_ge(B.ds.sem, B.ds.v)

                    @b.scalar
                    def _(s):
                        s.wait_ge(B.ds.sem, B.ds.v)
                emit_ln(B, xf, xhat, stats, scr)
                with nc.Block() as b:
                    @b.tensor
                    def _(t):
                        t.wait_ge(B.ev.sem, B.ev.v)
                        for c in range(8):
                            i1 = t.transpose(pbt[:, c * 128:(c + 1) * 128],
                                             xhat[:, c * 128:(c + 1) * 128],
                                             B.identb[:, :])
                        B.et.inc(i1)

                    @b.scalar
                    def _(s):
                        s.wait_ge(B.et.sem, B.et.v)
                        B.es.inc(s.copy(xTt[:, :], pbt[:, :]))
                B.bar()
                with nc.Block() as b:
                    @b.tensor
                    def _(t, _rt=rt):
                        t.wait_ge(B.dg.sem, B.dg.v)
                        for half in range(2):
                            for ic in range(8):
                                i1 = t.matmul(
                                    pq[half][:, :],
                                    xTt[:, ic * 128:(ic + 1) * 128],
                                    wqsb[:, ic * DM + half * 512:
                                         ic * DM + half * 512 + 512],
                                    start=(ic == 0), stop=(ic == 7))
                            B.et.inc(i1)

                    @b.vector
                    def _(v):
                        for half in range(2):
                            v.wait_ge(B.et.sem, B.et.v - 1 + half)
                            B.ev.inc(v.tensor_add(
                                qf[:, half * 512:half * 512 + 512],
                                pq[half][:, :],
                                bqb[:, half * 512:half * 512 + 512]))
                with nc.Block() as b:
                    @b.vector
                    def _(v):
                        v.wait_ge(B.ev.sem, B.ev.v)
                        src_o = strided_ap(qf[:, :], 1, [[DM, 128], [2, 512]])
                        dst_e = strided_ap(qrot[:, :], 0, [[DM, 128], [2, 512]])
                        B.ev.inc(v.tensor_scalar(dst_e, src_o, -1.0, None,
                                                 ALU.mult))
                        src_e = strided_ap(qf[:, :], 0, [[DM, 128], [2, 512]])
                        dst_o = strided_ap(qrot[:, :], 1, [[DM, 128], [2, 512]])
                        B.ev.inc(v.tensor_copy(dst_o, src_e))
                        v.wait_ge(B.ev.sem, B.ev.v)
                        B.ev.inc(v.tensor_mul(qf[:, :], qf[:, :], cost[:, :]))
                        B.ev.inc(v.tensor_mul(qrot[:, :], qrot[:, :],
                                              sint[:, :]))
                        v.wait_ge(B.ev.sem, B.ev.v)
                        B.ev.inc(v.tensor_add(qhat[:, :], qf[:, :],
                                              qrot[:, :]))
                with nc.Block() as b:
                    @b.tensor
                    def _(t, _rt=rt):
                        t.wait_ge(B.ev.sem, B.ev.v)
                        for c in range(8):
                            i1 = t.transpose(pbt[:, c * 128:(c + 1) * 128],
                                             qhat[:, c * 128:(c + 1) * 128],
                                             B.identb[:, :])
                        B.et.inc(i1)
                        for h in range(NH):
                            i1 = t.matmul(psqg[0:64, h * 4:(h + 1) * 4],
                                          qhat[:, h * 64:(h + 1) * 64],
                                          ohb[:, _rt * 4:(_rt + 1) * 4],
                                          start=(_rt == 0), stop=(_rt == 31),
                                          skip_group_check=True)
                        B.et.inc(i1)

                    @b.scalar
                    def _(s, _rt=rt):
                        s.wait_ge(B.et.sem, B.et.v - 1)
                        dst = strided_ap(qTp[:, :], _rt * 128,
                                         [[8 * S, 128], [S, 8], [1, 128]])
                        B.es.inc(s.copy(dst, pbt[:, :]))
                with nc.Block() as b:
                    @b.tensor
                    def _(t):
                        for ic in range(8):
                            i1 = t.matmul(pq[0][:, 0:256],
                                          xTt[:, ic * 128:(ic + 1) * 128],
                                          wksb[:, ic * 256:(ic + 1) * 256],
                                          start=(ic == 0), stop=(ic == 7))
                        B.et.inc(i1)
                        for ic in range(8):
                            i1 = t.matmul(pq[1][:, 0:256],
                                          xTt[:, ic * 128:(ic + 1) * 128],
                                          wvsb[:, ic * 256:(ic + 1) * 256],
                                          start=(ic == 0), stop=(ic == 7))
                        B.et.inc(i1)

                    @b.vector
                    def _(v, _rt=rt):
                        v.wait_ge(B.et.sem, B.et.v - 1)
                        B.ev.inc(v.tensor_add(qf[:, 0:256], pq[0][:, 0:256],
                                              bkb[:, :]))
                        src_o = strided_ap(qf[:, :], 1, [[DM, 128], [2, 128]])
                        dst_e = strided_ap(qrot[:, :], 0, [[DM, 128], [2, 128]])
                        v.wait_ge(B.ev.sem, B.ev.v)
                        B.ev.inc(v.tensor_scalar(dst_e, src_o, -1.0, None,
                                                 ALU.mult))
                        src_e = strided_ap(qf[:, :], 0, [[DM, 128], [2, 128]])
                        dst_o = strided_ap(qrot[:, :], 1, [[DM, 128], [2, 128]])
                        B.ev.inc(v.tensor_copy(dst_o, src_e))
                        v.wait_ge(B.ev.sem, B.ev.v)
                        B.ev.inc(v.tensor_mul(qf[:, 0:256], qf[:, 0:256],
                                              cost[:, 0:256]))
                        B.ev.inc(v.tensor_mul(qrot[:, 0:256], qrot[:, 0:256],
                                              sint[:, 0:256]))
                        v.wait_ge(B.ev.sem, B.ev.v)
                        B.ev.inc(v.tensor_add(khat[:, :], qf[:, 0:256],
                                              qrot[:, 0:256]))
                        v.wait_ge(B.et.sem, B.et.v)
                        B.ev.inc(v.tensor_add(
                            vsb[:, _rt * 256:(_rt + 1) * 256],
                            pq[1][:, 0:256], bvb[:, :]))
                with nc.Block() as b:
                    @b.tensor
                    def _(t, _rt=rt):
                        t.wait_ge(B.ev.sem, B.ev.v)
                        t.wait_ge(B.es.sem, B.es.v)
                        for g_ in range(4):
                            for bp in (0, 64):
                                i1 = t.transpose(
                                    pbt[bp:bp + 64,
                                        g_ * 128:g_ * 128 + 128],
                                    khat[:, g_ * 64:(g_ + 1) * 64],
                                    B.identb[:, :])
                        B.et.inc(i1)
                        for g_ in range(4):
                            for bp in (0, 64):
                                i1 = t.matmul(
                                    pskg[bp:bp + 64, g_ * 4:(g_ + 1) * 4],
                                    khat[:, g_ * 64:(g_ + 1) * 64],
                                    ohb[:, _rt * 4:(_rt + 1) * 4],
                                    start=(_rt == 0), stop=(_rt == 31),
                                    skip_group_check=True)
                        B.et.inc(i1)
                        for g_ in range(4):
                            i1 = t.matmul(psvg[0:4, g_ * 64:(g_ + 1) * 64],
                                          ohb[:, _rt * 4:(_rt + 1) * 4],
                                          vsb[:, _rt * 256 + g_ * 64:
                                              _rt * 256 + g_ * 64 + 64],
                                          start=(_rt == 0), stop=(_rt == 31),
                                          skip_group_check=True)
                        B.et.inc(i1)

                    @b.scalar
                    def _(s, _rt=rt):
                        s.wait_ge(B.et.sem, B.et.v - 2)
                        dst = strided_ap(kTp[:, :], _rt * 128,
                                         [[4 * S, 128], [S, 4], [1, 128]])
                        B.es.inc(s.copy(dst, pbt[:, 0:512]))
                B.bar()
            with nc.Block() as b:
                @b.vector
                def _(v):
                    B.ev.inc(v.tensor_copy(qgT[0:64, :], psqg[0:64, :]))
                    B.ev.inc(v.tensor_copy(kgT[:, :], pskg[:, :]))
                    B.ev.inc(v.tensor_copy(vgg[0:4, :], psvg[0:4, :]))
            B.bar()

        with contextlib.ExitStack() as gst:
            gsb = lambda nm, sh, dt: gst.enter_context(
                nc.sbuf_tensor(B.name(nm), sh, dt))
            scg = gsb("scg", [4, S], F32)
            pgbf = gsb("pgbf", [4, S], BF16)
            pTg = gsb("pTg", [128, 128], BF16)
            rg = gsb("rg", [4, 4], F32)
            psg = gst.enter_context(
                nc.psum_tensor(B.name("psg"), [4, 512], F32))
            ptg = gst.enter_context(
                nc.psum_tensor(B.name("ptg"), [128, 128], BF16))
            psog = gst.enter_context(
                nc.psum_tensor(B.name("psog"), [4, 64], F32))
            for h in range(NH):
                g_ = h // 4
                with nc.Block() as b:
                    @b.tensor
                    def _(t, _h=h, _g=g_):
                        for cg in range(8):
                            B.et.inc(t.matmul(
                                psg[0:4, :],
                                qgT[0:64, _h * 4:(_h + 1) * 4],
                                kTp[0:64, _g * S + cg * 512:
                                    _g * S + cg * 512 + 512],
                                start=True, stop=True))
                            t.wait_ge(B.es.sem, B.es.v + cg + 1)

                    @b.scalar
                    def _(s):
                        for cg in range(8):
                            s.wait_ge(B.et.sem, B.et.v - 8 + cg + 1)
                            B.es.inc(s.activation(
                                scg[0:4, cg * 512:(cg + 1) * 512],
                                psg[0:4, :], AF.Copy,
                                scale=B.kc[0:4, KC_ATT:KC_ATT + 1]))
                with nc.Block() as b:
                    @b.vector
                    def _(v):
                        v.wait_ge(B.es.sem, B.es.v)
                        B.ev.inc(v.reduce_max(rg[0:4, 0:1], scg[0:4, :],
                                              axis=AX.X, negate=True))

                    @b.scalar
                    def _(s):
                        s.wait_ge(B.ev.sem, B.ev.v)
                        B.es.inc(s.activation(pgbf[0:4, :], scg[0:4, :],
                                              AF.Exp, bias=rg[0:4, 0:1],
                                              accum_out=rg[0:4, 1:2]))

                    @b.vector
                    def _(v):
                        v.wait_ge(B.es.sem, B.es.v)
                        B.ev.inc(v.reciprocal(rg[0:4, 2:3], rg[0:4, 1:2]))
                with nc.Block() as b:
                    @b.tensor
                    def _(t):
                        t.wait_ge(B.ev.sem, B.ev.v)
                        for jb in range(32):
                            i1 = t.transpose(ptg[:, jb * 4:(jb + 1) * 4],
                                             pgbf[0:4, jb * 128:
                                                  (jb + 1) * 128],
                                             B.identb[0:4, 0:4])
                        B.et.inc(i1)

                    @b.vector
                    def _(v):
                        v.wait_ge(B.et.sem, B.et.v)
                        B.ev.inc(v.tensor_copy(pTg[:, :], ptg[:, :]))
                with nc.Block() as b:
                    @b.tensor
                    def _(t, _g=g_):
                        t.wait_ge(B.ev.sem, B.ev.v)
                        for jb in range(32):
                            i1 = t.matmul(psog[0:4, :],
                                          pTg[:, jb * 4:(jb + 1) * 4],
                                          vsb[:, jb * 256 + _g * 64:
                                              jb * 256 + _g * 64 + 64],
                                          start=(jb == 0), stop=(jb == 31))
                        B.et.inc(i1)

                    @b.vector
                    def _(v, _h=h):
                        v.wait_ge(B.et.sem, B.et.v)
                        B.ev.inc(v.tensor_scalar(
                            ogsb[0:4, _h * 64:(_h + 1) * 64],
                            psog[0:4, :], rg[0:4, 2:3], None, ALU.mult))
                B.bar()
            with nc.Block() as b:
                @b.vector
                def _(v):
                    B.ev.inc(v.tensor_copy(ogbf[0:4, :], ogsb[0:4, :]))
            B.bar()

        with contextlib.ExitStack() as mst:
            msb = lambda nm, sh, dt: mst.enter_context(
                nc.sbuf_tensor(B.name(nm), sh, dt))
            wosb = msb("wosb", [128, 8 * DM], BF16)
            sc = msb("sc", [128, 648], F32)
            pbf = msb("pbf", [128, 648], BF16)
            pT = msb("pT", [128, 640], BF16)
            pTg4 = msb("pTg4", [4, 128], BF16)
            ohT = msb("ohT", [4, 128], BF16)
            gadd = msb("gadd", [128, 4], F32)
            rv = msb("rv", [128, 8], F32)
            osb = msb("osb", [128, DM], F32)
            obf = msb("obf", [128, DM], BF16)
            oT = msb("oT", [128, DM], BF16)
            ysb = msb("ysb", [128, DM], F32)
            xf2 = msb("xf2", [128, DM], F32)
            stats2 = msb("stats2", [128, 4], F32)
            scr2 = msb("scr2", [128, DM], F32)
            osc = msb("osc", [128, DM], F32)
            psA = mst.enter_context(
                nc.psum_tensor(B.name("psA"), [128, 512], F32))
            psB = mst.enter_context(
                nc.psum_tensor(B.name("psB"), [128, 260], F32))
            ptpm = mst.enter_context(
                nc.psum_tensor(B.name("ptpm"), [128, 640], BF16))
            ptg4 = mst.enter_context(
                nc.psum_tensor(B.name("ptg4"), [4, 128], BF16))
            pso = mst.enter_context(
                nc.psum_tensor(B.name("pso"), [128, 64], F32))
            pssc = mst.enter_context(
                nc.psum_tensor(B.name("pssc"), [128, 512], F32))
            pbt2 = mst.enter_context(
                nc.psum_tensor(B.name("pbt2"), [128, DM], BF16))
            psw = mst.enter_context(
                nc.psum_tensor(B.name("psw"), [128, 512], F32))

            with nc.Block() as b:
                @b.gpsimd
                def _(g):
                    for oc in range(8):
                        B.dg.inc(g.dma_start(
                            wosb[:, oc * DM:(oc + 1) * DM],
                            W["l0a_Wo"][oc * 128:(oc + 1) * 128, :]), 16)

            for rt in range(32):
                ws = max(0, rt - 2)
                we = min(31, rt + 2)
                nb = we - ws + 1
                Wd = nb * 128
                toff = (ws - (rt - 2)) * 128
                net = 3 if Wd > 512 else 2
                GA = 6
                et0, es0, ev0 = B.et.v, B.ev.v, B.ev.v
                et0, es0, ev0 = B.et.v, B.es.v, B.ev.v
                with nc.Block() as b:
                    @b.vector
                    def _(v, _rt=rt, _Wd=Wd, _toff=toff, _net=net):
                        B.ev.inc(v.tensor_scalar(gadd[:, :], gpb[:, :],
                                                 B.iotac[:, 0:1],
                                                 float(128 * _rt),
                                                 ALU.subtract, ALU.subtract))
                        v.wait_ge(B.ev.sem, B.ev.v)
                        B.ev.inc(v.tensor_scalar(rv[:, 4:8], gadd[:, :],
                                                 -256.0, None, ALU.is_lt))
                        B.ev.inc(v.tensor_scalar(gadd[:, :], gadd[:, :],
                                                 256.0, None, ALU.is_gt))
                        v.wait_ge(B.ev.sem, B.ev.v)
                        B.ev.inc(v.tensor_max(gadd[:, :], gadd[:, :],
                                              rv[:, 4:8]))
                        v.wait_ge(B.ev.sem, B.ev.v)
                        B.ev.inc(v.tensor_scalar(gadd[:, :], gadd[:, :],
                                                 -1.0, -NEG, ALU.add,
                                                 ALU.mult))
                        v.wait_ge(B.ev.sem, B.ev.v)
                        B.ev.inc(v.tensor_add(gadd[:, :], gadd[:, :],
                                              dupaddb[:, :]))
                        for h in range(NH):
                            v.wait_ge(B.es.sem, es0 + h * (_net + 1) + _net)
                            B.ev.inc(v.tensor_add(
                                sc[:, 0:_Wd], sc[:, 0:_Wd],
                                band_sb[:, _toff:_toff + _Wd]))
                            B.ev.inc(v.tensor_add(sc[:, _Wd:_Wd + 4],
                                                  sc[:, _Wd:_Wd + 4],
                                                  gadd[:, :]))
                            v.wait_ge(B.ev.sem, B.ev.v)
                            B.ev.inc(v.reduce_max(rv[:, 0:1],
                                                  sc[:, 0:_Wd + 4],
                                                  axis=AX.X, negate=True))
                            v.wait_ge(B.es.sem,
                                      es0 + (h + 1) * (_net + 1))
                            B.ev.inc(v.reciprocal(rv[:, 2:3], rv[:, 1:2]))
                            v.wait_ge(B.et.sem,
                                      et0 + h * (_net + 3) + _net + 1)
                            B.ev.inc(v.tensor_copy(pT[:, 0:_Wd],
                                                   ptpm[:, 0:_Wd]))
                            v.wait_ge(B.et.sem,
                                      et0 + h * (_net + 3) + _net + 2)
                            B.ev.inc(v.tensor_copy(pTg4[0:4, :],
                                                   ptg4[0:4, :]))
                            v.wait_ge(B.et.sem,
                                      et0 + h * (_net + 3) + _net + 3)
                            B.ev.inc(v.tensor_scalar(
                                osb[:, h * 64:(h + 1) * 64], pso[:, :],
                                rv[:, 2:3], None, ALU.mult))

                    @b.tensor
                    def _(t, _rt=rt, _ws=ws, _Wd=Wd, _nb=nb, _net=net):
                        for h in range(NH):
                            g_ = h // 4
                            kb = 64 * (h % 2)
                            qsl = qTp[kb:kb + 64,
                                      (h // 2) * S + _rt * 128:
                                      (h // 2) * S + _rt * 128 + 128]
                            n1 = min(_Wd, 512)
                            if h >= 1:
                                t.wait_ge(B.es.sem,
                                          es0 + (h - 1) * (_net + 1) + _net)
                            B.et.inc(t.matmul(
                                psA[:, 0:n1], qsl,
                                kTp[kb:kb + 64,
                                    g_ * S + _ws * 128:
                                    g_ * S + _ws * 128 + n1],
                                start=True, stop=True))
                            if _Wd > 512:
                                B.et.inc(t.matmul(
                                    psB[:, 0:_Wd - 512], qsl,
                                    kTp[kb:kb + 64,
                                        g_ * S + _ws * 128 + 512:
                                        g_ * S + _ws * 128 + _Wd],
                                    start=True, stop=True))
                            B.et.inc(t.matmul(
                                psB[:, 128:132], qsl,
                                kgT[kb:kb + 64, g_ * 4:(g_ + 1) * 4],
                                start=True, stop=True))
                            t.wait_ge(B.es.sem,
                                      es0 + h * (_net + 1) + _net + 1)
                            if h >= 1:
                                t.wait_ge(B.ev.sem,
                                          ev0 + GA + 7 * (h - 1) + 6)
                            for cb in range(_nb):
                                i1 = t.transpose(
                                    ptpm[:, cb * 128:(cb + 1) * 128],
                                    pbf[:, cb * 128:(cb + 1) * 128],
                                    B.identb[:, :])
                            B.et.inc(i1)
                            B.et.inc(t.transpose(ptg4[0:4, :],
                                                 pbf[:, _Wd:_Wd + 4],
                                                 B.identb[:, :]))
                            t.wait_ge(B.ev.sem, ev0 + GA + 7 * h + 6)
                            for cb in range(_nb):
                                t.matmul(pso[:, :],
                                         pT[:, cb * 128:(cb + 1) * 128],
                                         vsb[:, (_ws + cb) * 256 + g_ * 64:
                                             (_ws + cb) * 256 + g_ * 64 + 64],
                                         start=(cb == 0), stop=False)
                            B.et.inc(t.matmul(
                                pso[:, :], pTg4[0:4, :],
                                vgg[0:4, g_ * 64:(g_ + 1) * 64],
                                start=False, stop=True))

                    @b.scalar
                    def _(s, _Wd=Wd, _net=net):
                        n1 = min(_Wd, 512)
                        for h in range(NH):
                            s.wait_ge(B.et.sem, et0 + h * (_net + 3) + 1)
                            if h >= 1:
                                s.wait_ge(B.ev.sem,
                                          ev0 + GA + 7 * (h - 1) + 3)
                            B.es.inc(s.activation(
                                sc[:, 0:n1], psA[:, 0:n1], AF.Copy,
                                scale=B.kc[:, KC_ATT:KC_ATT + 1]))
                            s.wait_ge(B.et.sem, et0 + h * (_net + 3) + _net)
                            if _Wd > 512:
                                B.es.inc(s.activation(
                                    sc[:, 512:_Wd], psB[:, 0:_Wd - 512],
                                    AF.Copy,
                                    scale=B.kc[:, KC_ATT:KC_ATT + 1]))
                            B.es.inc(s.activation(
                                sc[:, _Wd:_Wd + 4], psB[:, 128:132],
                                AF.Copy, scale=B.kc[:, KC_ATT:KC_ATT + 1]))
                            s.wait_ge(B.ev.sem, ev0 + GA + 7 * h + 3)
                            if h >= 1:
                                s.wait_ge(B.et.sem,
                                          et0 + (h - 1) * (_net + 3) +
                                          _net + 2)
                            B.es.inc(s.activation(
                                pbf[:, 0:_Wd + 4], sc[:, 0:_Wd + 4],
                                AF.Exp, bias=rv[:, 0:1],
                                accum_out=rv[:, 1:2]))

                with nc.Block() as b:
                    @b.tensor
                    def _(t, _rt=rt):
                        B.et.inc(t.transpose(ptg4[0:4, :],
                                             ohb[:, _rt * 4:(_rt + 1) * 4],
                                             B.identb[:, :]))

                    @b.vector
                    def _(v):
                        v.wait_ge(B.et.sem, B.et.v)
                        B.ev.inc(v.tensor_copy(ohT[0:4, :], ptg4[0:4, :]))
                with nc.Block() as b:
                    @b.tensor
                    def _(t):
                        t.wait_ge(B.ev.sem, B.ev.v)
                        for half in range(2):
                            B.et.inc(t.matmul(
                                pssc[:, :], ohT[0:4, :],
                                ogbf[0:4, half * 512:half * 512 + 512],
                                start=True, stop=True))
                            t.wait_ge(B.ev.sem, B.ev.v + half + 1)

                    @b.vector
                    def _(v, _rt=rt):
                        for half in range(2):
                            v.wait_ge(B.et.sem, B.et.v - 1 + half)
                            sl = slice(half * 512, half * 512 + 512)
                            B.ev.inc(v.scalar_tensor_tensor(
                                osb[:, sl], osb[:, sl],
                                rmc[:, _rt:_rt + 1], pssc[:, :],
                                ALU.mult, ALU.add))
                        v.wait_ge(B.ev.sem, B.ev.v)
                        B.ev.inc(v.tensor_copy(obf[:, :], osb[:, :]))
                with nc.Block() as b:
                    @b.tensor
                    def _(t):
                        t.wait_ge(B.ev.sem, B.ev.v)
                        for c in range(8):
                            i1 = t.transpose(pbt2[:, c * 128:(c + 1) * 128],
                                             obf[:, c * 128:(c + 1) * 128],
                                             B.identb[:, :])
                        B.et.inc(i1)

                    @b.scalar
                    def _(s):
                        s.wait_ge(B.et.sem, B.et.v)
                        B.es.inc(s.copy(oT[:, :], pbt2[:, :]))
                with nc.Block() as b:
                    @b.tensor
                    def _(t):
                        t.wait_ge(B.es.sem, B.es.v)
                        t.wait_ge(B.dg.sem, B.dg.v)
                        for half in range(2):
                            for oc in range(8):
                                i1 = t.matmul(
                                    psw[:, :],
                                    oT[:, oc * 128:(oc + 1) * 128],
                                    wosb[:, oc * DM + half * 512:
                                         oc * DM + half * 512 + 512],
                                    start=(oc == 0), stop=(oc == 7))
                            B.et.inc(i1)
                            t.wait_ge(B.ev.sem, B.ev.v + half + 1)

                    @b.vector
                    def _(v):
                        for half in range(2):
                            v.wait_ge(B.et.sem, B.et.v - 1 + half)
                            sl = slice(half * 512, half * 512 + 512)
                            B.ev.inc(v.tensor_add(ysb[:, sl], psw[:, :],
                                                  bob[:, sl]))
                B.bar()
                emit_epilogue_tile(B, ysb, x_in, x_out, rt * 128,
                                   xf2, osc, stats2, scr2)



def emit_l1_attn(B, x_in, x_out, W):
    nc = B.nc
    NH, HD = 8, 64
    CL = 256
    NC_ = S // CL

    with contextlib.ExitStack() as st:
        BB = Bld.__new__(Bld)
        BB.__dict__.update(B.__dict__)
        BB.st = st
        sb = BB.sbuf

        wqsb = sb("wqsb", [128, 8 * 512], BF16)
        wksb = sb("wksb", [128, 8 * 512], BF16)
        wvsb = sb("wvsb", [128, 8 * 512], BF16)
        wosb = sb("wosb", [128, 4 * DM], BF16)
        gates = sb("gates", [128, 512], F32)
        bet = sb("bet", [1, 512], F32)
        memz = sb("memz", [64, 8 * 65], F32)
        membf = sb("membf", [64, 8 * 65], BF16)

        with nc.Block() as b:
            @b.gpsimd
            def _(g):
                for ic in range(8):
                    B.dg.inc(g.dma_start(
                        wqsb[:, ic * 512:(ic + 1) * 512],
                        W["l1a_Wq"][ic * 128:(ic + 1) * 128, :]), 16)
                    B.dg.inc(g.dma_start(
                        wksb[:, ic * 512:(ic + 1) * 512],
                        W["l1a_Wk"][ic * 128:(ic + 1) * 128, :]), 16)
                    B.dg.inc(g.dma_start(
                        wvsb[:, ic * 512:(ic + 1) * 512],
                        W["l1a_Wv"][ic * 128:(ic + 1) * 128, :]), 16)
                for oc in range(4):
                    B.dg.inc(g.dma_start(
                        wosb[:, oc * DM:(oc + 1) * DM],
                        W["l1a_Wo"][oc * 128:(oc + 1) * 128, :]), 16)
                g.memset(memz[0:64, :], 0.0)
                for h in range(NH):
                    B.eg.inc(g.memset(memz[0:64, h * 65 + 64:h * 65 + 65],
                                      1.0 / 64.0))

            @b.sync
            def _(sy):
                B.ds.inc(sy.dma_start(
                    bet[0:1, :],
                    W["l1a_betas"].rearrange("(a b) -> a b", a=1)), 16)

            @b.scalar
            def _(s):
                s.wait_ge(B.ds.sem, B.ds.v)
                B.es.inc(s.activation(bet[0:1, :], bet[0:1, :], AF.Sigmoid))
        with contextlib.ExitStack() as pst:
            psx = pst.enter_context(nc.psum_tensor(B.name("psx"),
                                                   [128, 512], F32))
            with nc.Block() as b:
                @b.tensor
                def _(t):
                    t.wait_ge(B.es.sem, B.es.v)
                    B.et.inc(t.matmul(psx[:, :], B.onesf[0:1, :], bet[0:1, :],
                                      start=True, stop=True))

                @b.vector
                def _(v):
                    v.wait_ge(B.et.sem, B.et.v)
                    B.ev.inc(v.tensor_copy(gates[:, :], psx[:, :]))
        B.bar()

        xTc = sb("xTc", [128, 8 * CL], BF16)
        qsb = sb("qsb", [128, 2 * 512], F32)
        ksb = sb("ksb", [128, 2 * 512], F32)
        qbf = sb("qbf", [128, 2 * 512], BF16)
        kbf = sb("kbf", [128, 2 * 512], BF16)
        vbf = sb("vbf", [128, 2 * 512], BF16)
        sqbf = sb("sqbf", [128, 2 * 512], BF16)
        skbf = sb("skbf", [128, 2 * 512], BF16)
        qTsb = sb("qTsb", [64, 8 * CL], BF16)
        kTsb = sb("kTsb", [64, 8 * CL], BF16)
        sqTsb = sb("sqTsb", [64, 8 * CL], BF16)
        pTsb = sb("pTsb", [128, 256], BF16)
        amsb = sb("amsb", [128, 2 * 512], F32)
        adsb = sb("adsb", [128, 2 * 512], F32)
        obf = sb("obf", [128, 2 * 512], BF16)
        oTsb = sb("oTsb", [128, 512], BF16)
        scrA = sb("scrA", [128, 512], F32)
        scrB = sb("scrB", [128, 512], F32)
        rc = sb("rc", [128, 8], F32)
        xf = sb("xf", [128, DM], F32)
        xhat = sb("xhat", [128, DM], BF16)
        stats = sb("stats", [128, 4], F32)
        scr = sb("scr", [128, DM], F32)
        ysb = sb("ysb", [128, DM], F32)
        osb = sb("osb", [128, DM], F32)

        pbt = BB.psum("pbt", [128, DM], BF16)
        ps_am = BB.psum("ps_am", [128, 65], F32)
        ps_s = BB.psum("ps_s", [128, 256], F32)
        ptp = BB.psum("ptp", [128, 256], BF16)
        ps_pv = BB.psum("ps_pv", [128, 64], F32)
        ps_up = BB.psum("ps_up", [64, 65], F32)
        ps_o = BB.psum("ps_o", [128, 512], F32)
        ps_ot = BB.psum("ps_ot", [128, 512], BF16)

        for n in range(NC_):
            c0 = n * CL
            emit_load_ln_t(B, x_in, c0, 2, xTc, CL, xf, xhat, stats, scr, pbt)

            for ti in range(2):
                ev0 = B.ev.v
                with nc.Block() as b:
                    @b.tensor
                    def _(t, _ti=ti):
                        t.wait_ge(B.dg.sem, B.dg.v)
                        evtgt = [ev0 + 2, ev0 + 4, ev0 + 5]
                        for _j, wsb_t in enumerate((wqsb, wksb, wvsb)):
                            for ic in range(8):
                                i1 = t.matmul(
                                    ps_o[:, :],
                                    xTc[:, ic * CL + _ti * 128:
                                        ic * CL + _ti * 128 + 128],
                                    wsb_t[:, ic * 512:(ic + 1) * 512],
                                    start=(ic == 0), stop=(ic == 7))
                            B.et.inc(i1)
                            t.wait_ge(B.ev.sem, evtgt[_j])

                    @b.vector
                    def _(v, _ti=ti):
                        for _j, (dst_f, dst_b) in enumerate(
                                ((qsb, qbf), (ksb, kbf), (None, vbf))):
                            v.wait_ge(B.et.sem, B.et.v - 2 + _j)
                            sl = slice(_ti * 512, _ti * 512 + 512)
                            if dst_f is not None:
                                B.ev.inc(v.tensor_copy(dst_f[:, sl],
                                                       ps_o[:, :]))
                                v.wait_ge(B.ev.sem, B.ev.v)
                            B.ev.inc(v.tensor_copy(dst_b[:, sl], ps_o[:, :]))
                B.bar()

            for (src_f, dst_b) in ((qsb, sqbf), (ksb, skbf)):
                for ti in range(2):
                    with nc.Block() as b:
                        @b.vector
                        def _(v, _s=src_f, _ti=ti):
                            sl = slice(_ti * 512, _ti * 512 + 512)
                            B.ev.inc(v.tensor_scalar(scrA[:, :], _s[:, sl],
                                                     0.0, None, ALU.min))

                        @b.scalar
                        def _(s):
                            s.wait_ge(B.ev.sem, B.ev.v)
                            B.es.inc(s.activation(scrB[:, :], scrA[:, :],
                                                  AF.Exp))

                        @b.vector
                        def _(v, _s=src_f, _d=dst_b, _ti=ti):
                            sl = slice(_ti * 512, _ti * 512 + 512)
                            v.wait_ge(B.es.sem, B.es.v)
                            B.ev.inc(v.tensor_scalar(scrA[:, :], _s[:, sl],
                                                     0.0, None, ALU.max))
                            v.wait_ge(B.ev.sem, B.ev.v)
                            B.ev.inc(v.tensor_add(_d[:, sl], scrB[:, :],
                                                  scrA[:, :]))
                    B.bar()

            for (srcb, dstT) in ((qbf, qTsb), (kbf, kTsb), (sqbf, sqTsb)):
                for ti in range(2):
                    with nc.Block() as b:
                        @b.tensor
                        def _(t, _s=srcb, _ti=ti):
                            for h in range(NH):
                                i1 = t.transpose(
                                    ptp[0:64, (h % 2) * 128:
                                        (h % 2) * 128 + 128] if False else
                                    ps_ot[0:64, h * 64:h * 64 + 128]
                                    if False else
                                    ptp[0:64, 0:128],
                                    _s[:, _ti * 512 + h * 64:
                                       _ti * 512 + h * 64 + 64],
                                    B.identb[:, :])
                                B.et.inc(i1)
                                t.wait_ge(B.ev.sem, B.ev.v + h + 1)

                        @b.vector
                        def _(v, _d=dstT, _ti=ti):
                            for h in range(NH):
                                v.wait_ge(B.et.sem, B.et.v - NH + h + 1)
                                dst = strided_ap(
                                    _d[0:64, :], h * CL + _ti * 128,
                                    [[8 * CL, 64], [1, 128]])
                                B.ev.inc(v.tensor_copy(dst, ptp[0:64, 0:128]))
                    B.bar()

            with nc.Block() as b:
                @b.vector
                def _(v):
                    B.ev.inc(v.tensor_copy(membf[0:64, :], memz[0:64, :]))
            B.bar()
            if _DBG and n == 1:
                with nc.Block() as b:
                    @b.sync
                    def _(sy):
                        B.ds.inc(sy.dma_start(
                            _DBG_T["dbg_mem"][:, :], memz[0:64, :]), 16)
                        sy.wait_ge(B.ds.sem, B.ds.v)

            for h in range(NH):
                for ti in range(2):
                    with nc.Block() as b:
                        @b.tensor
                        def _(t, _h=h, _ti=ti):
                            B.et.inc(t.matmul(
                                ps_am[:, :],
                                sqTsb[0:64, _h * CL + _ti * 128:
                                      _h * CL + _ti * 128 + 128],
                                membf[0:64, _h * 65:(_h + 1) * 65],
                                start=True, stop=True))
                            B.et.inc(t.matmul(
                                ps_s[:, :],
                                qTsb[0:64, _h * CL + _ti * 128:
                                     _h * CL + _ti * 128 + 128],
                                kTsb[0:64, _h * CL:(_h + 1) * CL],
                                start=True, stop=True))

                        @b.vector
                        def _(v, _h=h, _ti=ti):
                            v.wait_ge(B.et.sem, B.et.v - 1)
                            B.ev.inc(v.reciprocal(rc[:, 0:1],
                                                  ps_am[:, 64:65]))
                            v.wait_ge(B.ev.sem, B.ev.v)
                            B.ev.inc(v.tensor_scalar(
                                amsb[:, _ti * 512 + _h * 64:
                                     _ti * 512 + _h * 64 + 64],
                                ps_am[:, 0:64], rc[:, 0:1], None, ALU.mult))
                            v.wait_ge(B.et.sem, B.et.v)
                            B.ev.inc(v.reduce_max(rc[:, 1:2], ps_s[:, :],
                                                  axis=AX.X, negate=True))
                            v.wait_ge(B.ev.sem, B.ev.v)
                            B.ev.inc(v.tensor_scalar(rc[:, 2:3], rc[:, 1:2],
                                                     0.125, None, ALU.mult))

                        @b.scalar
                        def _(s):
                            s.wait_ge(B.ev.sem, B.ev.v)
                            B.es.inc(s.activation(
                                scrA[:, 0:256],
                                ps_s[:, :], AF.Exp, bias=rc[:, 2:3],
                                scale=B.kc[:, KC_ATT:KC_ATT + 1],
                                accum_out=rc[:, 3:4]))

                        @b.vector
                        def _(v):
                            v.wait_ge(B.es.sem, B.es.v)
                            B.ev.inc(v.tensor_copy(pTsb[:, 0:256],
                                                   scrA[:, 0:256]))
                            B.ev.inc(v.reciprocal(rc[:, 4:5], rc[:, 3:4]))
                    with nc.Block() as b:
                        @b.tensor
                        def _(t, _h=h, _ti=ti):
                            t.wait_ge(B.ev.sem, B.ev.v)
                            for jb in range(2):
                                i1 = t.transpose(
                                    ptp[:, jb * 128:(jb + 1) * 128],
                                    pTsb[:, jb * 128:(jb + 1) * 128],
                                    B.identb[:, :])
                            B.et.inc(i1)

                        @b.vector
                        def _(v):
                            v.wait_ge(B.et.sem, B.et.v)
                            B.ev.inc(v.tensor_copy(oTsb[:, 0:256],
                                                   ptp[:, :]))
                    with nc.Block() as b:
                        @b.tensor
                        def _(t, _h=h, _ti=ti):
                            t.wait_ge(B.ev.sem, B.ev.v)
                            for jb in range(2):
                                i1 = t.matmul(
                                    ps_pv[:, :],
                                    oTsb[:, jb * 128:(jb + 1) * 128],
                                    vbf[:, jb * 512 + _h * 64:
                                        jb * 512 + _h * 64 + 64],
                                    start=(jb == 0), stop=(jb == 1))
                            B.et.inc(i1)

                        @b.vector
                        def _(v, _h=h, _ti=ti):
                            v.wait_ge(B.et.sem, B.et.v)
                            B.ev.inc(v.tensor_scalar(
                                adsb[:, _ti * 512 + _h * 64:
                                     _ti * 512 + _h * 64 + 64],
                                ps_pv[:, :], rc[:, 4:5], None, ALU.mult))
                    B.bar()
                with nc.Block() as b:
                    @b.tensor
                    def _(t, _h=h):
                        for ti in range(2):
                            i1 = t.matmul(ps_up[0:64, 0:64],
                                          skbf[:, ti * 512 + _h * 64:
                                               ti * 512 + _h * 64 + 64],
                                          vbf[:, ti * 512 + _h * 64:
                                              ti * 512 + _h * 64 + 64],
                                          start=(ti == 0), stop=(ti == 1))
                        B.et.inc(i1)
                        for ti in range(2):
                            i1 = t.matmul(ps_up[0:64, 64:65],
                                          skbf[:, ti * 512 + _h * 64:
                                               ti * 512 + _h * 64 + 64],
                                          B.onesb[:, 0:1],
                                          start=(ti == 0), stop=(ti == 1))
                        B.et.inc(i1)

                    @b.vector
                    def _(v, _h=h):
                        v.wait_ge(B.et.sem, B.et.v)
                        B.ev.inc(v.tensor_add(memz[0:64, _h * 65:(_h + 1) * 65],
                                              memz[0:64, _h * 65:(_h + 1) * 65],
                                              ps_up[0:64, :]))
                B.bar()

            if _DBG and n == 1:
                with nc.Block() as b:
                    @b.sync
                    def _(sy):
                        B.ds.inc(sy.dma_start(_DBG_T["dbg_am"][:, :],
                                              amsb[:, :]), 16)
                        B.ds.inc(sy.dma_start(_DBG_T["dbg_ad"][:, :],
                                              adsb[:, :]), 16)
                        sy.wait_ge(B.ds.sem, B.ds.v)

            for ti in range(2):
                with nc.Block() as b:
                    @b.vector
                    def _(v, _ti=ti):
                        sl = slice(_ti * 512, _ti * 512 + 512)
                        B.ev.inc(v.tensor_sub(scrA[:, :], amsb[:, sl],
                                              adsb[:, sl]))
                        v.wait_ge(B.ev.sem, B.ev.v)
                        B.ev.inc(v.tensor_mul(scrA[:, :], scrA[:, :],
                                              gates[:, :]))
                        v.wait_ge(B.ev.sem, B.ev.v)
                        B.ev.inc(v.tensor_add(obf[:, sl], scrA[:, :],
                                              adsb[:, sl]))
                B.bar()
                with nc.Block() as b:
                    @b.tensor
                    def _(t, _ti=ti):
                        for oc in range(4):
                            i1 = t.transpose(
                                ps_ot[:, oc * 128:(oc + 1) * 128],
                                obf[:, _ti * 512 + oc * 128:
                                    _ti * 512 + oc * 128 + 128],
                                B.identb[:, :])
                        B.et.inc(i1)

                    @b.vector
                    def _(v):
                        v.wait_ge(B.et.sem, B.et.v)
                        B.ev.inc(v.tensor_copy(oTsb[:, :], ps_ot[:, :]))
                with nc.Block() as b:
                    @b.tensor
                    def _(t, _ti=ti):
                        t.wait_ge(B.ev.sem, B.ev.v)
                        for ds_ in range(2):
                            for oc in range(4):
                                i1 = t.matmul(
                                    ps_o[:, :] if ds_ == 0 else ps_s[:, :]
                                    if False else ps_o[:, :],
                                    oTsb[:, oc * 128:(oc + 1) * 128],
                                    wosb[:, oc * DM + ds_ * 512:
                                         oc * DM + ds_ * 512 + 512],
                                    start=(oc == 0), stop=(oc == 3))
                            B.et.inc(i1)
                            t.wait_ge(B.ev.sem, B.ev.v + ds_ + 1)

                    @b.vector
                    def _(v):
                        for ds_ in range(2):
                            v.wait_ge(B.et.sem, B.et.v - 1 + ds_)
                            B.ev.inc(v.tensor_copy(
                                ysb[:, ds_ * 512:(ds_ + 1) * 512],
                                ps_o[:, :]))
                B.bar()
                emit_epilogue_tile(B, ysb, x_in, x_out, c0 + ti * 128,
                                   xf, osb, stats, scr)



def _host_tables():
    out = {}
    p = np.arange(128)[:, None]
    n = np.arange(640)[None, :]
    band = np.where((n - p >= 0) & (n - p <= 512), 0.0, NEG).astype(np.float32)
    out["band_tbl"] = band
    j = np.arange(32)
    invf = (10000.0 ** (-(2.0 * j) / 64.0)).astype(np.float32)
    t = np.arange(S, dtype=np.float32)
    ang = t[:, None] * invf[None, :]
    emb = np.concatenate([ang, ang], axis=1)
    sin1 = np.sin(emb).astype(np.float32)
    cos1 = np.cos(emb).astype(np.float32)
    out["sin_rep"] = np.ascontiguousarray(np.tile(sin1, (1, 16)))
    out["cos_rep"] = np.ascontiguousarray(np.tile(cos1, (1, 16)))
    return out


def _prep_host(inputs):
    g = {}
    f32 = np.float32
    asc = np.ascontiguousarray

    def scol(v):
        v = np.asarray(v)
        return asc(v.reshape(-1, 128).T)

    g["l1f_w1T"] = asc(np.asarray(inputs["L1_ffn_w1_w"]).T)
    g["l1f_w2T"] = asc(np.asarray(inputs["L1_ffn_w2_w"]).T)
    g["l1f_h1T"] = asc(np.asarray(inputs["L1_ffn_h1_w"]).T)
    g["l1f_h2T"] = asc(np.asarray(inputs["L1_ffn_h2_w"]).T)
    g["l1f_w3T"] = asc(np.asarray(inputs["L1_ffn_w3_w"]).T)
    g["l1f_s1"] = scol(inputs["L1_ffn_w1_s"])
    g["l1f_s2"] = scol(inputs["L1_ffn_w2_s"])
    g["l1f_sh1"] = scol(inputs["L1_ffn_h1_s"])
    g["l1f_sh2"] = scol(inputs["L1_ffn_h2_s"])
    g["l1f_s3"] = asc(np.asarray(inputs["L1_ffn_w3_s"], dtype=f32))
    g["l0f_w1"] = asc(np.asarray(inputs["L0_ffn_w1"]))
    g["l0f_w2"] = asc(np.asarray(inputs["L0_ffn_w2"]))
    g["l0f_gwT"] = asc(np.asarray(inputs["L0_ffn_gate_w"]).T)
    g["l0f_gate_b"] = asc(np.asarray(inputs["L0_ffn_gate_b"]))
    g["l0f_rms_w"] = asc(np.asarray(inputs["L0_ffn_rms_w"]))
    g["l0f_b1c"] = asc(np.asarray(inputs["L0_ffn_b1"])
                       .reshape(8, 16, 128).transpose(0, 2, 1))
    g["l0f_b2"] = asc(np.asarray(inputs["L0_ffn_b2"]))
    for k_ in ("Wq", "bq", "Wk", "bk", "Wv", "bv", "Wo", "bo"):
        g["l0a_" + k_] = asc(np.asarray(inputs["L0_attn_" + k_]))
    g["l1a_Wq"] = asc(np.asarray(inputs["L1_attn_Wq"]))
    g["l1a_Wk"] = asc(np.asarray(inputs["L1_attn_Wk"]))
    g["l1a_Wv"] = asc(np.asarray(inputs["L1_attn_Wv"]))
    g["l1a_Wo"] = asc(np.asarray(inputs["L1_attn_Wo"]))
    g["l1a_betas"] = asc(np.asarray(inputs["L1_attn_betas"]).reshape(512))
    for k in list(g):
        g[k] = np.asarray(g[k], dtype=f32)
    g["gp"] = np.ascontiguousarray(
        np.asarray(inputs["L0_attn_global_positions"], dtype=np.int32))
    g.update(_host_tables())
    return g


BLOCK_INPUTS = {
    "l1f": ["l1f_w1T", "l1f_w2T", "l1f_h1T", "l1f_h2T", "l1f_w3T",
            "l1f_s1", "l1f_s2", "l1f_sh1", "l1f_sh2", "l1f_s3"],
    "l0f": ["l0f_w1", "l0f_w2", "l0f_gwT", "l0f_gate_b", "l0f_rms_w",
            "l0f_b1c", "l0f_b2"],
    "l1a": ["l1a_Wq", "l1a_Wk", "l1a_Wv", "l1a_Wo", "l1a_betas"],
    "l0a": ["l0a_Wq", "l0a_bq", "l0a_Wk", "l0a_bk", "l0a_Wv", "l0a_bv",
            "l0a_Wo", "l0a_bo", "gp", "band_tbl", "sin_rep", "cos_rep"],
}

EMITS = {"l1f": emit_l1_ffn, "l0f": emit_l0_ffn, "l1a": emit_l1_attn,
         "l0a": emit_l0_attn}

SHAPES = {
    "l1f_w1T": [DM, 4096], "l1f_w2T": [DM, 4096],
    "l1f_h1T": [4096, 4096], "l1f_h2T": [4096, 4096],
    "l1f_w3T": [4096, DM],
    "l1f_s1": [128, 32], "l1f_s2": [128, 32],
    "l1f_sh1": [128, 32], "l1f_sh2": [128, 32],
    "l1f_s3": [DM],
    "l0f_w1": [8, DM, 2048], "l0f_w2": [8, 2048, DM],
    "l0f_gwT": [DM, 8], "l0f_gate_b": [8], "l0f_rms_w": [8],
    "l0f_b1c": [8, 128, 16], "l0f_b2": [8, DM],
    "l1a_Wq": [DM, 512], "l1a_Wk": [DM, 512], "l1a_Wv": [DM, 512],
    "l1a_Wo": [512, DM], "l1a_betas": [512],
    "l0a_Wq": [DM, DM], "l0a_bq": [DM], "l0a_Wk": [DM, 256],
    "l0a_bk": [256], "l0a_Wv": [DM, 256], "l0a_bv": [256],
    "l0a_Wo": [DM, DM], "l0a_bo": [DM], "gp": [4],
    "band_tbl": [128, 640], "sin_rep": [S, DM], "cos_rep": [S, DM],
}


_DBG = False
_DBG_T = {}


def _build(block_names):
    nc = bass.Bass("TRN2", target_bir_lowering=False)
    xin = nc.dram_tensor("xin", [S, DM], F32, kind="ExternalInput")
    xout = nc.dram_tensor("xout", [S, DM], F32, kind="ExternalOutput")
    if _DBG:
        _DBG_T["dbg_mem"] = nc.dram_tensor("dbg_mem", [64, 520], F32,
                                           kind="ExternalOutput")
        _DBG_T["dbg_am"] = nc.dram_tensor("dbg_am", [128, 1024], F32,
                                          kind="ExternalOutput")
        _DBG_T["dbg_ad"] = nc.dram_tensor("dbg_ad", [128, 1024], F32,
                                          kind="ExternalOutput")
    needed = []
    for bn in block_names:
        needed += BLOCK_INPUTS[bn]
    W = {}
    for name in needed:
        dt = I32 if name == "gp" else F32
        W[name] = nc.dram_tensor(name, SHAPES[name], dt,
                                 kind="ExternalInput")[tuple(
                                     slice(None) for _ in SHAPES[name])]
    with contextlib.ExitStack() as st:
        B = Bld(nc, st)
        emit_consts(B)
        cur_in = xin[:, :]
        for i, bn in enumerate(block_names):
            last = (i == len(block_names) - 1)
            if last:
                cur_out = xout[:, :]
            else:
                h = B.dram(f"xh{i}", [S, DM], F32)
                cur_out = h[:, :]
            EMITS[bn](B, cur_in, cur_out, W)
            cur_in = cur_out
            B.bar()
            B.dma_fence()
        with nc.Block() as b:
            @b.sync
            def _(sy):
                sy.wait_ge(B.ds.sem, B.ds.v)
                sy.wait_ge(B.dg.sem, B.dg.v)
    return nc, needed


_PROG_CACHE = {}


def _run_blocks(block_names, x_full, inputs):
    key = tuple(block_names)
    if key not in _PROG_CACHE:
        _PROG_CACHE[key] = _build(block_names)
    nc, needed = _PROG_CACHE[key]
    shared = _prep_host(inputs)
    in_maps = []
    for c in range(8):
        m = {"xin": np.ascontiguousarray(x_full[c])}
        for name in needed:
            m[name] = shared[name]
        in_maps.append(m)
    res = run_bass_kernel_spmd(nc, in_maps, core_ids=list(range(8)))
    if _DBG:
        _DBG["last"] = res.results
    return np.stack([np.asarray(r["xout"]) for r in res.results])


GROUPS = [["l0a", "l0f", "l1a", "l1f"]]


def kernel(**inputs):
    x = np.asarray(inputs["L0_attn_x"], dtype=np.float32)
    for grp in GROUPS:
        x = _run_blocks(grp, x, inputs)
    return x


# version 05 (final)
# speedup vs baseline: 1.5545x; 1.0508x over previous
import contextlib
import math

import numpy as np
import concourse.bass as bass
import concourse.mybir as mybir
from concourse.bass_utils import run_bass_kernel_spmd

F32 = mybir.dt.float32
BF16 = mybir.dt.bfloat16
I32 = mybir.dt.int32
AF = mybir.ActivationFunctionType
ALU = mybir.AluOpType
AX = mybir.AxisListType

S = 4096
DM = 1024
NEG = -1.0e30
LN_EPS = 1e-5
RMS_EPS = float(np.finfo(np.float32).eps)

ENGS = ("sync", "tensor", "vector", "scalar", "gpsimd")
_L1F_NEWY = True
_L1F_NEWH = True


class Ctr:

    def __init__(self, sem):
        self.sem = sem
        self.v = 0

    def inc(self, inst, n=1):
        inst.then_inc(self.sem, n)
        self.v += n
        return self.v


_NAME_CTR = [0]


class Bld:

    def __init__(self, nc, st):
        self.nc = nc
        self.st = st
        self.bars = 0
        self.ds = self.mksem("ds")
        self.dg = self.mksem("dg")
        self.dc = self.mksem("dc")
        self.et = self.mksem("et")
        self.ev = self.mksem("ev")
        self.es = self.mksem("es")
        self.eg = self.mksem("eg")
        self.w0 = self.mksem("w0")
        self.w1 = self.mksem("w1")
        self.w2 = self.mksem("w2")
        self.barsem = self.mksem("bar")
        self.identb = None
        self.identf = None
        self.iotac = None
        self.onesf = None
        self.onesb = None
        self.kc = None

    def name(self, base):
        _NAME_CTR[0] += 1
        return f"{base}{_NAME_CTR[0]}"

    def mksem(self, base):
        return Ctr(self.st.enter_context(self.nc.semaphore(self.name(base))))

    def sbuf(self, base, shape, dtype):
        return self.st.enter_context(
            self.nc.sbuf_tensor(self.name(base), shape, dtype))

    def psum(self, base, shape, dtype):
        return self.st.enter_context(
            self.nc.psum_tensor(self.name(base), shape, dtype))

    def dram(self, base, shape, dtype):
        return self.nc.dram_tensor(self.name(base), shape, dtype, kind="Internal")

    def bar(self):
        return

    def dma_fence(self):
        with self.nc.Block() as b:
            @b.sync
            def _(sy):
                sy.wait_ge(self.ds.sem, self.ds.v)
                sy.wait_ge(self.dg.sem, self.dg.v)
                sy.wait_ge(self.dc.sem, self.dc.v)


KC_LNEPS = 0
KC_MHALF = 1
KC_RMSEPS = 2
KC_TWO = 3
KC_ATT = 4
KC_ONE = 5


def emit_consts(B):
    nc = B.nc
    B.identb = B.sbuf("identb", [128, 128], BF16)
    B.identf = B.sbuf("identf", [128, 128], F32)
    B.iotac = B.sbuf("iotac", [128, 1], F32)
    B.onesf = B.sbuf("onesf", [1, 128], F32)
    B.onescf = B.sbuf("onescf", [128, 1], F32)
    B.onesb = B.sbuf("onesb", [128, 1], BF16)
    B.kc = B.sbuf("kc", [128, 8], F32)
    with nc.Block() as b:
        @b.gpsimd
        def _(g):
            g.memset(B.identb[:, :], 0.0)
            g.affine_select(out=B.identb[:, :], in_=B.identb[:, :],
                            compare_op=ALU.not_equal, fill=1.0, base=0,
                            pattern=[[-1, 128]], channel_multiplier=1)
            g.memset(B.identf[:, :], 0.0)
            g.affine_select(out=B.identf[:, :], in_=B.identf[:, :],
                            compare_op=ALU.not_equal, fill=1.0, base=0,
                            pattern=[[-1, 128]], channel_multiplier=1)
            g.iota(B.iotac[:, :], pattern=[[0, 1]], base=0, channel_multiplier=1,
                   allow_small_or_imprecise_dtypes=True)
            g.memset(B.onesf[0:1, :], 1.0)
            g.memset(B.onescf[:, :], 1.0)
            g.memset(B.onesb[:, :], 1.0)
            g.memset(B.kc[:, KC_LNEPS:KC_LNEPS + 1], LN_EPS)
            g.memset(B.kc[:, KC_MHALF:KC_MHALF + 1], -0.5)
            g.memset(B.kc[:, KC_RMSEPS:KC_RMSEPS + 1], RMS_EPS)
            g.memset(B.kc[:, KC_TWO:KC_TWO + 1], 2.0)
            g.memset(B.kc[:, KC_ATT:KC_ATT + 1], 0.125)
            g.memset(B.kc[:, KC_ONE:KC_ONE + 1], 1.0)
    B.bar()


def _flat(ap):
    nd = len(ap.shape)
    if nd == 1:
        return ap
    if nd == 2:
        return ap.rearrange("a b -> (a b)")
    if nd == 3:
        return ap.rearrange("a b c -> (a b c)")
    raise ValueError(nd)


def emit_cast_dram(B, dst, src):
    nc = B.nc
    n = 1
    for d in src.shape:
        n *= d
    sf = _flat(src)
    df = _flat(dst)
    with nc.Block() as b:
        @b.gpsimd
        def _(g):
            off = 0
            while off < n:
                m = min(2 * 1024 * 1024, n - off)
                B.dg.inc(g.dma_start(
                    df[off:off + m].rearrange("(a b) -> a b", a=1),
                    sf[off:off + m].rearrange("(a b) -> a b", a=1)), 16)
                off += m


def strided_ap(base_ap, extra_off, dims):
    return bass.AP(base_ap.tensor, base_ap.offset + extra_off, dims)


def emit_ln(B, src, dst, stats, scr, pre_sync=None, src_wait=False):
    nc = B.nc
    if not isinstance(src, bass.AP):
        src = src[:, :]
    if not isinstance(dst, bass.AP):
        dst = dst[:, :]
    if not isinstance(scr, bass.AP):
        scr = scr[:, :]
    D = src.shape[-1] if hasattr(src, "shape") else DM
    rinv = 1.0 / D
    ev0, es0 = B.ev.v, B.es.v
    with nc.Block() as b:
        if pre_sync is not None:
            @b.sync
            def _(sy):
                pre_sync(sy)
        src_rdy = B.ds.v if src_wait else None

        @b.vector
        def _(v):
            if src_rdy is not None:
                v.wait_ge(B.ds.sem, src_rdy)
            B.ev.inc(v.reduce_sum(stats[:, 0:1], src, axis=AX.X, negate=True))
            v.wait_ge(B.es.sem, es0 + 1)
            B.ev.inc(v.tensor_scalar(stats[:, 0:1], stats[:, 0:1], rinv, None,
                                     ALU.mult))
            v.wait_ge(B.ev.sem, ev0 + 2)
            B.ev.inc(v.tensor_mul(stats[:, 2:3], stats[:, 0:1], stats[:, 0:1]))
            B.ev.inc(v.tensor_scalar(stats[:, 1:2], stats[:, 1:2], rinv, None,
                                     ALU.mult))
            v.wait_ge(B.ev.sem, ev0 + 4)
            B.ev.inc(v.tensor_sub(stats[:, 1:2], stats[:, 1:2], stats[:, 2:3]))
            v.wait_ge(B.es.sem, es0 + 3)
            B.ev.inc(v.tensor_scalar(dst, src, stats[:, 0:1], stats[:, 3:4],
                                     ALU.add, ALU.mult))

        @b.scalar
        def _(s):
            if src_rdy is not None:
                s.wait_ge(B.ds.sem, src_rdy)
            B.es.inc(s.activation(scr, src, AF.Square,
                                  accum_out=stats[:, 1:2]))
            s.wait_ge(B.ev.sem, ev0 + 5)
            B.es.inc(s.activation(stats[:, 2:3], stats[:, 1:2], AF.Ln,
                                  bias=B.kc[:, KC_LNEPS:KC_LNEPS + 1]))
            s.wait_ge(B.es.sem, es0 + 2)
            B.es.inc(s.activation(stats[:, 3:4], stats[:, 2:3], AF.Exp,
                                  scale=B.kc[:, KC_MHALF:KC_MHALF + 1]))
    return B.ev.v


def emit_load_ln_t(B, x_in, r0, ntt, xT, ts_free, xf, xhat, stats, scr, pbt):
    nc = B.nc
    for tt in range(ntt):
        def _pre(sy, _tt=tt):
            B.ds.inc(sy.dma_start(
                xf[:, :], x_in[r0 + _tt * 128:r0 + _tt * 128 + 128, :]), 16)
        emit_ln(B, xf[:, :], xhat[:, :], stats, scr, pre_sync=_pre,
                src_wait=True)
        with nc.Block() as b:
            @b.tensor
            def _(t, _tt=tt):
                t.wait_ge(B.ev.sem, B.ev.v)
                for c in range(8):
                    inst = t.transpose(pbt[:, c * 128:(c + 1) * 128],
                                       xhat[:, c * 128:(c + 1) * 128],
                                       B.identb[:, :])
                B.et.inc(inst)

            @b.scalar
            def _(s, _tt=tt):
                s.wait_ge(B.et.sem, B.et.v)
                dst = strided_ap(xT[:, :], _tt * 128,
                                 [[xT.shape[1], 128], [ts_free, 8], [1, 128]])
                B.es.inc(s.copy(dst, pbt[:, :]))
        B.bar()


def emit_epilogue_tile(B, ysb, x_in, x_out, row0, xf, osb, stats, scr):
    nc = B.nc

    def _pre(sy):
        B.ds.inc(sy.dma_start(xf[:, :], x_in[row0:row0 + 128, :]), 16)
    emit_ln(B, ysb[:, :], scr[:, :], stats, osb[:, :], pre_sync=_pre)
    with nc.Block() as b:
        @b.vector
        def _(v):
            v.wait_ge(B.ds.sem, B.ds.v)
            v.wait_ge(B.ev.sem, B.ev.v)
            B.ev.inc(v.tensor_add(osb[:, :], scr[:, :], xf[:, :]))

        @b.sync
        def _(sy):
            sy.wait_ge(B.ev.sem, B.ev.v)
            B.ds.inc(sy.dma_start(x_out[row0:row0 + 128, :], osb[:, :]), 16)
    B.bar()



def emit_l1_ffn(B, x_in, x_out, W):
    nc = B.nc
    DH = 4096
    TS = 512
    NSW = S // TS
    NTT = TS // 128

    w1b = B.dram("l1fw1b", [DM, DH], BF16)
    w2b = B.dram("l1fw2b", [DM, DH], BF16)
    h1b = B.dram("l1fh1b", [DH, DH], BF16)
    h2b = B.dram("l1fh2b", [DH, DH], BF16)
    w3b = B.dram("l1fw3b", [DH, DM], BF16)
    emit_cast_dram(B, w1b[:, :], W["l1f_w1T"])
    emit_cast_dram(B, w2b[:, :], W["l1f_w2T"])
    emit_cast_dram(B, h1b[:, :], W["l1f_h1T"])
    emit_cast_dram(B, h2b[:, :], W["l1f_h2T"])
    emit_cast_dram(B, w3b[:, :], W["l1f_w3T"])
    B.dma_fence()

    with contextlib.ExitStack() as st:
        BB = Bld.__new__(Bld)
        BB.__dict__.update(B.__dict__)
        BB.st = st
        sb = BB.sbuf
        s1c = sb("s1c", [128, 32], F32)
        s2c = sb("s2c", [128, 32], F32)
        sh1c = sb("sh1c", [128, 32], F32)
        sh2c = sb("sh2c", [128, 32], F32)
        s3b = sb("s3b", [128, DM], F32)
        s3row = sb("s3row", [1, DM], F32)
        with nc.Block() as b:
            @b.sync
            def _(sy):
                B.ds.inc(sy.dma_start(s1c[:, :], W["l1f_s1"]), 16)
                B.ds.inc(sy.dma_start(s2c[:, :], W["l1f_s2"]), 16)
                B.ds.inc(sy.dma_start(sh1c[:, :], W["l1f_sh1"]), 16)
                B.ds.inc(sy.dma_start(sh2c[:, :], W["l1f_sh2"]), 16)
                B.ds.inc(sy.dma_start(
                    s3row[0:1, :],
                    W["l1f_s3"].rearrange("(a b) -> a b", a=1)), 16)
        ps3 = BB.psum("ps3", [128, 512], F32)
        for half in range(2):
            with nc.Block() as b:
                @b.tensor
                def _(t, _h=half):
                    t.wait_ge(B.ds.sem, B.ds.v)
                    B.et.inc(t.matmul(ps3[:, :], B.onesf[0:1, :],
                                      s3row[0:1, _h * 512:(_h + 1) * 512],
                                      start=True, stop=True))

                @b.vector
                def _(v, _h=half):
                    v.wait_ge(B.et.sem, B.et.v)
                    B.ev.inc(v.tensor_copy(s3b[:, _h * 512:(_h + 1) * 512],
                                           ps3[:, :]))
        B.bar()

        xT = sb("xT", [128, 8 * TS], BF16)
        hT = sb("hT", [128, 32 * TS], BF16)
        g1T = sb("g1T", [128, 32 * TS], BF16)
        wsa = [sb("wsa", [128, 8 * 512], BF16) for _ in range(2)]
        wsb_ = [sb("wsb", [128, 8 * 512], BF16) for _ in range(2)]
        wh2 = [sb("wh2", [128, 32 * 256], BF16) for _ in range(2)]
        us = sb("us", [128, 2 * TS], F32)
        ysb = sb("ysb", [128, NTT * DM], F32)
        xf = sb("xf", [128, DM], F32)
        xhat = sb("xhat", [128, DM], BF16)
        stats = sb("stats", [128, 4], F32)
        scr = sb("scr", [128, DM], F32)
        osb = sb("osb", [128, DM], F32)
        pbt = BB.psum("pbt", [128, DM], BF16)
        py1 = [BB.psum("py1", [128, TS], F32) for _ in range(2)]
        py2 = [BB.psum("py2", [128, TS], F32) for _ in range(2)]

        for sw in range(NSW):
            r0 = sw * TS
            emit_load_ln_t(B, x_in, r0, NTT, xT, TS, xf, xhat, stats, scr, pbt)

            et_y = B.et.v
            es_y = B.es.v
            ev_y = B.ev.v
            ds0_y = B.ds.v
            ds_y = [ds0_y + 32 * (og + 1) for og in range(8)]
            with nc.Block() as b:
                @b.sync
                def _(sy):
                    for og in range(8):
                        if og >= 2:
                            sy.wait_ge(B.et.sem, et_y + (og - 1) * 8)
                        for (dst, wsrc) in ((wsa[og % 2], w1b),
                                            (wsb_[og % 2], w2b)):
                            src = wsrc[:, og * 512:(og + 1) * 512] \
                                .rearrange("(c p) n -> p c n", p=128)
                            B.ds.inc(sy.dma_start(
                                dst[:, :].rearrange("p (c n) -> p c n", c=8),
                                src), 16)

                @b.tensor
                def _(t):
                    for og in range(8):
                        t.wait_ge(B.ds.sem, ds_y[og])
                        for j in range(4):
                            jj = og * 4 + j
                            if jj >= 2:
                                t.wait_ge(B.es.sem, es_y + jj - 1)
                            for ic in range(8):
                                i1 = t.matmul(py1[jj % 2][:, :],
                                              wsa[og % 2][:, ic * 512 + j * 128:
                                                  ic * 512 + j * 128 + 128],
                                              xT[:, ic * TS:(ic + 1) * TS],
                                              start=(ic == 0), stop=(ic == 7))
                            B.et.inc(i1)
                            if jj >= 2:
                                t.wait_ge(B.ev.sem, ev_y + jj - 1)
                            for ic in range(8):
                                i2 = t.matmul(py2[jj % 2][:, :],
                                              wsb_[og % 2][:, ic * 512 + j * 128:
                                                   ic * 512 + j * 128 + 128],
                                              xT[:, ic * TS:(ic + 1) * TS],
                                              start=(ic == 0), stop=(ic == 7))
                            B.et.inc(i2)

                @b.scalar
                def _(s):
                    for jj in range(32):
                        s.wait_ge(B.et.sem, et_y + 2 * jj + 1)
                        if jj >= 2:
                            s.wait_ge(B.ev.sem, ev_y + jj - 1)
                        B.es.inc(s.activation(us[:, jj % 2 * TS:
                                                 (jj % 2 + 1) * TS],
                                              py1[jj % 2][:, :], AF.Silu,
                                              scale=s1c[:, jj:jj + 1]))

                @b.vector
                def _(v):
                    for jj in range(32):
                        v.wait_ge(B.et.sem, et_y + 2 * jj + 2)
                        v.wait_ge(B.es.sem, es_y + jj + 1)
                        B.ev.inc(v.scalar_tensor_tensor(
                            hT[:, jj * TS:(jj + 1) * TS],
                            py2[jj % 2][:, :], s2c[:, jj:jj + 1],
                            us[:, jj % 2 * TS:(jj % 2 + 1) * TS],
                            ALU.mult, ALU.mult))

            for (wdram, scol, srcT, dstT) in ((h1b, sh1c, hT, g1T),
                                              (h2b, sh2c, g1T, hT)):
                et_h = B.et.v
                es_h = B.es.v
                ds0_h = B.ds.v
                ds_h = [ds0_h + 32 * (sg + 1) for sg in range(16)]
                with nc.Block() as b:
                    @b.sync
                    def _(sy, _w=wdram):
                        for sg in range(16):
                            if sg >= 2:
                                sy.wait_ge(B.et.sem, et_h + (sg - 1) * 2)
                            src = _w[:, sg * 256:(sg + 1) * 256] \
                                .rearrange("(c p) n -> p c n", p=128)
                            dst = wh2[sg % 2][:, :].rearrange(
                                "p (c n) -> p c n", c=32)
                            for hlf in range(2):
                                B.ds.inc(sy.dma_start(
                                    dst[:, hlf * 16:(hlf + 1) * 16, :],
                                    src[:, hlf * 16:(hlf + 1) * 16, :]), 16)

                    @b.tensor
                    def _(t, _src=srcT):
                        for sg in range(16):
                            t.wait_ge(B.ds.sem, ds_h[sg])
                            for j in range(2):
                                jj = sg * 2 + j
                                if jj >= 2:
                                    t.wait_ge(B.es.sem, es_h + jj - 1)
                                for hc in range(32):
                                    i1 = t.matmul(
                                        py1[jj % 2][:, :],
                                        wh2[sg % 2][:, hc * 256 + j * 128:
                                            hc * 256 + j * 128 + 128],
                                        _src[:, hc * TS:(hc + 1) * TS],
                                        start=(hc == 0), stop=(hc == 31))
                                B.et.inc(i1)

                    @b.scalar
                    def _(s, _sc=scol, _dst=dstT):
                        for jj in range(32):
                            s.wait_ge(B.et.sem, et_h + jj + 1)
                            B.es.inc(s.activation(
                                _dst[:, jj * TS:(jj + 1) * TS],
                                py1[jj % 2][:, :], AF.Gelu,
                                scale=_sc[:, jj:jj + 1]))

            et_w = B.et.v
            ev_w = B.ev.v
            ds0_w = B.ds.v
            ds_w = [ds0_w + 32 * (dg + 1) for dg in range(4)]
            with nc.Block() as b:
                @b.sync
                def _(sy):
                    for dg in range(4):
                        if dg >= 2:
                            sy.wait_ge(B.et.sem, et_w + (dg - 1) * NTT)
                        src = w3b[:, dg * 256:(dg + 1) * 256] \
                            .rearrange("(c p) n -> p c n", p=128)
                        dst = wh2[dg % 2][:, :].rearrange(
                            "p (c n) -> p c n", c=32)
                        for hlf in range(2):
                            B.ds.inc(sy.dma_start(
                                dst[:, hlf * 16:(hlf + 1) * 16, :],
                                src[:, hlf * 16:(hlf + 1) * 16, :]), 16)

                @b.tensor
                def _(t):
                    for dg in range(4):
                        t.wait_ge(B.ds.sem, ds_w[dg])
                        for tt in range(NTT):
                            jj = dg * NTT + tt
                            if jj >= 2:
                                t.wait_ge(B.ev.sem, ev_w + jj - 1)
                            for hc in range(32):
                                i1 = t.matmul(
                                    py1[jj % 2][:, 0:256],
                                    hT[:, hc * TS + tt * 128:
                                       hc * TS + tt * 128 + 128],
                                    wh2[dg % 2][:, hc * 256:(hc + 1) * 256],
                                    start=(hc == 0), stop=(hc == 31))
                            B.et.inc(i1)

                @b.vector
                def _(v):
                    for dg in range(4):
                        for tt in range(NTT):
                            jj = dg * NTT + tt
                            v.wait_ge(B.et.sem, et_w + jj + 1)
                            B.ev.inc(v.tensor_mul(
                                ysb[:, tt * DM + dg * 256:
                                    tt * DM + dg * 256 + 256],
                                py1[jj % 2][:, 0:256],
                                s3b[:, dg * 256:(dg + 1) * 256]))

            for tt in range(NTT):
                emit_epilogue_tile(B, ysb[:, tt * DM:(tt + 1) * DM],
                                   x_in, x_out, r0 + tt * 128,
                                   xf, osb, stats, scr)



def emit_l0_ffn(B, x_in, x_out, W):
    nc = B.nc
    E, H = 8, 2048
    TS = 512
    NSW = S // TS
    NTT = TS // 128

    w1b = B.dram("l0fw1b", [E, DM, H], BF16)
    w2b = B.dram("l0fw2b", [E, H, DM], BF16)
    emit_cast_dram(B, w1b[:, :, :], W["l0f_w1"])
    emit_cast_dram(B, w2b[:, :, :], W["l0f_w2"])

    with contextlib.ExitStack() as st:
        BB = Bld.__new__(Bld)
        BB.__dict__.update(B.__dict__)
        BB.st = st
        sb = BB.sbuf

        gwT = sb("gwT", [128, 64], BF16)
        gbb = sb("gbb", [128, 8], F32)
        rwb = sb("rwb", [128, 8], F32)
        b2b = sb("b2b", [128, DM], F32)
        b1c = sb("b1c", [128, 128], F32)
        row8 = sb("row8", [1, 16], F32)
        b2sb = sb("b2sb", [8, DM], F32)
        b2row = sb("b2row", [1, DM], F32)
        psx = BB.psum("psx", [128, 512], F32)

        with nc.Block() as b:
            @b.gpsimd
            def _(g):
                for ic in range(8):
                    B.dg.inc(g.dma_start(
                        gwT[:, ic * 8:(ic + 1) * 8],
                        W["l0f_gwT"][ic * 128:(ic + 1) * 128, :]), 16)

            @b.sync
            def _(sy):
                B.ds.inc(sy.dma_start(
                    row8[0:1, 0:8],
                    W["l0f_gate_b"].rearrange("(a b) -> a b", a=1)), 16)
                B.ds.inc(sy.dma_start(
                    row8[0:1, 8:16],
                    W["l0f_rms_w"].rearrange("(a b) -> a b", a=1)), 16)
                B.ds.inc(sy.dma_start(b2sb[:, :], W["l0f_b2"]), 16)
                B.ds.inc(sy.dma_start(
                    b1c[:, :].rearrange("p (e c) -> p e c", e=8),
                    W["l0f_b1c"].rearrange("e p c -> p e c")), 16)
        with nc.Block() as b:
            @b.tensor
            def _(t):
                t.wait_ge(B.ds.sem, B.ds.v)
                B.et.inc(t.matmul(psx[:, 0:8], B.onesf[0:1, :],
                                  row8[0:1, 0:8], start=True, stop=True))
                B.et.inc(t.matmul(psx[:, 8:16], B.onesf[0:1, :],
                                  row8[0:1, 8:16], start=True, stop=True))

            @b.vector
            def _(v):
                v.wait_ge(B.et.sem, B.et.v)
                B.ev.inc(v.tensor_copy(gbb[:, :], psx[:, 0:8]))
                B.ev.inc(v.tensor_copy(rwb[:, :], psx[:, 8:16]))
        with nc.Block() as b:
            @b.tensor
            def _(t):
                t.wait_ge(B.ev.sem, B.ev.v)
                for hf in range(2):
                    B.et.inc(t.matmul(psx[0:1, 0:512], B.onescf[0:8, 0:1],
                                      b2sb[:, hf * 512:(hf + 1) * 512],
                                      start=True, stop=True))
                    t.wait_ge(B.ev.sem, B.ev.v + hf + 1)

            @b.vector
            def _(v):
                for hf in range(2):
                    v.wait_ge(B.et.sem, B.et.v - 1 + hf)
                    B.ev.inc(v.tensor_copy(
                        b2row[0:1, hf * 512:(hf + 1) * 512], psx[0:1, 0:512]))
        B.bar()
        with nc.Block() as b:
            @b.tensor
            def _(t):
                for hf in range(2):
                    B.et.inc(t.matmul(psx[:, 0:512], B.onesf[0:1, :],
                                      b2row[0:1, hf * 512:(hf + 1) * 512],
                                      start=True, stop=True))
                    t.wait_ge(B.ev.sem, B.ev.v + hf + 1)

            @b.vector
            def _(v):
                for hf in range(2):
                    v.wait_ge(B.et.sem, B.et.v - 1 + hf)
                    B.ev.inc(v.tensor_copy(
                        b2b[:, hf * 512:(hf + 1) * 512], psx[:, 0:512]))
        B.bar()

        xT = sb("xT", [128, 8 * TS], BF16)
        hT = sb("hT", [128, 16 * TS], BF16)
        yacc = sb("yacc", [128, NTT * DM], F32)
        w1sb = [sb("w1sb", [128, 8 * H], BF16) for _ in range(2)]
        w2sb = sb("w2sb", [128, 16 * DM], BF16)
        ngc = sb("ngc", [128, NTT], F32)
        lg = sb("lg", [128, 8], F32)
        scr8 = sb("scr8", [128, 8], F32)
        gst = sb("gst", [128, 4], F32)
        xf = sb("xf", [128, DM], F32)
        xhat = sb("xhat", [128, DM], BF16)
        stats = sb("stats", [128, 4], F32)
        scr = sb("scr", [128, DM], F32)
        ysb = sb("ysb", [128, DM], F32)
        osb = sb("osb", [128, DM], F32)
        pbt = BB.psum("pbt", [128, DM], BF16)
        psg = BB.psum("psg", [128, 8], F32)
        ph = [BB.psum("ph", [128, 512], F32) for _ in range(2)]
        py = [BB.psum("py", [128, 512], F32) for _ in range(2)]

        for sw in range(NSW):
            r0 = sw * TS
            emit_load_ln_t(B, x_in, r0, NTT, xT, TS, xf, xhat, stats, scr, pbt)

            for tt in range(NTT):
                with nc.Block() as b:
                    @b.tensor
                    def _(t, _tt=tt):
                        for ic in range(8):
                            i1 = t.matmul(psg[:, :],
                                          xT[:, ic * TS + _tt * 128:
                                             ic * TS + _tt * 128 + 128],
                                          gwT[:, ic * 8:(ic + 1) * 8],
                                          start=(ic == 0), stop=(ic == 7))
                        B.et.inc(i1)

                    @b.vector
                    def _(v):
                        v.wait_ge(B.et.sem, B.et.v)
                        B.ev.inc(v.tensor_add(lg[:, :], gbb[:, :], psg[:, :]))

                    @b.scalar
                    def _(s):
                        s.wait_ge(B.ev.sem, B.ev.v)
                        B.es.inc(s.activation(scr8[:, :], lg[:, :], AF.Square,
                                              accum_out=gst[:, 0:1]))
                        s.wait_ge(B.es.sem, B.es.v)
                        B.es.inc(s.activation(
                            gst[:, 1:2], gst[:, 0:1], AF.Ln,
                            bias=B.kc[:, KC_RMSEPS:KC_RMSEPS + 1],
                            scale=B.kc[:, KC_ATT:KC_ATT + 1]))
                        s.wait_ge(B.es.sem, B.es.v)
                        B.es.inc(s.activation(
                            gst[:, 2:3], gst[:, 1:2], AF.Exp,
                            scale=B.kc[:, KC_MHALF:KC_MHALF + 1]))

                    @b.vector
                    def _(v):
                        v.wait_ge(B.es.sem, B.es.v)
                        B.ev.inc(v.tensor_scalar(lg[:, :], lg[:, :],
                                                 gst[:, 2:3], None, ALU.mult))
                        v.wait_ge(B.ev.sem, B.ev.v)
                        B.ev.inc(v.tensor_mul(lg[:, :], lg[:, :], rwb[:, :]))
                        v.wait_ge(B.ev.sem, B.ev.v)
                        B.ev.inc(v.reduce_max(gst[:, 0:1], lg[:, :],
                                              axis=AX.X, negate=True))
                        v.wait_ge(B.ev.sem, B.ev.v)
                        B.ev.inc(v.tensor_scalar(gst[:, 1:2], gst[:, 0:1],
                                                 2.0, None, ALU.mult))

                    @b.scalar
                    def _(s):
                        s.wait_ge(B.ev.sem, B.ev.v)
                        B.es.inc(s.activation(scr8[:, :], lg[:, :], AF.Exp,
                                              bias=gst[:, 1:2],
                                              scale=B.kc[:, KC_TWO:KC_TWO + 1],
                                              accum_out=gst[:, 2:3]))

                    @b.vector
                    def _(v, _tt=tt):
                        v.wait_ge(B.es.sem, B.es.v)
                        B.ev.inc(v.tensor_scalar(gst[:, 3:4], gst[:, 2:3],
                                                 1e-6, 1.0, ALU.mult, ALU.add))
                        v.wait_ge(B.ev.sem, B.ev.v)
                        B.ev.inc(v.reciprocal(ngc[:, _tt:_tt + 1],
                                              gst[:, 3:4]))
                B.bar()

            ds_w1 = [0] * 8
            ds_w2 = [0] * 8
            def w1_load(sy, e):
                src = w1b[e, :, :].rearrange("(c p) n -> p c n", p=128)
                dst = w1sb[e % 2][:, :].rearrange("p (c n) -> p c n", c=8)
                for hf in range(2):
                    B.ds.inc(sy.dma_start(dst[:, hf * 4:(hf + 1) * 4, :],
                                          src[:, hf * 4:(hf + 1) * 4, :]), 16)
                ds_w1[e] = B.ds.v

            def w2_load(sy, e):
                src = w2b[e, :, :].rearrange("(c p) n -> p c n", p=128)
                dst = w2sb[:, :].rearrange("p (c n) -> p c n", c=16)
                for hf in range(2):
                    B.ds.inc(sy.dma_start(dst[:, hf * 8:(hf + 1) * 8, :],
                                          src[:, hf * 8:(hf + 1) * 8, :]), 16)
                ds_w2[e] = B.ds.v

            with nc.Block() as b:
                @b.sync
                def _(sy):
                    w1_load(sy, 0)
            for e in range(8):
                with nc.Block() as b:
                    @b.sync
                    def _(sy, _e=e):
                        w2_load(sy, _e)
                        if _e < 7:
                            w1_load(sy, _e + 1)
                et0, es0 = B.et.v, B.es.v
                wb = w1sb[e % 2]
                with nc.Block() as b:
                    @b.tensor
                    def _(t, _e=e, _wb=wb):
                        t.wait_ge(B.ds.sem, ds_w1[_e])
                        for hc in range(16):
                            if hc >= 2:
                                t.wait_ge(B.es.sem, es0 + hc - 1)
                            for ic in range(8):
                                i1 = t.matmul(
                                    ph[hc % 2][:, :],
                                    _wb[:, ic * H + hc * 128:
                                        ic * H + hc * 128 + 128],
                                    xT[:, ic * TS:ic * TS + TS],
                                    start=(ic == 0), stop=(ic == 7))
                            B.et.inc(i1)

                    @b.scalar
                    def _(s, _e=e):
                        for hc in range(16):
                            s.wait_ge(B.et.sem, et0 + hc + 1)
                            B.es.inc(s.activation(
                                hT[:, hc * TS:(hc + 1) * TS],
                                ph[hc % 2][:, :], AF.Gelu,
                                bias=b1c[:, _e * 16 + hc:_e * 16 + hc + 1]))
                B.bar()
                et0, ev0 = B.et.v, B.ev.v
                with nc.Block() as b:
                    @b.tensor
                    def _(t, _e=e):
                        t.wait_ge(B.ds.sem, ds_w2[_e])
                        for j in range(8):
                            tt, ds_ = j // 2, j % 2
                            if j >= 2:
                                t.wait_ge(B.ev.sem, ev0 + j - 1)
                            for hc in range(16):
                                i1 = t.matmul(
                                    py[j % 2][:, :],
                                    hT[:, hc * TS + tt * 128:
                                       hc * TS + tt * 128 + 128],
                                    w2sb[:, hc * DM + ds_ * 512:
                                         hc * DM + ds_ * 512 + 512],
                                    start=(hc == 0), stop=(hc == 15))
                            B.et.inc(i1)

                    @b.vector
                    def _(v, _e=e):
                        for j in range(8):
                            tt, ds_ = j // 2, j % 2
                            v.wait_ge(B.et.sem, et0 + j + 1)
                            dst = yacc[:, tt * DM + ds_ * 512:
                                       tt * DM + ds_ * 512 + 512]
                            if _e == 0:
                                B.ev.inc(v.tensor_copy(dst, py[j % 2][:, :]))
                            else:
                                B.ev.inc(v.tensor_add(dst, dst,
                                                      py[j % 2][:, :]))
                B.bar()

            for tt in range(NTT):
                with nc.Block() as b:
                    @b.vector
                    def _(v, _tt=tt):
                        B.ev.inc(v.tensor_add(
                            ysb[:, :], yacc[:, _tt * DM:(_tt + 1) * DM],
                            b2b[:, :]))
                        v.wait_ge(B.ev.sem, B.ev.v)
                        B.ev.inc(v.tensor_scalar(ysb[:, :], ysb[:, :],
                                                 ngc[:, _tt:_tt + 1], None,
                                                 ALU.mult))

                    @b.scalar
                    def _(s):
                        s.wait_ge(B.ev.sem, B.ev.v)
                emit_epilogue_tile(B, ysb, x_in, x_out, r0 + tt * 128,
                                   xf, osb, stats, scr)



def emit_l0_attn(B, x_in, x_out, W):
    nc = B.nc
    NH, G, HD = 16, 4, 64

    with contextlib.ExitStack() as st:
        BB = Bld.__new__(Bld)
        BB.__dict__.update(B.__dict__)
        BB.st = st
        sb = BB.sbuf

        qTp = sb("qTp", [128, 8 * S], BF16)
        kTp = sb("kTp", [128, 4 * S], BF16)
        vsb = sb("vsb", [128, 32 * 256], BF16)
        band_sb = sb("band", [128, 640], F32)
        ohb = sb("ohb", [128, 32 * 4], BF16)
        rmc = sb("rmc", [128, 32], F32)
        gpb = sb("gpb", [128, 4], F32)
        dupaddb = sb("dupaddb", [128, 4], F32)
        keepbf = sb("keepbf", [128, 4], F32)
        gpi = sb("gpi", [1, 4], I32)
        gpf = sb("gpf", [1, 4], F32)
        dup01 = sb("dup01", [1, 12], F32)
        brow = sb("brow", [1, 2560], F32)
        bqb = sb("bqb", [128, DM], F32)
        bkb = sb("bkb", [128, 256], F32)
        bvb = sb("bvb", [128, 256], F32)
        bob = sb("bob", [128, DM], F32)
        qgT = sb("qgT", [64, 64], BF16)
        kgT = sb("kgT", [128, 16], BF16)
        vgg = sb("vgg", [4, 256], BF16)
        ogsb = sb("ogsb", [4, DM], F32)
        ogbf = sb("ogbf", [4, DM], BF16)

        with nc.Block() as b:
            @b.sync
            def _(sy):
                B.ds.inc(sy.dma_start(band_sb[:, :], W["band_tbl"]), 16)
                B.ds.inc(sy.dma_start(
                    gpi[0:1, :], W["gp"].rearrange("(a b) -> a b", a=1)), 16)
                off = 0
                for nm, ln_ in (("l0a_bq", DM), ("l0a_bk", 256),
                                ("l0a_bv", 256), ("l0a_bo", DM)):
                    B.ds.inc(sy.dma_start(
                        brow[0:1, off:off + ln_],
                        W[nm].rearrange("(a b) -> a b", a=1)), 16)
                    off += ln_

            @b.scalar
            def _(s):
                s.wait_ge(B.ds.sem, B.ds.v)
                B.es.inc(s.copy(gpf[0:1, :], gpi[0:1, :]))

            @b.vector
            def _(v):
                v.wait_ge(B.es.sem, B.es.v)
                B.ev.inc(v.memset(dup01[0:1, :], 0.0))
                for k in range(1, 4):
                    for k2 in range(k):
                        v.wait_ge(B.ev.sem, B.ev.v)
                        B.ev.inc(v.tensor_tensor(
                            dup01[0:1, 4 + k:5 + k], gpf[0:1, k:k + 1],
                            gpf[0:1, k2:k2 + 1], ALU.is_equal))
                        v.wait_ge(B.ev.sem, B.ev.v)
                        B.ev.inc(v.tensor_max(dup01[0:1, k:k + 1],
                                              dup01[0:1, k:k + 1],
                                              dup01[0:1, 4 + k:5 + k]))
                v.wait_ge(B.ev.sem, B.ev.v)
                B.ev.inc(v.tensor_scalar(dup01[0:1, 8:12], dup01[0:1, 0:4],
                                         -1.0, 1.0, ALU.mult, ALU.add))
                B.ev.inc(v.tensor_scalar(dup01[0:1, 0:4], dup01[0:1, 0:4],
                                         NEG, None, ALU.mult))
        with contextlib.ExitStack() as pst:
            psx = pst.enter_context(
                nc.psum_tensor(B.name("psx"), [128, 512], F32))
            bc_jobs = [(gpb[:, :], gpf[0:1, :]),
                       (dupaddb[:, :], dup01[0:1, 0:4]),
                       (keepbf[:, :], dup01[0:1, 8:12]),
                       (bkb[:, :], brow[0:1, DM:DM + 256]),
                       (bvb[:, :], brow[0:1, DM + 256:DM + 512]),
                       (bqb[:, 0:512], brow[0:1, 0:512]),
                       (bqb[:, 512:DM], brow[0:1, 512:DM]),
                       (bob[:, 0:512], brow[0:1, DM + 512:DM + 1024]),
                       (bob[:, 512:DM], brow[0:1, DM + 1024:DM + 1536])]
            for ji, (dst, srow) in enumerate(bc_jobs):
                with nc.Block() as b:
                    @b.tensor
                    def _(t, _d=dst, _s=srow):
                        t.wait_ge(B.ev.sem, B.ev.v)
                        B.et.inc(t.matmul(psx[:, 0:_s.shape[-1]],
                                          B.onesf[0:1, :], _s,
                                          start=True, stop=True))
                        t.wait_ge(B.ev.sem, B.ev.v + 1)

                    @b.vector
                    def _(v, _d=dst):
                        v.wait_ge(B.et.sem, B.et.v)
                        B.ev.inc(v.tensor_copy(_d, psx[:, 0:_d.shape[-1]]))
        B.bar()
        with nc.Block() as b:
            @b.vector
            def _(v):
                for b32 in range(32):
                    B.ev.inc(v.tensor_scalar(rmc[:, 0:4], gpb[:, :],
                                             B.iotac[:, 0:1],
                                             float(128 * b32),
                                             ALU.subtract, ALU.subtract))
                    v.wait_ge(B.ev.sem, B.ev.v)
                    B.ev.inc(v.tensor_scalar(rmc[:, 0:4], rmc[:, 0:4], 0.0,
                                             None, ALU.is_equal))
                    v.wait_ge(B.ev.sem, B.ev.v)
                    B.ev.inc(v.tensor_mul(ohb[:, b32 * 4:(b32 + 1) * 4],
                                          rmc[:, 0:4], keepbf[:, :]))
                    v.wait_ge(B.ev.sem, B.ev.v)
        with nc.Block() as b:
            @b.vector
            def _(v):
                for b32 in range(32):
                    B.ev.inc(v.reduce_sum(rmc[:, b32:b32 + 1],
                                          ohb[:, b32 * 4:(b32 + 1) * 4],
                                          axis=AX.X))
                v.wait_ge(B.ev.sem, B.ev.v)
                B.ev.inc(v.tensor_scalar(rmc[:, :], rmc[:, :], -1.0, 1.0,
                                         ALU.mult, ALU.add))
        B.bar()

        with contextlib.ExitStack() as ast:
            asb = lambda nm, sh, dt: ast.enter_context(
                nc.sbuf_tensor(B.name(nm), sh, dt))
            wqsb = asb("wqsb", [128, 8 * DM], BF16)
            wksb = asb("wksb", [128, 8 * 256], BF16)
            wvsb = asb("wvsb", [128, 8 * 256], BF16)
            xf = asb("xf", [128, DM], F32)
            xhat = asb("xhat", [128, DM], BF16)
            xTt = asb("xTt", [128, DM], BF16)
            stats = asb("stats", [128, 4], F32)
            scr = asb("scr", [128, DM], F32)
            sint = asb("sint", [128, DM], F32)
            cost = asb("cost", [128, DM], F32)
            qf = asb("qf", [128, DM], F32)
            qrot = asb("qrot", [128, DM], F32)
            qhat = asb("qhat", [128, DM], BF16)
            khat = asb("khat", [128, 256], BF16)
            pbt = ast.enter_context(
                nc.psum_tensor(B.name("pbt"), [128, DM], BF16))
            pq = [ast.enter_context(
                nc.psum_tensor(B.name("pq"), [128, 512], F32))
                for _ in range(2)]
            psqg = ast.enter_context(
                nc.psum_tensor(B.name("psqg"), [64, 64], F32))
            pskg = ast.enter_context(
                nc.psum_tensor(B.name("pskg"), [128, 16], F32))
            psvg = ast.enter_context(
                nc.psum_tensor(B.name("psvg"), [4, 256], F32))

            with nc.Block() as b:
                @b.gpsimd
                def _(g):
                    for ic in range(8):
                        B.dg.inc(g.dma_start(
                            wqsb[:, ic * DM:(ic + 1) * DM],
                            W["l0a_Wq"][ic * 128:(ic + 1) * 128, :]), 16)
                        B.dg.inc(g.dma_start(
                            wksb[:, ic * 256:(ic + 1) * 256],
                            W["l0a_Wk"][ic * 128:(ic + 1) * 128, :]), 16)
                        B.dg.inc(g.dma_start(
                            wvsb[:, ic * 256:(ic + 1) * 256],
                            W["l0a_Wv"][ic * 128:(ic + 1) * 128, :]), 16)

            for rt in range(32):
                r0 = rt * 128
                with nc.Block() as b:
                    @b.sync
                    def _(sy, _r0=r0):
                        B.ds.inc(sy.dma_start(xf[:, :],
                                              x_in[_r0:_r0 + 128, :]), 16)
                        B.ds.inc(sy.dma_start(
                            sint[:, :], W["sin_rep"][_r0:_r0 + 128, :]), 16)
                        B.ds.inc(sy.dma_start(
                            cost[:, :], W["cos_rep"][_r0:_r0 + 128, :]), 16)

                    @b.vector
                    def _(v):
                        v.wait_ge(B.ds.sem, B.ds.v)

                    @b.scalar
                    def _(s):
                        s.wait_ge(B.ds.sem, B.ds.v)
                emit_ln(B, xf, xhat, stats, scr)
                with nc.Block() as b:
                    @b.tensor
                    def _(t):
                        t.wait_ge(B.ev.sem, B.ev.v)
                        for c in range(8):
                            i1 = t.transpose(pbt[:, c * 128:(c + 1) * 128],
                                             xhat[:, c * 128:(c + 1) * 128],
                                             B.identb[:, :])
                        B.et.inc(i1)

                    @b.scalar
                    def _(s):
                        s.wait_ge(B.et.sem, B.et.v)
                        B.es.inc(s.copy(xTt[:, :], pbt[:, :]))
                B.bar()
                with nc.Block() as b:
                    @b.tensor
                    def _(t, _rt=rt):
                        t.wait_ge(B.dg.sem, B.dg.v)
                        for half in range(2):
                            for ic in range(8):
                                i1 = t.matmul(
                                    pq[half][:, :],
                                    xTt[:, ic * 128:(ic + 1) * 128],
                                    wqsb[:, ic * DM + half * 512:
                                         ic * DM + half * 512 + 512],
                                    start=(ic == 0), stop=(ic == 7))
                            B.et.inc(i1)

                    @b.vector
                    def _(v):
                        for half in range(2):
                            v.wait_ge(B.et.sem, B.et.v - 1 + half)
                            B.ev.inc(v.tensor_add(
                                qf[:, half * 512:half * 512 + 512],
                                pq[half][:, :],
                                bqb[:, half * 512:half * 512 + 512]))
                with nc.Block() as b:
                    @b.vector
                    def _(v):
                        v.wait_ge(B.ev.sem, B.ev.v)
                        src_o = strided_ap(qf[:, :], 1, [[DM, 128], [2, 512]])
                        dst_e = strided_ap(qrot[:, :], 0, [[DM, 128], [2, 512]])
                        B.ev.inc(v.tensor_scalar(dst_e, src_o, -1.0, None,
                                                 ALU.mult))
                        src_e = strided_ap(qf[:, :], 0, [[DM, 128], [2, 512]])
                        dst_o = strided_ap(qrot[:, :], 1, [[DM, 128], [2, 512]])
                        B.ev.inc(v.tensor_copy(dst_o, src_e))
                        v.wait_ge(B.ev.sem, B.ev.v)
                        B.ev.inc(v.tensor_mul(qf[:, :], qf[:, :], cost[:, :]))
                        B.ev.inc(v.tensor_mul(qrot[:, :], qrot[:, :],
                                              sint[:, :]))
                        v.wait_ge(B.ev.sem, B.ev.v)
                        B.ev.inc(v.tensor_add(qhat[:, :], qf[:, :],
                                              qrot[:, :]))
                with nc.Block() as b:
                    @b.tensor
                    def _(t, _rt=rt):
                        t.wait_ge(B.ev.sem, B.ev.v)
                        for c in range(8):
                            i1 = t.transpose(pbt[:, c * 128:(c + 1) * 128],
                                             qhat[:, c * 128:(c + 1) * 128],
                                             B.identb[:, :])
                        B.et.inc(i1)
                        for h in range(NH):
                            i1 = t.matmul(psqg[0:64, h * 4:(h + 1) * 4],
                                          qhat[:, h * 64:(h + 1) * 64],
                                          ohb[:, _rt * 4:(_rt + 1) * 4],
                                          start=(_rt == 0), stop=(_rt == 31),
                                          skip_group_check=True)
                        B.et.inc(i1)

                    @b.scalar
                    def _(s, _rt=rt):
                        s.wait_ge(B.et.sem, B.et.v - 1)
                        dst = strided_ap(qTp[:, :], _rt * 128,
                                         [[8 * S, 128], [S, 8], [1, 128]])
                        B.es.inc(s.copy(dst, pbt[:, :]))
                with nc.Block() as b:
                    @b.tensor
                    def _(t):
                        for ic in range(8):
                            i1 = t.matmul(pq[0][:, 0:256],
                                          xTt[:, ic * 128:(ic + 1) * 128],
                                          wksb[:, ic * 256:(ic + 1) * 256],
                                          start=(ic == 0), stop=(ic == 7))
                        B.et.inc(i1)
                        for ic in range(8):
                            i1 = t.matmul(pq[1][:, 0:256],
                                          xTt[:, ic * 128:(ic + 1) * 128],
                                          wvsb[:, ic * 256:(ic + 1) * 256],
                                          start=(ic == 0), stop=(ic == 7))
                        B.et.inc(i1)

                    @b.vector
                    def _(v, _rt=rt):
                        v.wait_ge(B.et.sem, B.et.v - 1)
                        B.ev.inc(v.tensor_add(qf[:, 0:256], pq[0][:, 0:256],
                                              bkb[:, :]))
                        src_o = strided_ap(qf[:, :], 1, [[DM, 128], [2, 128]])
                        dst_e = strided_ap(qrot[:, :], 0, [[DM, 128], [2, 128]])
                        v.wait_ge(B.ev.sem, B.ev.v)
                        B.ev.inc(v.tensor_scalar(dst_e, src_o, -1.0, None,
                                                 ALU.mult))
                        src_e = strided_ap(qf[:, :], 0, [[DM, 128], [2, 128]])
                        dst_o = strided_ap(qrot[:, :], 1, [[DM, 128], [2, 128]])
                        B.ev.inc(v.tensor_copy(dst_o, src_e))
                        v.wait_ge(B.ev.sem, B.ev.v)
                        B.ev.inc(v.tensor_mul(qf[:, 0:256], qf[:, 0:256],
                                              cost[:, 0:256]))
                        B.ev.inc(v.tensor_mul(qrot[:, 0:256], qrot[:, 0:256],
                                              sint[:, 0:256]))
                        v.wait_ge(B.ev.sem, B.ev.v)
                        B.ev.inc(v.tensor_add(khat[:, :], qf[:, 0:256],
                                              qrot[:, 0:256]))
                        v.wait_ge(B.et.sem, B.et.v)
                        B.ev.inc(v.tensor_add(
                            vsb[:, _rt * 256:(_rt + 1) * 256],
                            pq[1][:, 0:256], bvb[:, :]))
                with nc.Block() as b:
                    @b.tensor
                    def _(t, _rt=rt):
                        t.wait_ge(B.ev.sem, B.ev.v)
                        t.wait_ge(B.es.sem, B.es.v)
                        for g_ in range(4):
                            for bp in (0, 64):
                                i1 = t.transpose(
                                    pbt[bp:bp + 64,
                                        g_ * 128:g_ * 128 + 128],
                                    khat[:, g_ * 64:(g_ + 1) * 64],
                                    B.identb[:, :])
                        B.et.inc(i1)
                        for g_ in range(4):
                            for bp in (0, 64):
                                i1 = t.matmul(
                                    pskg[bp:bp + 64, g_ * 4:(g_ + 1) * 4],
                                    khat[:, g_ * 64:(g_ + 1) * 64],
                                    ohb[:, _rt * 4:(_rt + 1) * 4],
                                    start=(_rt == 0), stop=(_rt == 31),
                                    skip_group_check=True)
                        B.et.inc(i1)
                        for g_ in range(4):
                            i1 = t.matmul(psvg[0:4, g_ * 64:(g_ + 1) * 64],
                                          ohb[:, _rt * 4:(_rt + 1) * 4],
                                          vsb[:, _rt * 256 + g_ * 64:
                                              _rt * 256 + g_ * 64 + 64],
                                          start=(_rt == 0), stop=(_rt == 31),
                                          skip_group_check=True)
                        B.et.inc(i1)

                    @b.scalar
                    def _(s, _rt=rt):
                        s.wait_ge(B.et.sem, B.et.v - 2)
                        dst = strided_ap(kTp[:, :], _rt * 128,
                                         [[4 * S, 128], [S, 4], [1, 128]])
                        B.es.inc(s.copy(dst, pbt[:, 0:512]))
                B.bar()
            with nc.Block() as b:
                @b.vector
                def _(v):
                    B.ev.inc(v.tensor_copy(qgT[0:64, :], psqg[0:64, :]))
                    B.ev.inc(v.tensor_copy(kgT[:, :], pskg[:, :]))
                    B.ev.inc(v.tensor_copy(vgg[0:4, :], psvg[0:4, :]))
            B.bar()

        with contextlib.ExitStack() as gst:
            gsb = lambda nm, sh, dt: gst.enter_context(
                nc.sbuf_tensor(B.name(nm), sh, dt))
            scg = gsb("scg", [4, S], F32)
            pgbf = gsb("pgbf", [4, S], BF16)
            pTg = gsb("pTg", [128, 128], BF16)
            rg = gsb("rg", [4, 4], F32)
            psg = gst.enter_context(
                nc.psum_tensor(B.name("psg"), [4, 512], F32))
            ptg = gst.enter_context(
                nc.psum_tensor(B.name("ptg"), [128, 128], BF16))
            psog = gst.enter_context(
                nc.psum_tensor(B.name("psog"), [4, 64], F32))
            for h in range(NH):
                g_ = h // 4
                with nc.Block() as b:
                    @b.tensor
                    def _(t, _h=h, _g=g_):
                        for cg in range(8):
                            B.et.inc(t.matmul(
                                psg[0:4, :],
                                qgT[0:64, _h * 4:(_h + 1) * 4],
                                kTp[0:64, _g * S + cg * 512:
                                    _g * S + cg * 512 + 512],
                                start=True, stop=True))
                            t.wait_ge(B.es.sem, B.es.v + cg + 1)

                    @b.scalar
                    def _(s):
                        for cg in range(8):
                            s.wait_ge(B.et.sem, B.et.v - 8 + cg + 1)
                            B.es.inc(s.activation(
                                scg[0:4, cg * 512:(cg + 1) * 512],
                                psg[0:4, :], AF.Copy,
                                scale=B.kc[0:4, KC_ATT:KC_ATT + 1]))
                with nc.Block() as b:
                    @b.vector
                    def _(v):
                        v.wait_ge(B.es.sem, B.es.v)
                        B.ev.inc(v.reduce_max(rg[0:4, 0:1], scg[0:4, :],
                                              axis=AX.X, negate=True))

                    @b.scalar
                    def _(s):
                        s.wait_ge(B.ev.sem, B.ev.v)
                        B.es.inc(s.activation(pgbf[0:4, :], scg[0:4, :],
                                              AF.Exp, bias=rg[0:4, 0:1],
                                              accum_out=rg[0:4, 1:2]))

                    @b.vector
                    def _(v):
                        v.wait_ge(B.es.sem, B.es.v)
                        B.ev.inc(v.reciprocal(rg[0:4, 2:3], rg[0:4, 1:2]))
                with nc.Block() as b:
                    @b.tensor
                    def _(t):
                        t.wait_ge(B.ev.sem, B.ev.v)
                        for jb in range(32):
                            i1 = t.transpose(ptg[:, jb * 4:(jb + 1) * 4],
                                             pgbf[0:4, jb * 128:
                                                  (jb + 1) * 128],
                                             B.identb[0:4, 0:4])
                        B.et.inc(i1)

                    @b.vector
                    def _(v):
                        v.wait_ge(B.et.sem, B.et.v)
                        B.ev.inc(v.tensor_copy(pTg[:, :], ptg[:, :]))
                with nc.Block() as b:
                    @b.tensor
                    def _(t, _g=g_):
                        t.wait_ge(B.ev.sem, B.ev.v)
                        for jb in range(32):
                            i1 = t.matmul(psog[0:4, :],
                                          pTg[:, jb * 4:(jb + 1) * 4],
                                          vsb[:, jb * 256 + _g * 64:
                                              jb * 256 + _g * 64 + 64],
                                          start=(jb == 0), stop=(jb == 31))
                        B.et.inc(i1)

                    @b.vector
                    def _(v, _h=h):
                        v.wait_ge(B.et.sem, B.et.v)
                        B.ev.inc(v.tensor_scalar(
                            ogsb[0:4, _h * 64:(_h + 1) * 64],
                            psog[0:4, :], rg[0:4, 2:3], None, ALU.mult))
                B.bar()
            with nc.Block() as b:
                @b.vector
                def _(v):
                    B.ev.inc(v.tensor_copy(ogbf[0:4, :], ogsb[0:4, :]))
            B.bar()

        with contextlib.ExitStack() as mst:
            msb = lambda nm, sh, dt: mst.enter_context(
                nc.sbuf_tensor(B.name(nm), sh, dt))
            wosb = msb("wosb", [128, 8 * DM], BF16)
            sc = msb("sc", [128, 648], F32)
            pbf = msb("pbf", [128, 648], BF16)
            pT = msb("pT", [128, 640], BF16)
            pTg4 = msb("pTg4", [4, 128], BF16)
            ohT = msb("ohT", [4, 128], BF16)
            gadd = msb("gadd", [128, 4], F32)
            rv = msb("rv", [128, 8], F32)
            osb = msb("osb", [128, DM], F32)
            obf = msb("obf", [128, DM], BF16)
            oT = msb("oT", [128, DM], BF16)
            ysb = msb("ysb", [128, DM], F32)
            xf2 = msb("xf2", [128, DM], F32)
            stats2 = msb("stats2", [128, 4], F32)
            scr2 = msb("scr2", [128, DM], F32)
            osc = msb("osc", [128, DM], F32)
            psA = mst.enter_context(
                nc.psum_tensor(B.name("psA"), [128, 512], F32))
            psB = mst.enter_context(
                nc.psum_tensor(B.name("psB"), [128, 260], F32))
            ptpm = mst.enter_context(
                nc.psum_tensor(B.name("ptpm"), [128, 640], BF16))
            ptg4 = mst.enter_context(
                nc.psum_tensor(B.name("ptg4"), [4, 128], BF16))
            pso = mst.enter_context(
                nc.psum_tensor(B.name("pso"), [128, 64], F32))
            pssc = mst.enter_context(
                nc.psum_tensor(B.name("pssc"), [128, 512], F32))
            pbt2 = mst.enter_context(
                nc.psum_tensor(B.name("pbt2"), [128, DM], BF16))
            psw = mst.enter_context(
                nc.psum_tensor(B.name("psw"), [128, 512], F32))

            with nc.Block() as b:
                @b.gpsimd
                def _(g):
                    for oc in range(8):
                        B.dg.inc(g.dma_start(
                            wosb[:, oc * DM:(oc + 1) * DM],
                            W["l0a_Wo"][oc * 128:(oc + 1) * 128, :]), 16)

            for rt in range(32):
                ws = max(0, rt - 2)
                we = min(31, rt + 2)
                nb = we - ws + 1
                Wd = nb * 128
                toff = (ws - (rt - 2)) * 128
                net = 3 if Wd > 512 else 2
                GA = 6
                et0, es0, ev0 = B.et.v, B.ev.v, B.ev.v
                et0, es0, ev0 = B.et.v, B.es.v, B.ev.v
                with nc.Block() as b:
                    @b.vector
                    def _(v, _rt=rt, _Wd=Wd, _toff=toff, _net=net):
                        B.ev.inc(v.tensor_scalar(gadd[:, :], gpb[:, :],
                                                 B.iotac[:, 0:1],
                                                 float(128 * _rt),
                                                 ALU.subtract, ALU.subtract))
                        v.wait_ge(B.ev.sem, B.ev.v)
                        B.ev.inc(v.tensor_scalar(rv[:, 4:8], gadd[:, :],
                                                 -256.0, None, ALU.is_lt))
                        B.ev.inc(v.tensor_scalar(gadd[:, :], gadd[:, :],
                                                 256.0, None, ALU.is_gt))
                        v.wait_ge(B.ev.sem, B.ev.v)
                        B.ev.inc(v.tensor_max(gadd[:, :], gadd[:, :],
                                              rv[:, 4:8]))
                        v.wait_ge(B.ev.sem, B.ev.v)
                        B.ev.inc(v.tensor_scalar(gadd[:, :], gadd[:, :],
                                                 -1.0, -NEG, ALU.add,
                                                 ALU.mult))
                        v.wait_ge(B.ev.sem, B.ev.v)
                        B.ev.inc(v.tensor_add(gadd[:, :], gadd[:, :],
                                              dupaddb[:, :]))
                        for h in range(NH):
                            v.wait_ge(B.es.sem, es0 + h * (_net + 1) + _net)
                            B.ev.inc(v.tensor_add(
                                sc[:, 0:_Wd], sc[:, 0:_Wd],
                                band_sb[:, _toff:_toff + _Wd]))
                            B.ev.inc(v.tensor_add(sc[:, _Wd:_Wd + 4],
                                                  sc[:, _Wd:_Wd + 4],
                                                  gadd[:, :]))
                            v.wait_ge(B.ev.sem, B.ev.v)
                            B.ev.inc(v.reduce_max(rv[:, 0:1],
                                                  sc[:, 0:_Wd + 4],
                                                  axis=AX.X, negate=True))
                            v.wait_ge(B.es.sem,
                                      es0 + (h + 1) * (_net + 1))
                            B.ev.inc(v.reciprocal(rv[:, 2:3], rv[:, 1:2]))
                            v.wait_ge(B.et.sem,
                                      et0 + h * (_net + 3) + _net + 1)
                            B.ev.inc(v.tensor_copy(pT[:, 0:_Wd],
                                                   ptpm[:, 0:_Wd]))
                            v.wait_ge(B.et.sem,
                                      et0 + h * (_net + 3) + _net + 2)
                            B.ev.inc(v.tensor_copy(pTg4[0:4, :],
                                                   ptg4[0:4, :]))
                            v.wait_ge(B.et.sem,
                                      et0 + h * (_net + 3) + _net + 3)
                            B.ev.inc(v.tensor_scalar(
                                osb[:, h * 64:(h + 1) * 64], pso[:, :],
                                rv[:, 2:3], None, ALU.mult))

                    @b.tensor
                    def _(t, _rt=rt, _ws=ws, _Wd=Wd, _nb=nb, _net=net):
                        for h in range(NH):
                            g_ = h // 4
                            kb = 64 * (h % 2)
                            qsl = qTp[kb:kb + 64,
                                      (h // 2) * S + _rt * 128:
                                      (h // 2) * S + _rt * 128 + 128]
                            n1 = min(_Wd, 512)
                            if h >= 1:
                                t.wait_ge(B.es.sem,
                                          es0 + (h - 1) * (_net + 1) + _net)
                            B.et.inc(t.matmul(
                                psA[:, 0:n1], qsl,
                                kTp[kb:kb + 64,
                                    g_ * S + _ws * 128:
                                    g_ * S + _ws * 128 + n1],
                                start=True, stop=True))
                            if _Wd > 512:
                                B.et.inc(t.matmul(
                                    psB[:, 0:_Wd - 512], qsl,
                                    kTp[kb:kb + 64,
                                        g_ * S + _ws * 128 + 512:
                                        g_ * S + _ws * 128 + _Wd],
                                    start=True, stop=True))
                            B.et.inc(t.matmul(
                                psB[:, 128:132], qsl,
                                kgT[kb:kb + 64, g_ * 4:(g_ + 1) * 4],
                                start=True, stop=True))
                            t.wait_ge(B.es.sem,
                                      es0 + h * (_net + 1) + _net + 1)
                            if h >= 1:
                                t.wait_ge(B.ev.sem,
                                          ev0 + GA + 7 * (h - 1) + 6)
                            for cb in range(_nb):
                                i1 = t.transpose(
                                    ptpm[:, cb * 128:(cb + 1) * 128],
                                    pbf[:, cb * 128:(cb + 1) * 128],
                                    B.identb[:, :])
                            B.et.inc(i1)
                            B.et.inc(t.transpose(ptg4[0:4, :],
                                                 pbf[:, _Wd:_Wd + 4],
                                                 B.identb[:, :]))
                            t.wait_ge(B.ev.sem, ev0 + GA + 7 * h + 6)
                            for cb in range(_nb):
                                t.matmul(pso[:, :],
                                         pT[:, cb * 128:(cb + 1) * 128],
                                         vsb[:, (_ws + cb) * 256 + g_ * 64:
                                             (_ws + cb) * 256 + g_ * 64 + 64],
                                         start=(cb == 0), stop=False)
                            B.et.inc(t.matmul(
                                pso[:, :], pTg4[0:4, :],
                                vgg[0:4, g_ * 64:(g_ + 1) * 64],
                                start=False, stop=True))

                    @b.scalar
                    def _(s, _Wd=Wd, _net=net):
                        n1 = min(_Wd, 512)
                        for h in range(NH):
                            s.wait_ge(B.et.sem, et0 + h * (_net + 3) + 1)
                            if h >= 1:
                                s.wait_ge(B.ev.sem,
                                          ev0 + GA + 7 * (h - 1) + 3)
                            B.es.inc(s.activation(
                                sc[:, 0:n1], psA[:, 0:n1], AF.Copy,
                                scale=B.kc[:, KC_ATT:KC_ATT + 1]))
                            s.wait_ge(B.et.sem, et0 + h * (_net + 3) + _net)
                            if _Wd > 512:
                                B.es.inc(s.activation(
                                    sc[:, 512:_Wd], psB[:, 0:_Wd - 512],
                                    AF.Copy,
                                    scale=B.kc[:, KC_ATT:KC_ATT + 1]))
                            B.es.inc(s.activation(
                                sc[:, _Wd:_Wd + 4], psB[:, 128:132],
                                AF.Copy, scale=B.kc[:, KC_ATT:KC_ATT + 1]))
                            s.wait_ge(B.ev.sem, ev0 + GA + 7 * h + 3)
                            if h >= 1:
                                s.wait_ge(B.et.sem,
                                          et0 + (h - 1) * (_net + 3) +
                                          _net + 2)
                            B.es.inc(s.activation(
                                pbf[:, 0:_Wd + 4], sc[:, 0:_Wd + 4],
                                AF.Exp, bias=rv[:, 0:1],
                                accum_out=rv[:, 1:2]))

                with nc.Block() as b:
                    @b.tensor
                    def _(t, _rt=rt):
                        B.et.inc(t.transpose(ptg4[0:4, :],
                                             ohb[:, _rt * 4:(_rt + 1) * 4],
                                             B.identb[:, :]))

                    @b.vector
                    def _(v):
                        v.wait_ge(B.et.sem, B.et.v)
                        B.ev.inc(v.tensor_copy(ohT[0:4, :], ptg4[0:4, :]))
                with nc.Block() as b:
                    @b.tensor
                    def _(t):
                        t.wait_ge(B.ev.sem, B.ev.v)
                        for half in range(2):
                            B.et.inc(t.matmul(
                                pssc[:, :], ohT[0:4, :],
                                ogbf[0:4, half * 512:half * 512 + 512],
                                start=True, stop=True))
                            t.wait_ge(B.ev.sem, B.ev.v + half + 1)

                    @b.vector
                    def _(v, _rt=rt):
                        for half in range(2):
                            v.wait_ge(B.et.sem, B.et.v - 1 + half)
                            sl = slice(half * 512, half * 512 + 512)
                            B.ev.inc(v.scalar_tensor_tensor(
                                osb[:, sl], osb[:, sl],
                                rmc[:, _rt:_rt + 1], pssc[:, :],
                                ALU.mult, ALU.add))
                        v.wait_ge(B.ev.sem, B.ev.v)
                        B.ev.inc(v.tensor_copy(obf[:, :], osb[:, :]))
                with nc.Block() as b:
                    @b.tensor
                    def _(t):
                        t.wait_ge(B.ev.sem, B.ev.v)
                        for c in range(8):
                            i1 = t.transpose(pbt2[:, c * 128:(c + 1) * 128],
                                             obf[:, c * 128:(c + 1) * 128],
                                             B.identb[:, :])
                        B.et.inc(i1)

                    @b.scalar
                    def _(s):
                        s.wait_ge(B.et.sem, B.et.v)
                        B.es.inc(s.copy(oT[:, :], pbt2[:, :]))
                with nc.Block() as b:
                    @b.tensor
                    def _(t):
                        t.wait_ge(B.es.sem, B.es.v)
                        t.wait_ge(B.dg.sem, B.dg.v)
                        for half in range(2):
                            for oc in range(8):
                                i1 = t.matmul(
                                    psw[:, :],
                                    oT[:, oc * 128:(oc + 1) * 128],
                                    wosb[:, oc * DM + half * 512:
                                         oc * DM + half * 512 + 512],
                                    start=(oc == 0), stop=(oc == 7))
                            B.et.inc(i1)
                            t.wait_ge(B.ev.sem, B.ev.v + half + 1)

                    @b.vector
                    def _(v):
                        for half in range(2):
                            v.wait_ge(B.et.sem, B.et.v - 1 + half)
                            sl = slice(half * 512, half * 512 + 512)
                            B.ev.inc(v.tensor_add(ysb[:, sl], psw[:, :],
                                                  bob[:, sl]))
                B.bar()
                emit_epilogue_tile(B, ysb, x_in, x_out, rt * 128,
                                   xf2, osc, stats2, scr2)



def emit_l1_attn(B, x_in, x_out, W):
    nc = B.nc
    NH, HD = 8, 64
    CL = 256
    NC_ = S // CL

    with contextlib.ExitStack() as st:
        BB = Bld.__new__(Bld)
        BB.__dict__.update(B.__dict__)
        BB.st = st
        sb = BB.sbuf

        wqsb = sb("wqsb", [128, 8 * 512], BF16)
        wksb = sb("wksb", [128, 8 * 512], BF16)
        wvsb = sb("wvsb", [128, 8 * 512], BF16)
        wosb = sb("wosb", [128, 4 * DM], BF16)
        gates = sb("gates", [128, 512], F32)
        bet = sb("bet", [1, 512], F32)
        memz = sb("memz", [64, 8 * 65], F32)
        membf = sb("membf", [64, 8 * 65], BF16)

        with nc.Block() as b:
            @b.gpsimd
            def _(g):
                for ic in range(8):
                    B.dg.inc(g.dma_start(
                        wqsb[:, ic * 512:(ic + 1) * 512],
                        W["l1a_Wq"][ic * 128:(ic + 1) * 128, :]), 16)
                    B.dg.inc(g.dma_start(
                        wksb[:, ic * 512:(ic + 1) * 512],
                        W["l1a_Wk"][ic * 128:(ic + 1) * 128, :]), 16)
                    B.dg.inc(g.dma_start(
                        wvsb[:, ic * 512:(ic + 1) * 512],
                        W["l1a_Wv"][ic * 128:(ic + 1) * 128, :]), 16)
                for oc in range(4):
                    B.dg.inc(g.dma_start(
                        wosb[:, oc * DM:(oc + 1) * DM],
                        W["l1a_Wo"][oc * 128:(oc + 1) * 128, :]), 16)
                g.memset(memz[0:64, :], 0.0)
                for h in range(NH):
                    B.eg.inc(g.memset(memz[0:64, h * 65 + 64:h * 65 + 65],
                                      1.0 / 64.0))

            @b.sync
            def _(sy):
                B.ds.inc(sy.dma_start(
                    bet[0:1, :],
                    W["l1a_betas"].rearrange("(a b) -> a b", a=1)), 16)

            @b.scalar
            def _(s):
                s.wait_ge(B.ds.sem, B.ds.v)
                B.es.inc(s.activation(bet[0:1, :], bet[0:1, :], AF.Sigmoid))
        with contextlib.ExitStack() as pst:
            psx = pst.enter_context(nc.psum_tensor(B.name("psx"),
                                                   [128, 512], F32))
            with nc.Block() as b:
                @b.tensor
                def _(t):
                    t.wait_ge(B.es.sem, B.es.v)
                    B.et.inc(t.matmul(psx[:, :], B.onesf[0:1, :], bet[0:1, :],
                                      start=True, stop=True))

                @b.vector
                def _(v):
                    v.wait_ge(B.et.sem, B.et.v)
                    B.ev.inc(v.tensor_copy(gates[:, :], psx[:, :]))
        B.bar()

        xTc = sb("xTc", [128, 8 * CL], BF16)
        qsb = sb("qsb", [128, 2 * 512], F32)
        ksb = sb("ksb", [128, 2 * 512], F32)
        qbf = sb("qbf", [128, 2 * 512], BF16)
        kbf = sb("kbf", [128, 2 * 512], BF16)
        vbf = sb("vbf", [128, 2 * 512], BF16)
        sqbf = sb("sqbf", [128, 2 * 512], BF16)
        skbf = sb("skbf", [128, 2 * 512], BF16)
        qTsb = sb("qTsb", [64, 8 * CL], BF16)
        kTsb = sb("kTsb", [64, 8 * CL], BF16)
        sqTsb = sb("sqTsb", [64, 8 * CL], BF16)
        pTsb = sb("pTsb", [128, 256], BF16)
        amsb = sb("amsb", [128, 2 * 512], F32)
        adsb = sb("adsb", [128, 2 * 512], F32)
        obf = sb("obf", [128, 2 * 512], BF16)
        oTsb = sb("oTsb", [128, 512], BF16)
        scrA = sb("scrA", [128, 512], F32)
        scrB = sb("scrB", [128, 512], F32)
        rc = sb("rc", [128, 8], F32)
        xf = sb("xf", [128, DM], F32)
        xhat = sb("xhat", [128, DM], BF16)
        stats = sb("stats", [128, 4], F32)
        scr = sb("scr", [128, DM], F32)
        ysb = sb("ysb", [128, DM], F32)
        osb = sb("osb", [128, DM], F32)

        pbt = BB.psum("pbt", [128, DM], BF16)
        ps_am = BB.psum("ps_am", [128, 65], F32)
        ps_s = BB.psum("ps_s", [128, 256], F32)
        ptp = BB.psum("ptp", [128, 256], BF16)
        ps_pv = BB.psum("ps_pv", [128, 64], F32)
        ps_up = BB.psum("ps_up", [64, 65], F32)
        ps_o = BB.psum("ps_o", [128, 512], F32)
        ps_ot = BB.psum("ps_ot", [128, 512], BF16)

        for n in range(NC_):
            c0 = n * CL
            emit_load_ln_t(B, x_in, c0, 2, xTc, CL, xf, xhat, stats, scr, pbt)

            for ti in range(2):
                ev0 = B.ev.v
                with nc.Block() as b:
                    @b.tensor
                    def _(t, _ti=ti):
                        t.wait_ge(B.dg.sem, B.dg.v)
                        evtgt = [ev0 + 2, ev0 + 4, ev0 + 5]
                        for _j, wsb_t in enumerate((wqsb, wksb, wvsb)):
                            for ic in range(8):
                                i1 = t.matmul(
                                    ps_o[:, :],
                                    xTc[:, ic * CL + _ti * 128:
                                        ic * CL + _ti * 128 + 128],
                                    wsb_t[:, ic * 512:(ic + 1) * 512],
                                    start=(ic == 0), stop=(ic == 7))
                            B.et.inc(i1)
                            t.wait_ge(B.ev.sem, evtgt[_j])

                    @b.vector
                    def _(v, _ti=ti):
                        for _j, (dst_f, dst_b) in enumerate(
                                ((qsb, qbf), (ksb, kbf), (None, vbf))):
                            v.wait_ge(B.et.sem, B.et.v - 2 + _j)
                            sl = slice(_ti * 512, _ti * 512 + 512)
                            if dst_f is not None:
                                B.ev.inc(v.tensor_copy(dst_f[:, sl],
                                                       ps_o[:, :]))
                                v.wait_ge(B.ev.sem, B.ev.v)
                            B.ev.inc(v.tensor_copy(dst_b[:, sl], ps_o[:, :]))
                B.bar()

            for (src_f, dst_b) in ((qsb, sqbf), (ksb, skbf)):
                for ti in range(2):
                    with nc.Block() as b:
                        @b.vector
                        def _(v, _s=src_f, _ti=ti):
                            sl = slice(_ti * 512, _ti * 512 + 512)
                            B.ev.inc(v.tensor_scalar(scrA[:, :], _s[:, sl],
                                                     0.0, None, ALU.min))

                        @b.scalar
                        def _(s):
                            s.wait_ge(B.ev.sem, B.ev.v)
                            B.es.inc(s.activation(scrB[:, :], scrA[:, :],
                                                  AF.Exp))

                        @b.vector
                        def _(v, _s=src_f, _d=dst_b, _ti=ti):
                            sl = slice(_ti * 512, _ti * 512 + 512)
                            v.wait_ge(B.es.sem, B.es.v)
                            B.ev.inc(v.tensor_scalar(scrA[:, :], _s[:, sl],
                                                     0.0, None, ALU.max))
                            v.wait_ge(B.ev.sem, B.ev.v)
                            B.ev.inc(v.tensor_add(_d[:, sl], scrB[:, :],
                                                  scrA[:, :]))
                    B.bar()

            for (srcb, dstT) in ((qbf, qTsb), (kbf, kTsb), (sqbf, sqTsb)):
                for ti in range(2):
                    with nc.Block() as b:
                        @b.tensor
                        def _(t, _s=srcb, _ti=ti):
                            for h in range(NH):
                                i1 = t.transpose(
                                    ptp[0:64, (h % 2) * 128:
                                        (h % 2) * 128 + 128] if False else
                                    ps_ot[0:64, h * 64:h * 64 + 128]
                                    if False else
                                    ptp[0:64, 0:128],
                                    _s[:, _ti * 512 + h * 64:
                                       _ti * 512 + h * 64 + 64],
                                    B.identb[:, :])
                                B.et.inc(i1)
                                t.wait_ge(B.ev.sem, B.ev.v + h + 1)

                        @b.vector
                        def _(v, _d=dstT, _ti=ti):
                            for h in range(NH):
                                v.wait_ge(B.et.sem, B.et.v - NH + h + 1)
                                dst = strided_ap(
                                    _d[0:64, :], h * CL + _ti * 128,
                                    [[8 * CL, 64], [1, 128]])
                                B.ev.inc(v.tensor_copy(dst, ptp[0:64, 0:128]))
                    B.bar()

            with nc.Block() as b:
                @b.vector
                def _(v):
                    B.ev.inc(v.tensor_copy(membf[0:64, :], memz[0:64, :]))
            B.bar()
            if _DBG and n == 1:
                with nc.Block() as b:
                    @b.sync
                    def _(sy):
                        B.ds.inc(sy.dma_start(
                            _DBG_T["dbg_mem"][:, :], memz[0:64, :]), 16)
                        sy.wait_ge(B.ds.sem, B.ds.v)

            for h in range(NH):
                for ti in range(2):
                    with nc.Block() as b:
                        @b.tensor
                        def _(t, _h=h, _ti=ti):
                            B.et.inc(t.matmul(
                                ps_am[:, :],
                                sqTsb[0:64, _h * CL + _ti * 128:
                                      _h * CL + _ti * 128 + 128],
                                membf[0:64, _h * 65:(_h + 1) * 65],
                                start=True, stop=True))
                            B.et.inc(t.matmul(
                                ps_s[:, :],
                                qTsb[0:64, _h * CL + _ti * 128:
                                     _h * CL + _ti * 128 + 128],
                                kTsb[0:64, _h * CL:(_h + 1) * CL],
                                start=True, stop=True))

                        @b.vector
                        def _(v, _h=h, _ti=ti):
                            v.wait_ge(B.et.sem, B.et.v - 1)
                            B.ev.inc(v.reciprocal(rc[:, 0:1],
                                                  ps_am[:, 64:65]))
                            v.wait_ge(B.ev.sem, B.ev.v)
                            B.ev.inc(v.tensor_scalar(
                                amsb[:, _ti * 512 + _h * 64:
                                     _ti * 512 + _h * 64 + 64],
                                ps_am[:, 0:64], rc[:, 0:1], None, ALU.mult))
                            v.wait_ge(B.et.sem, B.et.v)
                            B.ev.inc(v.reduce_max(rc[:, 1:2], ps_s[:, :],
                                                  axis=AX.X, negate=True))
                            v.wait_ge(B.ev.sem, B.ev.v)
                            B.ev.inc(v.tensor_scalar(rc[:, 2:3], rc[:, 1:2],
                                                     0.125, None, ALU.mult))

                        @b.scalar
                        def _(s):
                            s.wait_ge(B.ev.sem, B.ev.v)
                            B.es.inc(s.activation(
                                scrA[:, 0:256],
                                ps_s[:, :], AF.Exp, bias=rc[:, 2:3],
                                scale=B.kc[:, KC_ATT:KC_ATT + 1],
                                accum_out=rc[:, 3:4]))

                        @b.vector
                        def _(v):
                            v.wait_ge(B.es.sem, B.es.v)
                            B.ev.inc(v.tensor_copy(pTsb[:, 0:256],
                                                   scrA[:, 0:256]))
                            B.ev.inc(v.reciprocal(rc[:, 4:5], rc[:, 3:4]))
                    with nc.Block() as b:
                        @b.tensor
                        def _(t, _h=h, _ti=ti):
                            t.wait_ge(B.ev.sem, B.ev.v)
                            for jb in range(2):
                                i1 = t.transpose(
                                    ptp[:, jb * 128:(jb + 1) * 128],
                                    pTsb[:, jb * 128:(jb + 1) * 128],
                                    B.identb[:, :])
                            B.et.inc(i1)

                        @b.vector
                        def _(v):
                            v.wait_ge(B.et.sem, B.et.v)
                            B.ev.inc(v.tensor_copy(oTsb[:, 0:256],
                                                   ptp[:, :]))
                    with nc.Block() as b:
                        @b.tensor
                        def _(t, _h=h, _ti=ti):
                            t.wait_ge(B.ev.sem, B.ev.v)
                            for jb in range(2):
                                i1 = t.matmul(
                                    ps_pv[:, :],
                                    oTsb[:, jb * 128:(jb + 1) * 128],
                                    vbf[:, jb * 512 + _h * 64:
                                        jb * 512 + _h * 64 + 64],
                                    start=(jb == 0), stop=(jb == 1))
                            B.et.inc(i1)

                        @b.vector
                        def _(v, _h=h, _ti=ti):
                            v.wait_ge(B.et.sem, B.et.v)
                            B.ev.inc(v.tensor_scalar(
                                adsb[:, _ti * 512 + _h * 64:
                                     _ti * 512 + _h * 64 + 64],
                                ps_pv[:, :], rc[:, 4:5], None, ALU.mult))
                    B.bar()
                with nc.Block() as b:
                    @b.tensor
                    def _(t, _h=h):
                        for ti in range(2):
                            i1 = t.matmul(ps_up[0:64, 0:64],
                                          skbf[:, ti * 512 + _h * 64:
                                               ti * 512 + _h * 64 + 64],
                                          vbf[:, ti * 512 + _h * 64:
                                              ti * 512 + _h * 64 + 64],
                                          start=(ti == 0), stop=(ti == 1))
                        B.et.inc(i1)
                        for ti in range(2):
                            i1 = t.matmul(ps_up[0:64, 64:65],
                                          skbf[:, ti * 512 + _h * 64:
                                               ti * 512 + _h * 64 + 64],
                                          B.onesb[:, 0:1],
                                          start=(ti == 0), stop=(ti == 1))
                        B.et.inc(i1)

                    @b.vector
                    def _(v, _h=h):
                        v.wait_ge(B.et.sem, B.et.v)
                        B.ev.inc(v.tensor_add(memz[0:64, _h * 65:(_h + 1) * 65],
                                              memz[0:64, _h * 65:(_h + 1) * 65],
                                              ps_up[0:64, :]))
                B.bar()

            if _DBG and n == 1:
                with nc.Block() as b:
                    @b.sync
                    def _(sy):
                        B.ds.inc(sy.dma_start(_DBG_T["dbg_am"][:, :],
                                              amsb[:, :]), 16)
                        B.ds.inc(sy.dma_start(_DBG_T["dbg_ad"][:, :],
                                              adsb[:, :]), 16)
                        sy.wait_ge(B.ds.sem, B.ds.v)

            for ti in range(2):
                with nc.Block() as b:
                    @b.vector
                    def _(v, _ti=ti):
                        sl = slice(_ti * 512, _ti * 512 + 512)
                        B.ev.inc(v.tensor_sub(scrA[:, :], amsb[:, sl],
                                              adsb[:, sl]))
                        v.wait_ge(B.ev.sem, B.ev.v)
                        B.ev.inc(v.tensor_mul(scrA[:, :], scrA[:, :],
                                              gates[:, :]))
                        v.wait_ge(B.ev.sem, B.ev.v)
                        B.ev.inc(v.tensor_add(obf[:, sl], scrA[:, :],
                                              adsb[:, sl]))
                B.bar()
                with nc.Block() as b:
                    @b.tensor
                    def _(t, _ti=ti):
                        for oc in range(4):
                            i1 = t.transpose(
                                ps_ot[:, oc * 128:(oc + 1) * 128],
                                obf[:, _ti * 512 + oc * 128:
                                    _ti * 512 + oc * 128 + 128],
                                B.identb[:, :])
                        B.et.inc(i1)

                    @b.vector
                    def _(v):
                        v.wait_ge(B.et.sem, B.et.v)
                        B.ev.inc(v.tensor_copy(oTsb[:, :], ps_ot[:, :]))
                with nc.Block() as b:
                    @b.tensor
                    def _(t, _ti=ti):
                        t.wait_ge(B.ev.sem, B.ev.v)
                        for ds_ in range(2):
                            for oc in range(4):
                                i1 = t.matmul(
                                    ps_o[:, :] if ds_ == 0 else ps_s[:, :]
                                    if False else ps_o[:, :],
                                    oTsb[:, oc * 128:(oc + 1) * 128],
                                    wosb[:, oc * DM + ds_ * 512:
                                         oc * DM + ds_ * 512 + 512],
                                    start=(oc == 0), stop=(oc == 3))
                            B.et.inc(i1)
                            t.wait_ge(B.ev.sem, B.ev.v + ds_ + 1)

                    @b.vector
                    def _(v):
                        for ds_ in range(2):
                            v.wait_ge(B.et.sem, B.et.v - 1 + ds_)
                            B.ev.inc(v.tensor_copy(
                                ysb[:, ds_ * 512:(ds_ + 1) * 512],
                                ps_o[:, :]))
                B.bar()
                emit_epilogue_tile(B, ysb, x_in, x_out, c0 + ti * 128,
                                   xf, osb, stats, scr)



def _host_tables():
    out = {}
    p = np.arange(128)[:, None]
    n = np.arange(640)[None, :]
    band = np.where((n - p >= 0) & (n - p <= 512), 0.0, NEG).astype(np.float32)
    out["band_tbl"] = band
    j = np.arange(32)
    invf = (10000.0 ** (-(2.0 * j) / 64.0)).astype(np.float32)
    t = np.arange(S, dtype=np.float32)
    ang = t[:, None] * invf[None, :]
    emb = np.concatenate([ang, ang], axis=1)
    sin1 = np.sin(emb).astype(np.float32)
    cos1 = np.cos(emb).astype(np.float32)
    out["sin_rep"] = np.ascontiguousarray(np.tile(sin1, (1, 16)))
    out["cos_rep"] = np.ascontiguousarray(np.tile(cos1, (1, 16)))
    return out


def _prep_host(inputs):
    g = {}
    f32 = np.float32
    asc = np.ascontiguousarray

    def scol(v):
        v = np.asarray(v)
        return asc(v.reshape(-1, 128).T)

    g["l1f_w1T"] = asc(np.asarray(inputs["L1_ffn_w1_w"]).T)
    g["l1f_w2T"] = asc(np.asarray(inputs["L1_ffn_w2_w"]).T)
    g["l1f_h1T"] = asc(np.asarray(inputs["L1_ffn_h1_w"]).T)
    g["l1f_h2T"] = asc(np.asarray(inputs["L1_ffn_h2_w"]).T)
    g["l1f_w3T"] = asc(np.asarray(inputs["L1_ffn_w3_w"]).T)
    g["l1f_s1"] = scol(inputs["L1_ffn_w1_s"])
    g["l1f_s2"] = scol(inputs["L1_ffn_w2_s"])
    g["l1f_sh1"] = scol(inputs["L1_ffn_h1_s"])
    g["l1f_sh2"] = scol(inputs["L1_ffn_h2_s"])
    g["l1f_s3"] = asc(np.asarray(inputs["L1_ffn_w3_s"], dtype=f32))
    g["l0f_w1"] = asc(np.asarray(inputs["L0_ffn_w1"]))
    g["l0f_w2"] = asc(np.asarray(inputs["L0_ffn_w2"]))
    g["l0f_gwT"] = asc(np.asarray(inputs["L0_ffn_gate_w"]).T)
    g["l0f_gate_b"] = asc(np.asarray(inputs["L0_ffn_gate_b"]))
    g["l0f_rms_w"] = asc(np.asarray(inputs["L0_ffn_rms_w"]))
    g["l0f_b1c"] = asc(np.asarray(inputs["L0_ffn_b1"])
                       .reshape(8, 16, 128).transpose(0, 2, 1))
    g["l0f_b2"] = asc(np.asarray(inputs["L0_ffn_b2"]))
    for k_ in ("Wq", "bq", "Wk", "bk", "Wv", "bv", "Wo", "bo"):
        g["l0a_" + k_] = asc(np.asarray(inputs["L0_attn_" + k_]))
    g["l1a_Wq"] = asc(np.asarray(inputs["L1_attn_Wq"]))
    g["l1a_Wk"] = asc(np.asarray(inputs["L1_attn_Wk"]))
    g["l1a_Wv"] = asc(np.asarray(inputs["L1_attn_Wv"]))
    g["l1a_Wo"] = asc(np.asarray(inputs["L1_attn_Wo"]))
    g["l1a_betas"] = asc(np.asarray(inputs["L1_attn_betas"]).reshape(512))
    for k in list(g):
        g[k] = np.asarray(g[k], dtype=f32)
    g["gp"] = np.ascontiguousarray(
        np.asarray(inputs["L0_attn_global_positions"], dtype=np.int32))
    g.update(_host_tables())
    return g


BLOCK_INPUTS = {
    "l1f": ["l1f_w1T", "l1f_w2T", "l1f_h1T", "l1f_h2T", "l1f_w3T",
            "l1f_s1", "l1f_s2", "l1f_sh1", "l1f_sh2", "l1f_s3"],
    "l0f": ["l0f_w1", "l0f_w2", "l0f_gwT", "l0f_gate_b", "l0f_rms_w",
            "l0f_b1c", "l0f_b2"],
    "l1a": ["l1a_Wq", "l1a_Wk", "l1a_Wv", "l1a_Wo", "l1a_betas"],
    "l0a": ["l0a_Wq", "l0a_bq", "l0a_Wk", "l0a_bk", "l0a_Wv", "l0a_bv",
            "l0a_Wo", "l0a_bo", "gp", "band_tbl", "sin_rep", "cos_rep"],
}

EMITS = {"l1f": emit_l1_ffn, "l0f": emit_l0_ffn, "l1a": emit_l1_attn,
         "l0a": emit_l0_attn}

SHAPES = {
    "l1f_w1T": [DM, 4096], "l1f_w2T": [DM, 4096],
    "l1f_h1T": [4096, 4096], "l1f_h2T": [4096, 4096],
    "l1f_w3T": [4096, DM],
    "l1f_s1": [128, 32], "l1f_s2": [128, 32],
    "l1f_sh1": [128, 32], "l1f_sh2": [128, 32],
    "l1f_s3": [DM],
    "l0f_w1": [8, DM, 2048], "l0f_w2": [8, 2048, DM],
    "l0f_gwT": [DM, 8], "l0f_gate_b": [8], "l0f_rms_w": [8],
    "l0f_b1c": [8, 128, 16], "l0f_b2": [8, DM],
    "l1a_Wq": [DM, 512], "l1a_Wk": [DM, 512], "l1a_Wv": [DM, 512],
    "l1a_Wo": [512, DM], "l1a_betas": [512],
    "l0a_Wq": [DM, DM], "l0a_bq": [DM], "l0a_Wk": [DM, 256],
    "l0a_bk": [256], "l0a_Wv": [DM, 256], "l0a_bv": [256],
    "l0a_Wo": [DM, DM], "l0a_bo": [DM], "gp": [4],
    "band_tbl": [128, 640], "sin_rep": [S, DM], "cos_rep": [S, DM],
}


_DBG = False
_DBG_T = {}


def _build(block_names):
    nc = bass.Bass("TRN2", target_bir_lowering=False)
    xin = nc.dram_tensor("xin", [S, DM], F32, kind="ExternalInput")
    xout = nc.dram_tensor("xout", [S, DM], F32, kind="ExternalOutput")
    if _DBG:
        _DBG_T["dbg_mem"] = nc.dram_tensor("dbg_mem", [64, 520], F32,
                                           kind="ExternalOutput")
        _DBG_T["dbg_am"] = nc.dram_tensor("dbg_am", [128, 1024], F32,
                                          kind="ExternalOutput")
        _DBG_T["dbg_ad"] = nc.dram_tensor("dbg_ad", [128, 1024], F32,
                                          kind="ExternalOutput")
    needed = []
    for bn in block_names:
        needed += BLOCK_INPUTS[bn]
    W = {}
    for name in needed:
        dt = I32 if name == "gp" else F32
        W[name] = nc.dram_tensor(name, SHAPES[name], dt,
                                 kind="ExternalInput")[tuple(
                                     slice(None) for _ in SHAPES[name])]
    with contextlib.ExitStack() as st:
        B = Bld(nc, st)
        emit_consts(B)
        cur_in = xin[:, :]
        for i, bn in enumerate(block_names):
            last = (i == len(block_names) - 1)
            if last:
                cur_out = xout[:, :]
            else:
                h = B.dram(f"xh{i}", [S, DM], F32)
                cur_out = h[:, :]
            EMITS[bn](B, cur_in, cur_out, W)
            cur_in = cur_out
            B.bar()
            B.dma_fence()
        with nc.Block() as b:
            @b.sync
            def _(sy):
                sy.wait_ge(B.ds.sem, B.ds.v)
                sy.wait_ge(B.dg.sem, B.dg.v)
    return nc, needed


_PROG_CACHE = {}


def _run_blocks(block_names, x_full, inputs):
    key = tuple(block_names)
    if key not in _PROG_CACHE:
        _PROG_CACHE[key] = _build(block_names)
    nc, needed = _PROG_CACHE[key]
    shared = _prep_host(inputs)
    in_maps = []
    for c in range(8):
        m = {"xin": np.ascontiguousarray(x_full[c])}
        for name in needed:
            m[name] = shared[name]
        in_maps.append(m)
    res = run_bass_kernel_spmd(nc, in_maps, core_ids=list(range(8)))
    if _DBG:
        _DBG["last"] = res.results
    return np.stack([np.asarray(r["xout"]) for r in res.results])


GROUPS = [["l0a", "l0f", "l1a", "l1f"]]


def kernel(**inputs):
    x = np.asarray(inputs["L0_attn_x"], dtype=np.float32)
    for grp in GROUPS:
        x = _run_blocks(grp, x, inputs)
    return x
